# version 05
# baseline (speedup 1.0000x reference)
import contextlib
import numpy as np

import concourse.bass as bass
import concourse.mybir as mybir
from concourse.bass_utils import run_bass_kernel_spmd

F32 = mybir.dt.float32
BF16 = mybir.dt.bfloat16
I32 = mybir.dt.int32
AF = mybir.ActivationFunctionType
OP = mybir.AluOpType
AX = mybir.AxisListType

D = 1024
NTOK = 4096
T = 512
NT = NTOK // T
NDC = D // 128
HID = 4096
NHC = HID // 128
TH = T + 2
C = 128
NCH = T // C
H = 8
DK = 128
M = 64
EPS = 1e-5
NCORE = 8


class Prog:
    ENGS = ("sync", "vector", "scalar", "tensor", "gpsimd")
    DMA_ENGS = ("sync", "gpsimd")

    def __init__(self, nc, ctx):
        self.nc = nc
        self.streams = {k: [] for k in self.ENGS}
        self.sems = {}
        self.dirty = set()
        for e in self.ENGS:
            self.sems["bar_" + e] = [ctx.enter_context(nc.semaphore("bar_" + e)), 0]
            self.sems["ch_" + e] = [ctx.enter_context(nc.semaphore("ch_" + e)), 0]
        self._ctx = ctx

    def sem(self, name):
        self.sems[name] = [self._ctx.enter_context(self.nc.semaphore(name)), 0]

    def count(self, name):
        return self.sems[name][1]

    def do(self, eng, fn, sig=None, inc=1, dep=False):
        self.dirty.add(eng)
        if dep:
            tok = self._retire_token(eng)
            if tok is not None:
                h, v = tok
                self.streams[eng].append(
                    ["wait", lambda e, h=h, v=v: e.wait_ge(h, v), []])
        sigs = []
        if sig is not None:
            rec = self.sems[sig]
            rec[1] += inc
            sigs.append((rec[0], inc, rec[1]))
        self.streams[eng].append(["op", fn, sigs])
        return self.sems[sig][1] if sig is not None else None

    def _retire_token(self, eng):
        for ent in reversed(self.streams[eng]):
            if ent[0] == "op":
                if ent[2]:
                    h, inc, val = ent[2][-1]
                    return (h, val)
                rec = self.sems["ch_" + eng]
                rec[1] += 1
                ent[2].append((rec[0], 1, rec[1]))
                return (rec[0], rec[1])
        return None

    def wait(self, eng, sig, val=None):
        rec = self.sems[sig]
        v = rec[1] if val is None else val
        h = rec[0]
        self.streams[eng].append(["wait", lambda e, h=h, v=v: e.wait_ge(h, v), []])

    def bar(self):
        prods = sorted(self.dirty)
        if not prods:
            return
        toks = []
        for p in prods:
            if p in self.DMA_ENGS:
                rec = self.sems["bar_" + p]
                rec[1] += 1
                self.streams[p].append(
                    ["op", lambda e: e.nop(), [(rec[0], 1, rec[1])]])
                toks.append((p, rec[0], rec[1]))
            else:
                tok = self._retire_token(p)
                if tok is None:
                    raise RuntimeError(f"bar: no op to attach on {p}")
                toks.append((p, tok[0], tok[1]))
        for e in self.ENGS:
            for p, h, v in toks:
                if p == e:
                    continue
                self.streams[e].append(
                    ["wait", lambda en, h=h, v=v: en.wait_ge(h, v), []])
        self.dirty = set()

    def _replay(self, e, entries):
        for kind, fn, sigs in entries:
            inst = fn(e)
            for h, inc, _v in sigs:
                inst = inst.then_inc(h, inc)

    def finalize(self):
        with self.nc.Block() as block:
            @block.sync
            def _(e):
                self._replay(e, self.streams["sync"])

            @block.vector
            def _(e):
                self._replay(e, self.streams["vector"])

            @block.scalar
            def _(e):
                self._replay(e, self.streams["scalar"])

            @block.tensor
            def _(e):
                self._replay(e, self.streams["tensor"])

            @block.gpsimd
            def _(e):
                self._replay(e, self.streams["gpsimd"])


def build_program(attn_mode="attn", debug=False, stop_after="full"):
    nc = bass.Bass("TRN2", target_bir_lowering=False, detect_race_conditions=False)
    ctx = contextlib.ExitStack()

    xT = nc.dram_tensor("xT", [D, NTOK], F32, kind="ExternalInput").ap()
    outT = nc.dram_tensor("outT", [D, NTOK], F32, kind="ExternalOutput").ap()
    W = {}
    for name, shape in [
        ("Wq", [D, D]), ("Wk", [D, D]), ("Wv", [D, D]), ("Wo", [D, D]),
        ("Wf1", [D, DK]), ("Wf2", [DK, H * M]), ("Wg1", [D, DK]), ("Wg2", [DK, D]),
        ("w1", [HID, D]), ("w2", [D, HID]),
    ]:
        W[name] = nc.dram_tensor(name, shape, F32, kind="ExternalInput").ap()
    c_ident = nc.dram_tensor("c_ident", [128, 128], F32, kind="ExternalInput").ap()
    c_tri = nc.dram_tensor("c_tri", [128, 128], F32, kind="ExternalInput").ap()
    c_trih = nc.dram_tensor("c_trih", [64, 64], F32, kind="ExternalInput").ap()
    c_convw = nc.dram_tensor("c_convw", [128, 3 * NHC], F32, kind="ExternalInput").ap()
    c_convb = nc.dram_tensor("c_convb", [128, NHC], F32, kind="ExternalInput").ap()
    c_normw = nc.dram_tensor("c_normw", [128, NDC], F32, kind="ExternalInput").ap()
    c_ones = nc.dram_tensor("c_ones", [128, 128], F32, kind="ExternalInput").ap()
    c_zeros = nc.dram_tensor("c_zeros", [128, 512], F32, kind="ExternalInput").ap()
    x1d = nc.dram_tensor("x1d", [D, NTOK], F32).ap()
    x2d = nc.dram_tensor("x2d", [D, NTOK], F32).ap()
    w1qT = nc.dram_tensor("w1qT", [D, HID], BF16).ap()
    w2qT = nc.dram_tensor("w2qT", [HID, D], BF16).ap()
    dbg = {}
    if debug:
        for nm in ("x1o", "x2o"):
            dbg[nm] = nc.dram_tensor(nm, [D, NTOK], F32, kind="ExternalOutput").ap()
        dbg["d_xq"] = nc.dram_tensor("d_xq", [128, NDC * TH], BF16, kind="ExternalOutput").ap()
        dbg["d_h"] = nc.dram_tensor("d_h", [128, NHC * TH], F32, kind="ExternalOutput").ap()
        dbg["d_tg"] = nc.dram_tensor("d_tg", [128, 16], F32, kind="ExternalOutput").ap()
        dbg["d_xhat"] = nc.dram_tensor("d_xhat", [128, NDC * TH], F32, kind="ExternalOutput").ap()
        dbg["d_yq"] = nc.dram_tensor("d_yq", [128, 4 * T], BF16, kind="ExternalOutput").ap()
        dbg["d_sc"] = nc.dram_tensor("d_sc", [128, 2 * T], F32, kind="ExternalOutput").ap()
        dbg["a_q"] = nc.dram_tensor("a_q", [128, H * T], F32, kind="ExternalOutput").ap()
        dbg["a_k"] = nc.dram_tensor("a_k", [128, H * T], F32, kind="ExternalOutput").ap()
        dbg["a_sg"] = nc.dram_tensor("a_sg", [128, 4 * T], F32, kind="ExternalOutput").ap()
        dbg["a_ok"] = nc.dram_tensor("a_ok", [128, 512], F32, kind="ExternalOutput").ap()
        dbg["a_qv"] = nc.dram_tensor("a_qv", [128, 512], F32, kind="ExternalOutput").ap()
        dbg["a_og"] = nc.dram_tensor("a_og", [128, H * T], F32, kind="ExternalOutput").ap()
        dbg["a_S"] = nc.dram_tensor("a_S", [128, 512], F32, kind="ExternalOutput").ap()
        dbg["a_T"] = nc.dram_tensor("a_T", [128, 512], F32, kind="ExternalOutput").ap()
        dbg["a_Wm"] = nc.dram_tensor("a_Wm", [128, 128], F32, kind="ExternalOutput").ap()
        dbg["a_ov"] = nc.dram_tensor("a_ov", [128, 128], F32, kind="ExternalOutput").ap()
        dbg["a_ovh"] = nc.dram_tensor("a_ovh", [128, H * 128], F32, kind="ExternalOutput").ap()
        dbg["a_wmh"] = nc.dram_tensor("a_wmh", [128, H * 128], F32, kind="ExternalOutput").ap()
        dbg["a_sf"] = nc.dram_tensor("a_sf", [128, 512], F32, kind="ExternalOutput").ap()
        dbg["a_vqf"] = nc.dram_tensor("a_vqf", [128, 512], F32, kind="ExternalOutput").ap()

    def sb(name, shape, dtype=F32):
        return ctx.enter_context(nc.sbuf_tensor(name, shape, dtype))

    def psum(name, shape, dtype=F32):
        return ctx.enter_context(nc.psum_tensor(name, shape, dtype))

    ident = sb("ident", [128, 128])
    tri = sb("tri", [128, 128])
    trih = sb("trih", [64, 64])
    convw = sb("convw", [128, 3 * NHC])
    convb = sb("convb", [128, NHC])
    normw = sb("normw", [128, NDC])
    onest = sb("onest", [128, 128])
    zer = sb("zer", [128, 512])
    wdeq = sb("wdeq", [128, 2])
    wthr = sb("wthr", [128, 4])
    tg = sb("tg", [128, 16])
    tgrow = sb("tgrow", [16, 128])
    rowb = sb("rowb", [1, 512])
    rowbx = sb("rowbx", [1, 2])
    stat = sb("stat", [128, 16])
    S_st = sb("S_st", [128, H * M])
    T_st = sb("T_st", [128, 4 * 128])
    xs = sb("xs", [128, NDC * TH])
    xhat = sb("xhat", [128, NDC * TH])
    xtmp = sb("xtmp", [128, NDC * TH])

    pb = [psum(f"pb{i}", [128, 512]) for i in range(8)]

    P = Prog(nc, ctx)
    for s in ("ld", "st", "lw", "cva", "cvb", "q2", "m2h", "pj"):
        P.sem(s)

    V, S, TE, G = "vector", "scalar", "tensor", "gpsimd"

    def dma_in(dst_ap, src_ap):
        return P.do("sync", lambda e, d=dst_ap, s=src_ap: e.dma_start(out=d, in_=s), sig="ld", inc=16)

    def dma_w(dst_ap, src_ap):
        return P.do("sync", lambda e, d=dst_ap, s=src_ap: e.dma_start(out=d, in_=s), sig="lw", inc=16)

    def dma_out(dst_ap, src_ap):
        return P.do("gpsimd", lambda e, d=dst_ap, s=src_ap: e.dma_start(out=d, in_=s), sig="st", inc=16)

    def v_op(fn, **kw):
        P.do(V, fn, **kw)

    def s_op(fn, **kw):
        P.do(S, fn, **kw)

    def t_op(fn, **kw):
        P.do(TE, fn, **kw)

    def t_transpose(out_psum, in_ap):
        p = in_ap.shape[0]
        t_op(lambda e, o=out_psum, i=in_ap, p=p: e.transpose(o, i, ident[0:p, 0:p]))

    def t_bcast(out_psum, row_ap):
        t_op(lambda e, o=out_psum, r=row_ap: e.matmul(o, onest[0:1, 0:128], r, start=True, stop=True))

    dma_in(ident[:], c_ident)
    dma_in(tri[:], c_tri)
    dma_in(trih[:], c_trih)
    dma_in(convw[:], c_convw)
    dma_in(convb[:], c_convb)
    dma_in(normw[:], c_normw)
    dma_in(onest[:], c_ones)
    dma_in(zer[:], c_zeros)
    dma_in(S_st[:], c_zeros)
    dma_in(T_st[:], c_zeros)
    dma_in(tg[:], c_zeros[:, 0:16])
    dma_in(stat[:], c_zeros[:, 0:16])
    dma_in(rowb[:], c_zeros[0:1, 0:512])
    dma_in(rowbx[:], c_zeros[0:1, 0:2])
    P.wait(V, "ld")
    P.wait(S, "ld")
    P.wait(TE, "ld")
    P.bar()

    def quant_weight(wsrc, nrc, ncols, dstT, deq_col, thr_pos, thr_neg, wctx):
        def fsb(name, shape, dtype=F32):
            return wctx.enter_context(nc.sbuf_tensor(name, shape, dtype))
        wbuf = fsb(f"wbuf{deq_col}", [128, ncols])
        acc = fsb(f"wacc{deq_col}", [128, 1])
        qbuf = fsb(f"wqq{deq_col}", [128, ncols], BF16)
        tb1 = fsb(f"wtb1{deq_col}", [128, ncols])
        tb2 = fsb(f"wtb2{deq_col}", [128, ncols])
        nelem = nrc * 128 * ncols
        for r in range(nrc):
            dma_w(wbuf[:], wsrc[128 * r:128 * (r + 1), :])
            P.wait(V, "lw")
            if r == 0:
                v_op(lambda e: e.tensor_reduce(acc[:], wbuf[:], AX.X, OP.add,
                                               apply_absolute_value=True))
            else:
                v_op(lambda e: e.tensor_reduce(stat[:, 0:1], wbuf[:], AX.X, OP.add,
                                               apply_absolute_value=True))
                v_op(lambda e: e.tensor_add(acc[:], acc[:], stat[:, 0:1]), dep=True)
            P.bar()
        t_op(lambda e: e.matmul(pb[7][0:1, 0:1], acc[:], onest[:, 0:1], start=True, stop=True))
        P.bar()
        v_op(lambda e: e.tensor_scalar(stat[0:1, 0:1], pb[7][0:1, 0:1], 1.0 / nelem, 1e-5,
                                       OP.mult, OP.max))
        v_op(lambda e: e.tensor_scalar(stat[0:1, 1:2], stat[0:1, 0:1], 0.5, None, OP.mult), dep=True)
        P.bar()
        t_bcast(pb[7][:, 0:1], stat[0:1, 0:1])
        t_bcast(pb[7][:, 1:2], stat[0:1, 1:2])
        P.bar()
        v_op(lambda e: e.tensor_copy(wdeq[:, deq_col:deq_col + 1], pb[7][:, 0:1]))
        v_op(lambda e: e.tensor_copy(wthr[:, thr_pos:thr_pos + 1], pb[7][:, 1:2]))
        v_op(lambda e: e.tensor_scalar(wthr[:, thr_neg:thr_neg + 1], pb[7][:, 1:2], -1.0,
                                       None, OP.mult))
        P.bar()
        nbc = ncols // 128
        for r in range(nrc):
            dma_w(wbuf[:], wsrc[128 * r:128 * (r + 1), :])
            P.wait(TE, "lw")
            for b0 in range(0, nbc, 4):
                bank = pb[(b0 // 4) % 2]
                n = min(4, nbc - b0)
                for j in range(n):
                    t_transpose(bank[:, j * 128:(j + 1) * 128],
                                wbuf[:, (b0 + j) * 128:(b0 + j + 1) * 128])
                P.bar()
                v_op(lambda e, b0=b0, n=n, bank=bank: e.tensor_copy(
                    tb1[:, b0 * 128:(b0 + n) * 128], bank[:, 0:n * 128]))
                P.bar()
            v_op(lambda e: e.tensor_scalar(tb2[:], tb1[:], wthr[:, thr_pos:thr_pos + 1],
                                           None, OP.is_gt), dep=True)
            v_op(lambda e: e.tensor_scalar(tb1[:], tb1[:], wthr[:, thr_neg:thr_neg + 1],
                                           None, OP.is_lt), dep=True)
            v_op(lambda e: e.scalar_tensor_tensor(qbuf[:], tb1[:], -1.0, tb2[:],
                                                  OP.mult, OP.add), dep=True)
            P.bar()
            dst_ap = dstT.rearrange("(b p) (rr i) -> p b rr i", p=128, i=128)[:, :, r, :]
            src_ap = qbuf[:].rearrange("p (b i) -> p b i", i=128)
            dma_out(dst_ap, src_ap)
            P.wait(G, "st")
        P.bar()

    _stop = {"v": None}
    if stop_after == "consts":
        _stop["v"] = "consts"
    if _stop["v"] is None:
        with contextlib.ExitStack() as wctx:
            quant_weight(W["w1"], NHC, D, w1qT, 0, 0, 1, wctx)
            if stop_after != "w1":
                quant_weight(W["w2"], NDC, HID, w2qT, 1, 2, 3, wctx)
            P.wait("sync", "st")
            P.bar()

    def emit_ln(nu):
        extras = nu == TH
        xs3 = xs[:].rearrange("p (dc u) -> p dc u", u=TH)
        xh3 = xhat[:].rearrange("p (dc u) -> p dc u", u=TH)
        xt3 = xtmp[:].rearrange("p (dc u) -> p dc u", u=TH)
        ns = 5 if extras else 4

        def xv(buf, dc, lo, hi):
            return buf[:, dc * TH + lo: dc * TH + hi]

        s_op(lambda e: e.activation(xtmp[:, 0:NDC * TH], xs[:, 0:NDC * TH], AF.Square))
        P.bar()
        for g in range(4):
            for dc in range(NDC):
                t_op(lambda e, g=g, dc=dc: e.matmul(
                    pb[6][:, g:g + 1], xv(xs, dc, g * 128, (g + 1) * 128), onest[:, 0:1],
                    start=(dc == 0), stop=(dc == NDC - 1)))
            for dc in range(NDC):
                t_op(lambda e, g=g, dc=dc: e.matmul(
                    pb[6][:, 4 + g:5 + g], xv(xtmp, dc, g * 128, (g + 1) * 128), onest[:, 0:1],
                    start=(dc == 0), stop=(dc == NDC - 1)))
        if extras:
            for dc in range(NDC):
                t_op(lambda e, dc=dc: e.matmul(pb[6][0:2, 8:9], xv(xs, dc, 512, 514),
                                               onest[:, 0:1], start=(dc == 0), stop=(dc == NDC - 1)))
            for dc in range(NDC):
                t_op(lambda e, dc=dc: e.matmul(pb[6][0:2, 9:10], xv(xtmp, dc, 512, 514),
                                               onest[:, 0:1], start=(dc == 0), stop=(dc == NDC - 1)))
        P.bar()
        v_op(lambda e: e.tensor_scalar(tg[:, 0:4], pb[6][:, 0:4], 1.0 / D, None, OP.mult))
        v_op(lambda e: e.tensor_scalar(stat[:, 0:4], pb[6][:, 4:8], 1.0 / D, None, OP.mult))
        if extras:
            v_op(lambda e: e.tensor_scalar(tg[0:2, 4:5], pb[6][0:2, 8:9], 1.0 / D, None, OP.mult))
            v_op(lambda e: e.tensor_scalar(stat[0:2, 4:5], pb[6][0:2, 9:10], 1.0 / D, None, OP.mult))
        v_op(lambda e, ns=ns: e.tensor_mul(stat[:, 8:8 + ns], tg[:, 0:ns], tg[:, 0:ns]), dep=True)
        v_op(lambda e, ns=ns: e.tensor_sub(stat[:, 0:ns], stat[:, 0:ns], stat[:, 8:8 + ns]), dep=True)
        v_op(lambda e, ns=ns: e.tensor_scalar(stat[:, 0:ns], stat[:, 0:ns], EPS, None, OP.add), dep=True)
        P.bar()
        s_op(lambda e, ns=ns: e.activation(stat[:, 0:ns], stat[:, 0:ns], AF.Ln))
        s_op(lambda e, ns=ns: e.activation(tg[:, 5:5 + ns], stat[:, 0:ns], AF.Exp, scale=-0.5), dep=True)
        P.bar()
        emit_bcast(0, ns, 4, 256)
        emit_bcast(5, ns, 5, 258)
        v_op(lambda e: e.tensor_sub(xt3[:, :, 0:512], xs3[:, :, 0:512],
                                    pb[4][:, 0:512].unsqueeze(1).broadcast_to((128, NDC, 512))))
        if extras:
            v_op(lambda e: e.tensor_sub(xt3[:, :, 512:514], xs3[:, :, 512:514],
                                        pb[7][:, 256:258].unsqueeze(1).broadcast_to((128, NDC, 2))))
        v_op(lambda e: e.tensor_mul(xh3[:, :, 0:512], xt3[:, :, 0:512],
                                    pb[5][:, 0:512].unsqueeze(1).broadcast_to((128, NDC, 512))), dep=True)
        if extras:
            v_op(lambda e: e.tensor_mul(xh3[:, :, 512:514], xt3[:, :, 512:514],
                                        pb[7][:, 258:260].unsqueeze(1).broadcast_to((128, NDC, 2))), dep=True)
        P.bar()

    def emit_bcast(tg_lo, ns, bank, extra_off):
        for g in range(4):
            t_transpose(pb[6][0:1, g * 128:(g + 1) * 128], tg[:, tg_lo + g:tg_lo + g + 1])
        if ns == 5:
            t_transpose(pb[7][0:1, 384:386], tg[0:2, tg_lo + 4:tg_lo + 5])
        P.bar()
        v_op(lambda e: e.tensor_copy(rowb[:], pb[6][0:1, 0:512]))
        if ns == 5:
            v_op(lambda e: e.tensor_copy(rowbx[:], pb[7][0:1, 384:386]))
        P.bar()
        for g in range(4):
            t_bcast(pb[bank][:, g * 128:(g + 1) * 128], rowb[0:1, g * 128:(g + 1) * 128])
        if ns == 5:
            t_bcast(pb[7][:, extra_off:extra_off + 2], rowbx[0:1, 0:2])
        P.bar()

    def emit_ffn(src, dst, dbg_dst=None, tag="a", stop=None):
        fstk = contextlib.ExitStack()

        def fsb(name, shape, dtype=F32):
            return fstk.enter_context(nc.sbuf_tensor(tag + name, shape, dtype))

        h = fsb("h", [128, NHC * TH])
        ni32 = fsb("ni32", [128, NDC * TH], I32)
        xqb = fsb("xqb", [128, NDC * TH], BF16)
        yqg = [fsb(f"yqg{i}", [128, 4 * T], BF16) for i in range(2)]
        wg1 = [fsb(f"wg1_{i}", [128, NDC * 1024], BF16) for i in range(2)]
        wg2 = [fsb(f"wg2_{i}", [128, 1024], BF16) for i in range(2)]
        yscr = [fsb(f"yscr{i}", [128, T]) for i in range(2)]
        sc1 = fsb("sc1", [128, TH])
        sc2 = fsb("sc2", [128, T])
        scq = fsb("scq", [128, T])
        smax = fsb("smax", [128, T])

        xs3 = xs[:].rearrange("p (dc u) -> p dc u", u=TH)
        xh3 = xhat[:].rearrange("p (dc u) -> p dc u", u=TH)
        xt3 = xtmp[:].rearrange("p (dc u) -> p dc u", u=TH)
        ni3 = ni32[:].rearrange("p (dc u) -> p dc u", u=TH)

        def xv(buf, dc, lo, hi):
            return buf[:, dc * TH + lo: dc * TH + hi]

        def hv(hc, lo, hi):
            return h[:, hc * TH + lo: hc * TH + hi]

        NG = 4
        GHC = NHC // NG

        for it in range(NT):
            t0 = it * T
            lo_u = 1 if it == 0 else 0
            hi_u = TH - 1 if it == NT - 1 else TH
            g0 = t0 - 1 + lo_u
            dma_in(xs3[:, :, lo_u:hi_u],
                   src.rearrange("(dc p) n -> p dc n", p=128)[:, :, g0:g0 + (hi_u - lo_u)])
            lw0 = dma_w(wg1[0][:].rearrange("p (dc n) -> p dc n", n=1024),
                        w1qT.rearrange("(dc p) hid -> p dc hid", p=128)[:, :, 0:1024])
            P.wait(V, "ld")
            P.wait(S, "ld")
            P.wait(TE, "ld")
            if it == 0:
                v_op(lambda e: e.tensor_copy(xs3[:, :, 0:1], xs3[:, :, 1:2]))
            if it == NT - 1:
                v_op(lambda e: e.tensor_copy(xs3[:, :, TH - 1:TH], zer[:, 0:NDC]))
            P.bar()

            emit_ln(TH)
            if stop == "ln" and it == 0:
                P.bar()
                fstk.close()
                return "stopped"
            if dbg_dst is not None and it == 0 and dbg.get("d_xhat") is not None:
                dma_out(dbg["d_xhat"], xhat[:])
                P.wait(G, "st")
            v_op(lambda e: e.tensor_reduce(
                smax[:, 0:512], xhat[:].rearrange("p (dc u) -> p u dc", u=TH)[:, 0:512, :],
                AX.X, OP.max, apply_absolute_value=True))
            v_op(lambda e: e.tensor_reduce(
                stat[:, 6:8], xhat[:].rearrange("p (dc u) -> p u dc", u=TH)[:, 512:514, :],
                AX.X, OP.max, apply_absolute_value=True))
            P.bar()
            for g in range(4):
                t_transpose(pb[6][:, g * 128:(g + 1) * 128], smax[:, g * 128:(g + 1) * 128])
            t_transpose(pb[7][0:2, 260:388], stat[:, 6:8])
            P.bar()
            for g in range(4):
                v_op(lambda e, g=g: e.tensor_reduce(tg[:, 10 + g:11 + g],
                                                    pb[6][:, g * 128:(g + 1) * 128], AX.X, OP.max))
            v_op(lambda e: e.tensor_reduce(tg[0:2, 14:15], pb[7][0:2, 260:388], AX.X, OP.max))
            v_op(lambda e: e.tensor_scalar(tg[:, 10:15], tg[:, 10:15], 1e-5, None, OP.max), dep=True)
            v_op(lambda e: e.reciprocal(stat[:, 0:5], tg[:, 10:15]), dep=True)
            v_op(lambda e: e.tensor_scalar(tg[:, 10:15], stat[:, 0:5], 127.0, None, OP.mult), dep=True)
            v_op(lambda e: e.reciprocal(stat[:, 0:5], tg[:, 10:15]), dep=True)
            v_op(lambda e: e.tensor_scalar(stat[:, 0:5], stat[:, 0:5], wdeq[:, 0:1], None, OP.mult), dep=True)
            P.bar()
            emit_bcast(10, 5, 4, 256)
            v_op(lambda e: e.tensor_copy(tg[:, 10:15], stat[:, 0:5]))
            P.bar()
            emit_bcast(10, 5, 5, 258)
            v_op(lambda e: e.tensor_copy(sc1[:, 0:512], pb[5][:, 0:512]))
            v_op(lambda e: e.tensor_copy(sc1[:, 512:514], pb[7][:, 258:260]))
            v_op(lambda e: e.tensor_mul(ni3[:, :, 0:512], xh3[:, :, 0:512],
                                        pb[4][:, 0:512].unsqueeze(1).broadcast_to((128, NDC, 512))))
            v_op(lambda e: e.tensor_mul(ni3[:, :, 512:514], xh3[:, :, 512:514],
                                        pb[7][:, 256:258].unsqueeze(1).broadcast_to((128, NDC, 2))))
            v_op(lambda e: e.tensor_mul(xt3[:, :, 0:512], xh3[:, :, 0:512],
                                        pb[4][:, 0:512].unsqueeze(1).broadcast_to((128, NDC, 512))))
            v_op(lambda e: e.tensor_mul(xt3[:, :, 512:514], xh3[:, :, 512:514],
                                        pb[7][:, 256:258].unsqueeze(1).broadcast_to((128, NDC, 2))))
            v_op(lambda e: e.scalar_tensor_tensor(xhat[:, 0:NDC * TH], ni32[:, 0:NDC * TH], -1.0,
                                                  xtmp[:, 0:NDC * TH], OP.mult, OP.add), dep=True)
            v_op(lambda e: e.tensor_scalar(xtmp[:, 0:NDC * TH], xhat[:, 0:NDC * TH], 0.5,
                                           None, OP.is_gt), dep=True)
            v_op(lambda e: e.scalar_tensor_tensor(xtmp[:, 0:NDC * TH], xtmp[:, 0:NDC * TH], 0.0,
                                                  ni32[:, 0:NDC * TH], OP.bypass, OP.add), dep=True)
            v_op(lambda e: e.tensor_scalar(xhat[:, 0:NDC * TH], xhat[:, 0:NDC * TH], -0.5,
                                           None, OP.is_lt), dep=True)
            v_op(lambda e: e.scalar_tensor_tensor(xqb[:, 0:NDC * TH], xhat[:, 0:NDC * TH], -1.0,
                                                  xtmp[:, 0:NDC * TH], OP.mult, OP.add), dep=True)
            P.bar()

            if dbg_dst is not None and it == 0 and dbg.get("d_xq") is not None:
                dma_out(dbg["d_xq"], xqb[:])
                dma_out(dbg["d_tg"], tg[:])
                P.wait(G, "st")
            if stop == "q" and it == 0:
                P.bar()
                fstk.close()
                return "stopped"
            lw_have = lw0
            for g in range(NG):
                if g + 1 < NG:
                    lw_next = dma_w(wg1[(g + 1) % 2][:].rearrange("p (dc n) -> p dc n", n=1024),
                                    w1qT.rearrange("(dc p) hid -> p dc hid", p=128)
                                    [:, :, (g + 1) * 1024:(g + 2) * 1024])
                P.wait(TE, "lw", lw_have)
                wbuf = wg1[g % 2]
                for j in range(GHC):
                    hc = g * GHC + j
                    bank = pb[j % 2]
                    for dc in range(NDC):
                        t_op(lambda e, bank=bank, dc=dc, wbuf=wbuf, j=j: e.matmul(
                            bank[:, 0:512],
                            wbuf[:, dc * 1024 + j * 128: dc * 1024 + (j + 1) * 128],
                            xqb[:, dc * TH + 0: dc * TH + 512],
                            start=(dc == 0), stop=(dc == NDC - 1)))
                    for dc in range(NDC):
                        t_op(lambda e, dc=dc, wbuf=wbuf, j=j: e.matmul(
                            pb[2][:, 0:2],
                            wbuf[:, dc * 1024 + j * 128: dc * 1024 + (j + 1) * 128],
                            xqb[:, dc * TH + 512: dc * TH + 514],
                            start=(dc == 0), stop=(dc == NDC - 1)))
                    P.bar()
                    v_op(lambda e, hc=hc, bank=bank: e.tensor_mul(hv(hc, 0, 512),
                                                                  bank[:, 0:512], sc1[:, 0:512]))
                    v_op(lambda e, hc=hc: e.tensor_mul(hv(hc, 512, 514), pb[2][:, 0:2],
                                                       sc1[:, 512:514]))
                    P.bar()
                if g + 1 < NG:
                    lw_have = lw_next
            if it == 0:
                v_op(lambda e: e.tensor_copy(h[:].rearrange("p (hc u) -> p hc u", u=TH)[:, :, 0:1], zer[:, 0:NHC]))
            if it == NT - 1:
                v_op(lambda e: e.tensor_copy(h[:].rearrange("p (hc u) -> p hc u", u=TH)[:, :, TH - 1:TH], zer[:, 0:NHC]))
            P.bar()

            if stop == "mm1" and it == 0:
                P.bar()
                fstk.close()
                return "stopped"
            for hc in range(NHC):
                yb = yscr[hc % 2]
                if hc >= 2:
                    P.wait(V, "cvb", P.count("cvb") - 1)
                v_op(lambda e, hc=hc, yb=yb: e.tensor_scalar(
                    yb[:], hv(hc, 1, 513), convw[:, hc * 3 + 1:hc * 3 + 2],
                    convb[:, hc:hc + 1], OP.mult, OP.add))
                v_op(lambda e, hc=hc, yb=yb: e.scalar_tensor_tensor(
                    yb[:], hv(hc, 0, 512), convw[:, hc * 3 + 0:hc * 3 + 1], yb[:],
                    OP.mult, OP.add), dep=True)
                v_op(lambda e, hc=hc, yb=yb: e.scalar_tensor_tensor(
                    yb[:], hv(hc, 2, 514), convw[:, hc * 3 + 2:hc * 3 + 3], yb[:],
                    OP.mult, OP.add), sig="cva", dep=True)
                P.wait(S, "cva")
                s_op(lambda e, hc=hc, yb=yb: e.activation(hv(hc, 1, 513), yb[:], AF.Silu),
                     sig="cvb")
            P.bar()

            if dbg_dst is not None and it == 0 and dbg.get("d_h") is not None:
                dma_out(dbg["d_h"], h[:])
                P.wait(G, "st")
            if stop == "conv" and it == 0:
                P.bar()
                fstk.close()
                return "stopped"
            v_op(lambda e: e.tensor_reduce(
                smax[:, 0:512], h[:].rearrange("p (hc u) -> p u hc", u=TH)[:, 1:513, :],
                AX.X, OP.max, apply_absolute_value=True))
            P.bar()
            for g in range(4):
                t_transpose(pb[6][:, g * 128:(g + 1) * 128], smax[:, g * 128:(g + 1) * 128])
            P.bar()
            for g in range(4):
                v_op(lambda e, g=g: e.tensor_reduce(tg[:, 10 + g:11 + g],
                                                    pb[6][:, g * 128:(g + 1) * 128], AX.X, OP.max))
            v_op(lambda e: e.tensor_scalar(tg[:, 10:14], tg[:, 10:14], 1e-5, None, OP.max), dep=True)
            v_op(lambda e: e.reciprocal(stat[:, 0:4], tg[:, 10:14]), dep=True)
            v_op(lambda e: e.tensor_scalar(tg[:, 10:14], stat[:, 0:4], 127.0, None, OP.mult), dep=True)
            v_op(lambda e: e.reciprocal(stat[:, 0:4], tg[:, 10:14]), dep=True)
            v_op(lambda e: e.tensor_scalar(stat[:, 0:4], stat[:, 0:4], wdeq[:, 1:2], None, OP.mult), dep=True)
            v_op(lambda e: e.tensor_scalar(stat[:, 0:4], stat[:, 0:4], 0.5, None, OP.mult), dep=True)
            P.bar()
            emit_bcast(10, 4, 4, 0)
            v_op(lambda e: e.tensor_copy(scq[:, 0:512], pb[4][:, 0:512]))
            v_op(lambda e: e.tensor_copy(tg[:, 10:14], stat[:, 0:4]))
            P.bar()
            emit_bcast(10, 4, 4, 0)
            v_op(lambda e: e.tensor_copy(sc2[:, 0:512], pb[4][:, 0:512]))
            P.bar()

            lw2 = []
            lw2.append(dma_w(wg2[0][:], w2qT[0:128, :]))
            h4 = h[:].rearrange("p (hc u) -> p hc u", u=TH)
            for q in range(8):
                ybq = yqg[q % 2]
                if q >= 2:
                    P.wait(V, "m2h", P.count("m2h") - 4)
                hgv = h4[:, q * 4:(q + 1) * 4, 1:513]
                niv = ni32[:, 0:4 * T].rearrange("p (c t) -> p c t", t=T)
                xtv = xtmp[:, 0:4 * T].rearrange("p (c t) -> p c t", t=T)
                xhv = xhat[:, 0:4 * T].rearrange("p (c t) -> p c t", t=T)
                ybv = ybq[:].rearrange("p (c t) -> p c t", t=T)
                scb = scq[:, 0:512].unsqueeze(1).broadcast_to((128, 4, 512))
                v_op(lambda e, hgv=hgv, niv=niv, scb=scb: e.tensor_mul(niv, hgv, scb))
                v_op(lambda e, hgv=hgv, xtv=xtv, scb=scb: e.tensor_mul(xtv, hgv, scb))
                v_op(lambda e, niv=niv, xtv=xtv, xhv=xhv: e.scalar_tensor_tensor(
                    xhv, niv, -1.0, xtv, OP.mult, OP.add))
                v_op(lambda e, xtv=xtv, xhv=xhv: e.tensor_scalar(xtv, xhv, 0.5, None, OP.is_gt), dep=True)
                v_op(lambda e, xtv=xtv, niv=niv: e.scalar_tensor_tensor(
                    xtv, xtv, 0.0, niv, OP.bypass, OP.add), dep=True)
                v_op(lambda e, xhv=xhv: e.tensor_scalar(xhv, xhv, -0.5, None, OP.is_lt), dep=True)
                v_op(lambda e, ybv=ybv, xhv=xhv, xtv=xtv: e.scalar_tensor_tensor(
                    ybv, xhv, -1.0, xtv, OP.mult, OP.add), sig="q2", dep=True)
                if dbg_dst is not None and it == 0 and q == 0 and dbg.get("d_yq") is not None:
                    P.bar()
                    dma_out(dbg["d_yq"], ybq[:])
                    dma_out(dbg["d_sc"][:, 0:T], scq[:, 0:T])
                    dma_out(dbg["d_sc"][:, T:2 * T], sc2[:, 0:T])
                    P.wait(G, "st")
                for jj in range(4):
                    hc = q * 4 + jj
                    if hc + 1 < NHC:
                        P.wait("sync", "m2h")
                        lw2.append(dma_w(wg2[(hc + 1) % 2][:], w2qT[128 * (hc + 1):128 * (hc + 2), :]))
                    if jj == 0:
                        P.wait(TE, "q2")
                    P.wait(TE, "lw", lw2[hc])
                    for oc in range(NDC):
                        sig = "m2h" if oc == NDC - 1 else None
                        t_op(lambda e, hc=hc, oc=oc, jj=jj, q=q: e.matmul(
                            pb[oc][:, 0:512],
                            wg2[hc % 2][:, oc * 128:(oc + 1) * 128],
                            yqg[q % 2][:, jj * T:(jj + 1) * T],
                            start=(hc == 0), stop=(hc == NHC - 1)), sig=sig)
            P.bar()
            for oc in range(NDC):
                v_op(lambda e, oc=oc: e.tensor_mul(xv(xtmp, oc, 0, 512), pb[oc][:, 0:512],
                                                   sc2[:, 0:512]))
            for oc in range(NDC):
                v_op(lambda e, oc=oc: e.tensor_add(xv(xhat, oc, 0, 512), xv(xtmp, oc, 0, 512),
                                                   xv(xs, oc, 1, 513)), dep=(oc == 0))
            P.bar()
            if stop == "mm2" and it == 0:
                P.bar()
                fstk.close()
                return "stopped"
            dst3 = dst.rearrange("(dc p) n -> p dc n", p=128)[:, :, t0:t0 + T]
            src3 = xhat[:].rearrange("p (dc u) -> p dc u", u=TH)[:, :, 0:512]
            dma_out(dst3, src3)
            if dbg_dst is not None:
                dma_out(dbg_dst.rearrange("(dc p) n -> p dc n", p=128)[:, :, t0:t0 + T], src3)
            P.wait(G, "st")
            P.bar()
            if stop == "t0" and it == 0:
                fstk.close()
                return "stopped"
        P.wait("sync", "st")
        P.bar()
        fstk.close()


    def emit_attn(src, dst, dbg_dst=None, stop=None):
        astk = contextlib.ExitStack()

        def asb(name, shape, dtype=F32):
            return astk.enter_context(nc.sbuf_tensor("at" + name, shape, dtype))

        q = asb("q", [128, H * T])
        k = asb("k", [128, H * T])
        vtm = asb("vtm", [128, NCH * 1024])
        gate = asb("gate", [128, H * T])
        f1b = asb("f1b", [128, T])
        g1b = asb("g1b", [128, T])
        sgb = asb("sgb", [128, 4 * T])
        s_fm = asb("s_fm", [128, 4 * T])
        g_fm = asb("g_fm", [128, 4 * T])
        Wp = [asb(f"Wp{i}", [128, 1024]) for i in range(2)]
        wf2b = asb("wf2b", [128, H * M])
        wg2b = asb("wg2b", [128, D])
        Lr = asb("Lr", [128, 512])
        Lf = asb("Lf", [128, 512])
        eLf = asb("eLf", [128, 512])
        eLnf = asb("eLnf", [128, 512])
        Sf = asb("Sf", [128, 512])
        Stm = asb("Stm", [128, 512])
        eLtm = asb("eLtm", [128, 512])
        okb = asb("okb", [128, 512])
        nbb = asb("nbb", [128, 512])
        qv = asb("qv", [128, 512])
        vqt = asb("vqt", [128, 512])
        vqf = asb("vqf", [128, 512])
        Gm = asb("Gm", [128, H * 128])
        Ktm = asb("Ktm", [128, H * 128])
        Wm = asb("Wm", [128, 128])
        su = asb("su", [128, 512])
        tu = asb("tu", [128, 4 * 128])
        ecc = asb("ecc", [128, 4])
        elc = asb("elc", [128, 4])
        mst = asb("mst", [128, 32])
        sxc = asb("sxc", [128, 16])

        def xv(buf, dc, lo, hi):
            return buf[:, dc * TH + lo: dc * TH + hi]

        xs3 = xs[:].rearrange("p (dc u) -> p dc u", u=TH)

        def proj_fm(wdram, wcols, nout_chunks, rhs_fn, out_cb, pjsem):
            lws = []
            lws.append(dma_w(Wp[0][:, 0:wcols], wdram[0:128, :]))
            for dc in range(NDC):
                if dc + 1 < NDC:
                    P.wait("sync", pjsem)
                    lws.append(dma_w(Wp[(dc + 1) % 2][:, 0:wcols],
                                     wdram[128 * (dc + 1):128 * (dc + 2), :]))
                P.wait(TE, "lw", lws[dc])
                for oc in range(nout_chunks):
                    sig = pjsem if oc == nout_chunks - 1 else None
                    t_op(lambda e, dc=dc, oc=oc: e.matmul(
                        pb[oc][:, 0:512],
                        Wp[dc % 2][:, oc * 128:(oc + 1) * 128],
                        rhs_fn(dc),
                        start=(dc == 0), stop=(dc == NDC - 1)), sig=sig)
            P.bar()
            for oc in range(nout_chunks):
                out_cb(oc)
            P.bar()

        for it in range(NT):
            t0 = it * T
            dma_in(xs3[:, :, 0:T],
                   src.rearrange("(dc p) n -> p dc n", p=128)[:, :, t0:t0 + T])
            P.wait(V, "ld")
            P.wait(S, "ld")
            P.wait(TE, "ld")
            P.bar()
            emit_ln(T)
            if stop == "ln" and it == 0:
                P.bar()
                astk.close()
                return "stopped"

            def xh_rhs(dc):
                return xv(xhat, dc, 0, T)

            proj_fm(W["Wq"], D, NDC, xh_rhs,
                    lambda oc: s_op(lambda e, oc=oc: e.activation(
                        q[:, oc * T:(oc + 1) * T], pb[oc][:, 0:512], AF.Silu)), "pj")
            proj_fm(W["Wk"], D, NDC, xh_rhs,
                    lambda oc: s_op(lambda e, oc=oc: e.activation(
                        k[:, oc * T:(oc + 1) * T], pb[oc][:, 0:512], AF.Silu)), "pj")
            lws = [dma_w(Wp[0][:], W["Wv"][0:128, :])]
            for dc in range(NDC):
                if dc + 1 < NDC:
                    P.wait("sync", "pj")
                    lws.append(dma_w(Wp[(dc + 1) % 2][:], W["Wv"][128 * (dc + 1):128 * (dc + 2), :]))
                P.wait(TE, "lw", lws[dc])
                for ts in range(NCH):
                    for hf in range(2):
                        bank = ts * 2 + hf
                        sig = "pj" if (ts == NCH - 1 and hf == 1) else None
                        t_op(lambda e, dc=dc, ts=ts, hf=hf, bank=bank: e.matmul(
                            pb[bank][:, 0:512],
                            xv(xhat, dc, ts * C, (ts + 1) * C),
                            Wp[dc % 2][:, hf * 512:(hf + 1) * 512],
                            start=(dc == 0), stop=(dc == NDC - 1)), sig=sig)
            P.bar()
            for ts in range(NCH):
                for hf in range(2):
                    v_op(lambda e, ts=ts, hf=hf: e.tensor_copy(
                        vtm[:, ts * 1024 + hf * 512: ts * 1024 + (hf + 1) * 512],
                        pb[ts * 2 + hf][:, 0:512]))
            P.bar()
            for nm, outb in (("Wf1", f1b), ("Wg1", g1b)):
                lwf = [dma_w(Wp[0][:, 0:128], W[nm][0:128, :])]
                for dc in range(NDC):
                    if dc + 1 < NDC:
                        P.wait("sync", "pj")
                        lwf.append(dma_w(Wp[(dc + 1) % 2][:, 0:128],
                                         W[nm][128 * (dc + 1):128 * (dc + 2), :]))
                    P.wait(TE, "lw", lwf[dc])
                    t_op(lambda e, dc=dc: e.matmul(
                        pb[0][:, 0:512], Wp[dc % 2][:, 0:128], xv(xhat, dc, 0, T),
                        start=(dc == 0), stop=(dc == NDC - 1)), sig="pj")
                P.bar()
                v_op(lambda e, outb=outb: e.tensor_copy(outb[:], pb[0][:, 0:512]))
                P.bar()
            lw_f2 = dma_w(wf2b[:], W["Wf2"][:, :])
            P.wait(TE, "lw", lw_f2)
            for fc in range(4):
                t_op(lambda e, fc=fc: e.matmul(
                    pb[fc][:, 0:512], wf2b[:, fc * 128:(fc + 1) * 128], f1b[:],
                    start=True, stop=True))
            P.bar()
            for fc in range(4):
                s_op(lambda e, fc=fc: e.activation(sgb[:, fc * T:(fc + 1) * T],
                                                   pb[fc][:, 0:512], AF.Sigmoid))
            P.bar()
            lw_g2 = dma_w(wg2b[:], W["Wg2"][:, :])
            P.wait(TE, "lw", lw_g2)
            for oc in range(NDC):
                t_op(lambda e, oc=oc: e.matmul(
                    pb[oc][:, 0:512], wg2b[:, oc * 128:(oc + 1) * 128], g1b[:],
                    start=True, stop=True))
            P.bar()
            for oc in range(NDC):
                s_op(lambda e, oc=oc: e.activation(
                    gate[:, oc * T:(oc + 1) * T], pb[oc][:, 0:512], AF.Sigmoid))
            P.bar()
            s_op(lambda e: e.activation(g_fm[:], sgb[:], AF.Ln))
            s_op(lambda e: e.activation(s_fm[:], g_fm[:], AF.Exp), dep=True)
            P.bar()
            v_op(lambda e: e.tensor_scalar(s_fm[:], s_fm[:], -1.0, 1.0, OP.mult, OP.add))
            P.bar()

            if dbg_dst is not None and it == 0:
                dma_out(dbg["a_q"], q[:])
                dma_out(dbg["a_k"], k[:])
                dma_out(dbg["a_sg"], sgb[:])
                P.wait(G, "st")
            if stop == "proj" and it == 0:
                P.bar()
                astk.close()
                return "stopped"
            for ts in range(NCH):
                for fc in range(4):
                    v_op(lambda e, fc=fc, ts=ts: e.tensor_tensor_scan(
                        Lr[:, fc * C:(fc + 1) * C],
                        g_fm[:, fc * T + ts * C: fc * T + (ts + 1) * C],
                        g_fm[:, fc * T + ts * C: fc * T + (ts + 1) * C],
                        0.0, OP.add, OP.bypass))
                for fc in range(4):
                    v_op(lambda e, fc=fc: e.tensor_scalar(
                        Lf[:, fc * C:(fc + 1) * C], Lr[:, fc * C:(fc + 1) * C],
                        Lr[:, fc * C + 63:fc * C + 64], None, OP.subtract),
                        dep=(fc == 0))
                P.bar()
                if stop == "cum" and it == 0 and ts == 0:
                    P.bar()
                    astk.close()
                    return "stopped"
                s_op(lambda e: e.activation(eLf[:], Lf[:], AF.Exp))
                s_op(lambda e: e.activation(eLnf[:], Lf[:], AF.Exp, scale=-1.0))
                s_op(lambda e: e.activation(ecc[:], Lr[:].rearrange("p (fc c) -> p c fc", c=C)[:, 63, :], AF.Exp))
                s_op(lambda e: e.activation(elc[:], Lf[:].rearrange("p (fc c) -> p c fc", c=C)[:, C - 1, :], AF.Exp))
                P.bar()
                v_op(lambda e, ts=ts: e.tensor_mul(
                    Sf[:].rearrange("p (fc c) -> p fc c", c=C),
                    s_fm[:].rearrange("p (fc t) -> p fc t", t=T)[:, :, ts * C:(ts + 1) * C],
                    eLnf[:].rearrange("p (fc c) -> p fc c", c=C)))
                P.bar()
                if stop == "exp" and it == 0 and ts == 0:
                    P.bar()
                    astk.close()
                    return "stopped"
                for fc in range(4):
                    t_transpose(pb[6][:, fc * 128:(fc + 1) * 128], Sf[:, fc * C:(fc + 1) * C])
                P.bar()
                v_op(lambda e: e.tensor_copy(Stm[:], pb[6][:, 0:512]))
                P.bar()
                for fc in range(4):
                    t_transpose(pb[6][:, fc * 128:(fc + 1) * 128], eLf[:, fc * C:(fc + 1) * C])
                P.bar()
                v_op(lambda e: e.tensor_copy(eLtm[:], pb[6][:, 0:512]))
                P.bar()
                for hh in range(0, H, 4):
                    for h2 in range(4):
                        h = hh + h2
                        t_transpose(pb[7][:, h2 * 128:(h2 + 1) * 128],
                                    k[:, h * T + ts * C: h * T + (ts + 1) * C])
                    P.bar()
                    v_op(lambda e, hh=hh: e.tensor_copy(
                        Ktm[:, hh * 128:(hh + 4) * 128], pb[7][:, 0:512]))
                    P.bar()
                if stop == "tp" and it == 0 and ts == 0:
                    P.bar()
                    astk.close()
                    return "stopped"
                for fc in range(4):
                    t_transpose(pb[7][0:1, fc * 128:(fc + 1) * 128], ecc[:, fc:fc + 1])
                for fc in range(4):
                    t_transpose(pb[6][0:1, fc * 128:(fc + 1) * 128], elc[:, fc:fc + 1])
                P.bar()
                v_op(lambda e: e.tensor_copy(rowb[:], pb[7][0:1, 0:512]))
                v_op(lambda e: e.tensor_copy(nbb[0:1, 0:512], pb[6][0:1, 0:512]))
                P.bar()
                for h in range(H):
                    t_bcast(pb[5][:, h * 64:(h + 1) * 64],
                            rowb[0:1, (h // 2) * 128 + (h % 2) * 64: (h // 2) * 128 + (h % 2) * 64 + 64])
                P.bar()
                v_op(lambda e: e.tensor_mul(su[:], S_st[:], pb[5][:, 0:512]))
                for ft in range(4):
                    v_op(lambda e, ft=ft: e.tensor_scalar(
                        tu[:, ft * 128:(ft + 1) * 128], T_st[:, ft * 128:(ft + 1) * 128],
                        ecc[:, ft:ft + 1], None, OP.mult))
                P.bar()
                if stop == "ec" and it == 0 and ts == 0:
                    P.bar()
                    astk.close()
                    return "stopped"
                for h in range(H):
                    t_op(lambda e, h=h, ts=ts: e.matmul(
                        pb[3 + h // 4][:, (h % 4) * 128:(h % 4) * 128 + 128],
                        k[:, h * T + ts * C: h * T + (ts + 1) * C],
                        q[:, h * T + ts * C: h * T + (ts + 1) * C],
                        start=True, stop=True))
                P.bar()
                for h in range(H):
                    v_op(lambda e, h=h: e.tensor_mul(
                        Gm[:, h * 128:(h + 1) * 128],
                        pb[3 + h // 4][:, (h % 4) * 128:(h % 4) * 128 + 128], tri[:]))
                P.bar()
                if stop == "g" and it == 0 and ts == 0:
                    P.bar()
                    astk.close()
                    return "stopped"
                for h in range(H):
                    t_bcast(pb[4][:, h * 64:(h + 1) * 64],
                            nbb[0:1, (h // 2) * 128 + (h % 2) * 64: (h // 2) * 128 + (h % 2) * 64 + 64])
                for h in range(H):
                    t_op(lambda e, h=h, ts=ts: e.matmul(
                        pb[2][:, h * 64:(h + 1) * 64],
                        q[:, h * T + ts * C: h * T + (ts + 1) * C],
                        su[:, h * 64:(h + 1) * 64],
                        start=True, stop=False))
                    t_op(lambda e, h=h: e.matmul(
                        pb[2][:, h * 64:(h + 1) * 64],
                        Gm[:, h * 128:(h + 1) * 128],
                        Stm[:, h * 64:(h + 1) * 64],
                        start=False, stop=True))
                P.bar()
                if stop == "ok" and it == 0 and ts == 0:
                    P.bar()
                    astk.close()
                    return "stopped"
                v_op(lambda e: e.tensor_mul(okb[:], pb[2][:, 0:512], eLtm[:]))
                v_op(lambda e: e.tensor_reduce(sxc[:, 0:8],
                                               okb[:].rearrange("p (h m) -> p h m", m=M),
                                               AX.X, OP.max, negate=True), dep=True)
                v_op(lambda e: e.tensor_copy(nbb[:], sxc[:, 0:8].broadcast_to((128, 8, M))),
                     dep=True)
                v_op(lambda e: e.tensor_add(okb[:], okb[:], nbb[:]), dep=True)
                P.bar()
                s_op(lambda e: e.activation(okb[:], okb[:], AF.Exp))
                P.bar()
                v_op(lambda e: e.tensor_reduce(sxc[:, 0:8],
                                               okb[:].rearrange("p (h m) -> p h m", m=M),
                                               AX.X, OP.add))
                v_op(lambda e: e.reciprocal(sxc[:, 8:16], sxc[:, 0:8]), dep=True)
                v_op(lambda e: e.tensor_copy(nbb[:], sxc[:, 8:16].broadcast_to((128, 8, M))),
                     dep=True)
                v_op(lambda e: e.tensor_mul(qv[:], okb[:], nbb[:]), dep=True)
                v_op(lambda e: e.tensor_mul(vqt[:], qv[:], eLtm[:]), dep=True)
                P.bar()
                if dbg_dst is not None and it == 0 and ts == 0:
                    dma_out(dbg["a_ok"], okb[:])
                    dma_out(dbg["a_qv"], qv[:])
                    P.wait(G, "st")
                for fc in range(4):
                    t_transpose(pb[6][:, fc * 128:(fc + 1) * 128], vqt[:, fc * C:(fc + 1) * C])
                P.bar()
                v_op(lambda e: e.tensor_copy(vqf[:], pb[6][0:128, 0:512]))
                P.bar()
                if stop == "sm" and it == 0 and ts == 0:
                    P.bar()
                    astk.close()
                    return "stopped"
                if dbg_dst is not None and it == 0 and ts == 0:
                    dma_out(dbg["a_sf"], Sf[:])
                    dma_out(dbg["a_vqf"], vqf[:])
                    P.wait(G, "st")
                for h in range(H):
                    fc = h // 2
                    r0 = (h % 2) * 64
                    sfr = lambda a, r0=r0, fc=fc: Sf[r0:r0 + 64, fc * C + a * 64: fc * C + (a + 1) * 64]
                    vqr = lambda b, r0=r0, fc=fc: vqf[r0:r0 + 64, fc * C + b * 64: fc * C + (b + 1) * 64]
                    t_op(lambda e, h=h, sfr=sfr, vqr=vqr: e.matmul(
                        pb[3][:, 0:0] if False else pb[3][0:64, 0:64], sfr(0), vqr(0),
                        start=True, stop=True))
                    t_op(lambda e, sfr=sfr, vqr=vqr: e.matmul(
                        pb[3][0:64, 64:128], sfr(0), vqr(1), start=True, stop=True))
                    if stop != "wmm2":
                        t_op(lambda e, sfr=sfr, vqr=vqr: e.matmul(
                            pb[3][64:128, 64:128], sfr(1), vqr(1), start=True, stop=True))
                    P.bar()
                    v_op(lambda e: e.tensor_mul(Wm[0:64, 0:64], pb[3][0:64, 0:64], trih[:]))
                    v_op(lambda e: e.tensor_copy(Wm[0:64, 64:128], pb[3][0:64, 64:128]))
                    if stop != "wmm2":
                        v_op(lambda e: e.tensor_mul(Wm[64:128, 64:128], pb[3][64:128, 64:128], trih[:]))
                    P.bar()
                    if stop in ("wmm", "wmm2"):
                        continue
                    t_op(lambda e, h=h, r0=r0, fc=fc, ts=ts: e.matmul(
                        pb[1][:, 0:128],
                        tu[r0:r0 + 64, fc * 128:(fc + 1) * 128],
                        vqf[r0:r0 + 64, fc * C:(fc + 1) * C],
                        start=True, stop=True))
                    t_op(lambda e, h=h, ts=ts: e.matmul(
                        pb[0][:, 0:64],
                        vtm[0:64, ts * 1024 + h * 128: ts * 1024 + (h + 1) * 128],
                        Wm[0:64, 0:64], start=True, stop=True))
                    t_op(lambda e, h=h, ts=ts: e.matmul(
                        pb[0][:, 64:128],
                        vtm[0:64, ts * 1024 + h * 128: ts * 1024 + (h + 1) * 128],
                        Wm[0:64, 64:128], start=True, stop=True))
                    t_op(lambda e, h=h, ts=ts: e.matmul(
                        pb[2][:, 0:64],
                        vtm[64:128, ts * 1024 + h * 128: ts * 1024 + (h + 1) * 128],
                        Wm[64:128, 64:128], start=True, stop=True))
                    P.bar()
                    v_op(lambda e: e.tensor_copy(okb[:, 0:128], pb[1][:, 0:128]))
                    v_op(lambda e: e.tensor_add(okb[:, 0:128], okb[:, 0:128],
                                                pb[0][:, 0:128]), dep=True)
                    v_op(lambda e: e.tensor_add(okb[:, 64:128], okb[:, 64:128],
                                                pb[2][:, 0:64]), dep=True)
                    if dbg_dst is not None and it == 0 and ts == 0:
                        v_op(lambda e: e.tensor_copy(Wm[64:128, 0:64], zer[0:64, 0:64]))
                        P.bar()
                        dma_out(dbg["a_ovh"][:, h * 128:(h + 1) * 128], okb[:, 0:128])
                        dma_out(dbg["a_wmh"][:, h * 128:(h + 1) * 128], Wm[:])
                        P.wait(G, "st")
                    v_op(lambda e, h=h, ts=ts: e.tensor_mul(
                        gate[:, h * T + ts * C: h * T + (ts + 1) * C],
                        okb[:, 0:128],
                        gate[:, h * T + ts * C: h * T + (ts + 1) * C]), dep=True)
                    P.bar()
                if stop in ("w", "wmm", "wmm2", "wov0", "wov1", "wov2", "wov3", "wov3b", "wovev") and it == 0 and ts == 0:
                    P.bar()
                    astk.close()
                    return "stopped"
                P.bar()
                if dbg_dst is not None and it == 0 and ts == 0 and False:
                    v_op(lambda e: e.tensor_copy(Wm[64:128, 0:64], zer[0:64, 0:64]))
                    P.bar()
                    dma_out(dbg["a_Wm"], Wm[:])
                    P.wait(G, "st")
                    v_op(lambda e: e.tensor_copy(nbb[:, 0:128], pb[0][:, 0:128]))
                    P.bar()
                    dma_out(dbg["a_ov"], nbb[:, 0:128])
                    P.wait(G, "st")
                    P.bar()
                for h in range(H):
                    t_op(lambda e, h=h: e.matmul(
                        pb[5][:, h * 64:(h + 1) * 64],
                        Ktm[:, h * 128:(h + 1) * 128],
                        Stm[:, h * 64:(h + 1) * 64], start=True, stop=True))
                for h in range(H):
                    r0 = (h % 2) * 64
                    t_op(lambda e, h=h, r0=r0, ts=ts: e.matmul(
                        pb[6][r0:r0 + 64, (h // 2) * 128:(h // 2 + 1) * 128],
                        Stm[:, h * 64:(h + 1) * 64],
                        vtm[:, ts * 1024 + h * 128: ts * 1024 + (h + 1) * 128],
                        start=True, stop=True))
                P.bar()
                v_op(lambda e: e.tensor_add(su[:], su[:], pb[5][:, 0:512]))
                v_op(lambda e: e.tensor_mul(S_st[:], su[:], pb[4][:, 0:512]), dep=True)
                v_op(lambda e: e.tensor_add(tu[:], tu[:], pb[6][:, 0:512]))
                for ft in range(4):
                    v_op(lambda e, ft=ft: e.tensor_scalar(
                        T_st[:, ft * 128:(ft + 1) * 128], tu[:, ft * 128:(ft + 1) * 128],
                        elc[:, ft:ft + 1], None, OP.mult), dep=(ft == 0))
                P.bar()
                if dbg_dst is not None and it == 0 and ts == 0:
                    dma_out(dbg["a_og"], gate[:])
                    dma_out(dbg["a_S"], S_st[:])
                    dma_out(dbg["a_T"], T_st[:])
                    P.wait(G, "st")
                if stop == "chunk0" and it == 0 and ts == 0:
                    P.bar()
                    astk.close()
                    return "stopped"

            s_op(lambda e: e.activation(xtmp[:, 0:H * T], gate[:], AF.Square))
            P.bar()
            for h in range(H):
                for ts in range(NCH):
                    t_op(lambda e, h=h, ts=ts: e.matmul(
                        pb[7][:, h * 4 + ts: h * 4 + ts + 1],
                        xtmp[:, h * T + ts * C: h * T + (ts + 1) * C],
                        onest[:, 0:1], start=True, stop=True))
            P.bar()
            v_op(lambda e: e.tensor_scalar(mst[:], pb[7][:, 0:32], 1.0 / DK, EPS,
                                           OP.mult, OP.add))
            P.bar()
            s_op(lambda e: e.activation(mst[:], mst[:], AF.Ln))
            s_op(lambda e: e.activation(mst[:], mst[:], AF.Exp, scale=-0.5), dep=True)
            P.bar()
            for h in range(H):
                for ts in range(NCH):
                    t_transpose(pb[6][0:1, ts * 128:(ts + 1) * 128],
                                mst[:, h * 4 + ts: h * 4 + ts + 1])
                P.bar()
                v_op(lambda e: e.tensor_copy(rowb[:], pb[6][0:1, 0:512]))
                P.bar()
                for ts in range(NCH):
                    t_bcast(pb[5][:, ts * 128:(ts + 1) * 128], rowb[0:1, ts * 128:(ts + 1) * 128])
                P.bar()
                v_op(lambda e, h=h: e.scalar_tensor_tensor(
                    k[:, h * T:(h + 1) * T], gate[:, h * T:(h + 1) * T],
                    normw[:, h:h + 1], pb[5][:, 0:512], OP.mult, OP.mult))
                P.bar()
            proj_fm(W["Wo"], D, NDC, lambda dc: k[:, dc * T:(dc + 1) * T],
                    lambda oc: v_op(lambda e, oc=oc: e.tensor_add(
                        xv(xhat, oc, 0, T), pb[oc][:, 0:512], xv(xs, oc, 0, T))), "pj")
            dst3 = dst.rearrange("(dc p) n -> p dc n", p=128)[:, :, t0:t0 + T]
            src3 = xhat[:].rearrange("p (dc u) -> p dc u", u=TH)[:, :, 0:T]
            dma_out(dst3, src3)
            if dbg_dst is not None:
                dma_out(dbg_dst.rearrange("(dc p) n -> p dc n", p=128)[:, :, t0:t0 + T], src3)
            P.wait(G, "st")
            P.bar()
            if stop == "tile0" and it == 0:
                astk.close()
                return "stopped"
        P.wait("sync", "st")
        P.bar()
        astk.close()

    def emit_attn_copy(src, dst, dbg_dst=None):
        for it in range(NT):
            t0 = it * T
            dma_in(xs[:, 0:NDC * T].rearrange("p (dc n) -> p dc n", n=T),
                   src.rearrange("(dc p) n -> p dc n", p=128)[:, :, t0:t0 + T])
            P.wait(G, "ld")
            dma_out(dst.rearrange("(dc p) n -> p dc n", p=128)[:, :, t0:t0 + T],
                    xs[:, 0:NDC * T].rearrange("p (dc n) -> p dc n", n=T))
            if dbg_dst is not None:
                dma_out(dbg_dst.rearrange("(dc p) n -> p dc n", p=128)[:, :, t0:t0 + T],
                        xs[:, 0:NDC * T].rearrange("p (dc n) -> p dc n", n=T))
            P.wait("sync", "st")
            P.bar()

    def finish_early():
        dma_out(outT.rearrange("(dc p) n -> p dc n", p=128)[:, :, 0:512],
                xs[:, 0:NDC * 512].rearrange("p (dc n) -> p dc n", n=512))
        P.wait(G, "st")
        P.bar()
        P.finalize()
        ctx.close()
        return nc

    if stop_after in ("prologue", "consts", "w1"):
        dma_in(xs[:, 0:NDC * 512].rearrange("p (dc n) -> p dc n", n=512),
               xT.rearrange("(dc p) n -> p dc n", p=128)[:, :, 0:512])
        P.wait(G, "ld")
        return finish_early()
    if stop_after.startswith("ffn_"):
        r = emit_ffn(xT, x1d, dbg.get("x1o"), stop=stop_after[4:])
        return finish_early()
    emit_ffn(xT, x1d, dbg.get("x1o"))
    if stop_after == "ffnA":
        dma_in(xs[:, 0:NDC * 512].rearrange("p (dc n) -> p dc n", n=512),
               x1d.rearrange("(dc p) n -> p dc n", p=128)[:, :, 0:512])
        P.wait(G, "ld")
        return finish_early()
    if attn_mode == "copy":
        emit_attn_copy(x1d, x2d, dbg.get("x2o"))
    elif stop_after.startswith("attn_"):
        emit_attn(x1d, x2d, dbg.get("x2o"), stop=stop_after[5:])
        return finish_early()
    else:
        emit_attn(x1d, x2d, dbg.get("x2o"))
    emit_ffn(x2d, outT, None, tag="c")
    P.wait(G, "st")
    P.bar()

    P.finalize()
    ctx.close()
    return nc


def _pack_consts():
    j = np.arange(128)
    tri = (j[:, None] <= j[None, :]).astype(np.float32)
    trih = (np.arange(64)[:, None] <= np.arange(64)[None, :]).astype(np.float32)
    return {
        "c_ident": np.eye(128, dtype=np.float32),
        "c_tri": tri,
        "c_trih": trih,
        "c_ones": np.ones((128, 128), np.float32),
        "c_zeros": np.zeros((128, 512), np.float32),
    }


def kernel(L0_attn_x, L0_attn_Wq, L0_attn_Wk, L0_attn_Wv, L0_attn_Wf1, L0_attn_Wf2,
           L0_attn_Wg1, L0_attn_Wg2, L0_attn_Wo, L0_attn_norm_w,
           L0_ffn_w1, L0_ffn_conv_w, L0_ffn_conv_b, L0_ffn_w2):
    x = np.asarray(L0_attn_x, dtype=np.float32)
    consts = _pack_consts()
    convw = np.ascontiguousarray(
        np.asarray(L0_ffn_conv_w, np.float32).reshape(NHC, 128, 3).transpose(1, 0, 2)
        .reshape(128, NHC * 3))
    convw = np.ascontiguousarray(
        np.asarray(L0_ffn_conv_w, np.float32).reshape(NHC, 128, 3)
        .transpose(1, 0, 2).reshape(128, NHC * 3))
    convb = np.ascontiguousarray(np.asarray(L0_ffn_conv_b, np.float32).reshape(NHC, 128).T)
    normw = np.ascontiguousarray(np.asarray(L0_attn_norm_w, np.float32).reshape(NDC, 128).T)

    nc = build_program(attn_mode="attn")
    base = {
        "Wq": np.asarray(L0_attn_Wq, np.float32), "Wk": np.asarray(L0_attn_Wk, np.float32),
        "Wv": np.asarray(L0_attn_Wv, np.float32), "Wo": np.asarray(L0_attn_Wo, np.float32),
        "Wf1": np.asarray(L0_attn_Wf1, np.float32), "Wf2": np.asarray(L0_attn_Wf2, np.float32),
        "Wg1": np.asarray(L0_attn_Wg1, np.float32), "Wg2": np.asarray(L0_attn_Wg2, np.float32),
        "w1": np.asarray(L0_ffn_w1, np.float32), "w2": np.asarray(L0_ffn_w2, np.float32),
        "c_convw": convw, "c_convb": convb, "c_normw": normw,
        **consts,
    }
    in_maps = []
    for b in range(NCORE):
        m = dict(base)
        m["xT"] = np.ascontiguousarray(x[b].T)
        in_maps.append(m)
    res = run_bass_kernel_spmd(nc, in_maps, core_ids=list(range(NCORE)))
    out = np.stack([np.asarray(res.results[b]["outT"]).T for b in range(NCORE)], axis=0)
    return out.astype(np.float32)


# version 09
# speedup vs baseline: 1.0782x; 1.0782x over previous
import contextlib
import numpy as np

import concourse.bass as bass
import concourse.mybir as mybir
from concourse.bass_utils import run_bass_kernel_spmd

F32 = mybir.dt.float32
BF16 = mybir.dt.bfloat16
I32 = mybir.dt.int32
AF = mybir.ActivationFunctionType
OP = mybir.AluOpType
AX = mybir.AxisListType

D = 1024
NTOK = 4096
T = 512
NT = NTOK // T
NDC = D // 128
HID = 4096
NHC = HID // 128
TH = T + 2
C = 128
NCH = T // C
H = 8
DK = 128
M = 64
EPS = 1e-5
NCORE = 8


class Prog:
    ENGS = ("sync", "vector", "scalar", "tensor", "gpsimd")
    DMA_ENGS = ("sync", "gpsimd")

    def __init__(self, nc, ctx):
        self.nc = nc
        self.streams = {k: [] for k in self.ENGS}
        self.sems = {}
        self.dirty = set()
        for e in self.ENGS:
            self.sems["bar_" + e] = [ctx.enter_context(nc.semaphore("bar_" + e)), 0]
            self.sems["ch_" + e] = [ctx.enter_context(nc.semaphore("ch_" + e)), 0]
        self._ctx = ctx

    def sem(self, name):
        self.sems[name] = [self._ctx.enter_context(self.nc.semaphore(name)), 0]

    def count(self, name):
        return self.sems[name][1]

    def do(self, eng, fn, sig=None, inc=1, dep=False):
        self.dirty.add(eng)
        if dep:
            tok = self._retire_token(eng)
            if tok is not None:
                h, v = tok
                self.streams[eng].append(
                    ["wait", lambda e, h=h, v=v: e.wait_ge(h, v), []])
        sigs = []
        if sig is not None:
            rec = self.sems[sig]
            rec[1] += inc
            sigs.append((rec[0], inc, rec[1]))
        self.streams[eng].append(["op", fn, sigs])
        return self.sems[sig][1] if sig is not None else None

    def _retire_token(self, eng):
        for ent in reversed(self.streams[eng]):
            if ent[0] == "op":
                if ent[2]:
                    h, inc, val = ent[2][-1]
                    return (h, val)
                rec = self.sems["ch_" + eng]
                rec[1] += 1
                ent[2].append((rec[0], 1, rec[1]))
                return (rec[0], rec[1])
        return None

    def wait(self, eng, sig, val=None):
        rec = self.sems[sig]
        v = rec[1] if val is None else val
        h = rec[0]
        self.streams[eng].append(["wait", lambda e, h=h, v=v: e.wait_ge(h, v), []])

    def bar(self):
        prods = sorted(self.dirty)
        if not prods:
            return
        toks = []
        for p in prods:
            if p in self.DMA_ENGS:
                rec = self.sems["bar_" + p]
                rec[1] += 1
                self.streams[p].append(
                    ["op", lambda e: e.nop(), [(rec[0], 1, rec[1])]])
                toks.append((p, rec[0], rec[1]))
            else:
                tok = self._retire_token(p)
                if tok is None:
                    raise RuntimeError(f"bar: no op to attach on {p}")
                toks.append((p, tok[0], tok[1]))
        for e in self.ENGS:
            for p, h, v in toks:
                if p == e:
                    continue
                self.streams[e].append(
                    ["wait", lambda en, h=h, v=v: en.wait_ge(h, v), []])
        self.dirty = set()

    def _replay(self, e, entries):
        for kind, fn, sigs in entries:
            inst = fn(e)
            for h, inc, _v in sigs:
                inst = inst.then_inc(h, inc)

    def finalize(self):
        with self.nc.Block() as block:
            @block.sync
            def _(e):
                self._replay(e, self.streams["sync"])

            @block.vector
            def _(e):
                self._replay(e, self.streams["vector"])

            @block.scalar
            def _(e):
                self._replay(e, self.streams["scalar"])

            @block.tensor
            def _(e):
                self._replay(e, self.streams["tensor"])

            @block.gpsimd
            def _(e):
                self._replay(e, self.streams["gpsimd"])


def build_program(attn_mode="attn", debug=False, stop_after="full"):
    nc = bass.Bass("TRN2", target_bir_lowering=False, detect_race_conditions=False)
    ctx = contextlib.ExitStack()

    xT = nc.dram_tensor("xT", [D, NTOK], F32, kind="ExternalInput").ap()
    outT = nc.dram_tensor("outT", [D, NTOK], F32, kind="ExternalOutput").ap()
    W = {}
    for name, shape in [
        ("Wq", [D, D]), ("Wk", [D, D]), ("Wv", [D, D]), ("Wo", [D, D]),
        ("Wf1", [D, DK]), ("Wf2", [DK, H * M]), ("Wg1", [D, DK]), ("Wg2", [DK, D]),
        ("w1", [HID, D]), ("w2", [D, HID]),
    ]:
        W[name] = nc.dram_tensor(name, shape, F32, kind="ExternalInput").ap()
    c_ident = nc.dram_tensor("c_ident", [128, 128], F32, kind="ExternalInput").ap()
    c_tri = nc.dram_tensor("c_tri", [128, 128], F32, kind="ExternalInput").ap()
    c_trih = nc.dram_tensor("c_trih", [64, 64], F32, kind="ExternalInput").ap()
    c_convw = nc.dram_tensor("c_convw", [128, 3 * NHC], F32, kind="ExternalInput").ap()
    c_convb = nc.dram_tensor("c_convb", [128, NHC], F32, kind="ExternalInput").ap()
    c_normw = nc.dram_tensor("c_normw", [128, NDC], F32, kind="ExternalInput").ap()
    c_ones = nc.dram_tensor("c_ones", [128, 128], F32, kind="ExternalInput").ap()
    c_zeros = nc.dram_tensor("c_zeros", [128, 512], F32, kind="ExternalInput").ap()
    x1d = nc.dram_tensor("x1d", [D, NTOK], F32).ap()
    x2d = nc.dram_tensor("x2d", [D, NTOK], F32).ap()
    w1qT = nc.dram_tensor("w1qT", [D, HID], BF16).ap()
    w2qT = nc.dram_tensor("w2qT", [HID, D], BF16).ap()
    dbg = {}
    if debug:
        for nm in ("x1o", "x2o"):
            dbg[nm] = nc.dram_tensor(nm, [D, NTOK], F32, kind="ExternalOutput").ap()
        dbg["d_xq"] = nc.dram_tensor("d_xq", [128, NDC * TH], BF16, kind="ExternalOutput").ap()
        dbg["d_h"] = nc.dram_tensor("d_h", [128, NHC * TH], F32, kind="ExternalOutput").ap()
        dbg["d_tg"] = nc.dram_tensor("d_tg", [128, 16], F32, kind="ExternalOutput").ap()
        dbg["d_xhat"] = nc.dram_tensor("d_xhat", [128, NDC * TH], F32, kind="ExternalOutput").ap()
        dbg["d_yq"] = nc.dram_tensor("d_yq", [128, 4 * T], BF16, kind="ExternalOutput").ap()
        dbg["d_sc"] = nc.dram_tensor("d_sc", [128, 2 * T], F32, kind="ExternalOutput").ap()
        dbg["a_q"] = nc.dram_tensor("a_q", [128, H * T], F32, kind="ExternalOutput").ap()
        dbg["a_k"] = nc.dram_tensor("a_k", [128, H * T], F32, kind="ExternalOutput").ap()
        dbg["a_sg"] = nc.dram_tensor("a_sg", [128, 4 * T], F32, kind="ExternalOutput").ap()
        dbg["a_ok"] = nc.dram_tensor("a_ok", [128, 512], F32, kind="ExternalOutput").ap()
        dbg["a_qv"] = nc.dram_tensor("a_qv", [128, 512], F32, kind="ExternalOutput").ap()
        dbg["a_og"] = nc.dram_tensor("a_og", [128, H * T], F32, kind="ExternalOutput").ap()
        dbg["a_S"] = nc.dram_tensor("a_S", [128, 512], F32, kind="ExternalOutput").ap()
        dbg["a_T"] = nc.dram_tensor("a_T", [128, 512], F32, kind="ExternalOutput").ap()
        dbg["a_Wm"] = nc.dram_tensor("a_Wm", [128, 128], F32, kind="ExternalOutput").ap()
        dbg["a_ov"] = nc.dram_tensor("a_ov", [128, 128], F32, kind="ExternalOutput").ap()
        dbg["a_ovh"] = nc.dram_tensor("a_ovh", [128, H * 128], F32, kind="ExternalOutput").ap()
        dbg["a_wmh"] = nc.dram_tensor("a_wmh", [128, H * 128], F32, kind="ExternalOutput").ap()
        dbg["a_sf"] = nc.dram_tensor("a_sf", [128, 512], F32, kind="ExternalOutput").ap()
        dbg["a_vqf"] = nc.dram_tensor("a_vqf", [128, 512], F32, kind="ExternalOutput").ap()

    def sb(name, shape, dtype=F32):
        return ctx.enter_context(nc.sbuf_tensor(name, shape, dtype))

    def psum(name, shape, dtype=F32):
        return ctx.enter_context(nc.psum_tensor(name, shape, dtype))

    ident = sb("ident", [128, 128])
    tri = sb("tri", [128, 128])
    trih = sb("trih", [64, 64])
    convw = sb("convw", [128, 3 * NHC])
    convb = sb("convb", [128, NHC])
    normw = sb("normw", [128, NDC])
    onest = sb("onest", [128, 128])
    zer = sb("zer", [128, 512])
    wdeq = sb("wdeq", [128, 2])
    wthr = sb("wthr", [128, 4])
    tg = sb("tg", [128, 16])
    tgrow = sb("tgrow", [16, 128])
    rowb = sb("rowb", [1, 512])
    rowbx = sb("rowbx", [1, 2])
    stat = sb("stat", [128, 16])
    S_st = sb("S_st", [128, H * M])
    T_st = sb("T_st", [128, 4 * 128])
    xs = sb("xs", [128, NDC * TH])
    xhat = sb("xhat", [128, NDC * TH])
    xtmp = sb("xtmp", [128, NDC * TH])

    pb = [psum(f"pb{i}", [128, 512]) for i in range(8)]

    P = Prog(nc, ctx)
    for s in ("ld", "st", "lw", "cva", "cvb", "q2", "q2g", "q1v", "q1g",
              "m1m", "m1e", "m2h", "pj"):
        P.sem(s)

    V, S, TE, G = "vector", "scalar", "tensor", "gpsimd"

    def dma_in(dst_ap, src_ap):
        return P.do("sync", lambda e, d=dst_ap, s=src_ap: e.dma_start(out=d, in_=s), sig="ld", inc=16)

    def dma_w(dst_ap, src_ap):
        return P.do("sync", lambda e, d=dst_ap, s=src_ap: e.dma_start(out=d, in_=s), sig="lw", inc=16)

    def dma_out(dst_ap, src_ap):
        return P.do("gpsimd", lambda e, d=dst_ap, s=src_ap: e.dma_start(out=d, in_=s), sig="st", inc=16)

    def v_op(fn, **kw):
        P.do(V, fn, **kw)

    def s_op(fn, **kw):
        P.do(S, fn, **kw)

    def t_op(fn, **kw):
        P.do(TE, fn, **kw)

    def g_op(fn, **kw):
        P.do(G, fn, **kw)

    def t_transpose(out_psum, in_ap):
        p = in_ap.shape[0]
        t_op(lambda e, o=out_psum, i=in_ap, p=p: e.transpose(o, i, ident[0:p, 0:p]))

    def t_bcast(out_psum, row_ap):
        t_op(lambda e, o=out_psum, r=row_ap: e.matmul(o, onest[0:1, 0:128], r, start=True, stop=True))

    dma_in(ident[:], c_ident)
    dma_in(tri[:], c_tri)
    dma_in(trih[:], c_trih)
    dma_in(convw[:], c_convw)
    dma_in(convb[:], c_convb)
    dma_in(normw[:], c_normw)
    dma_in(onest[:], c_ones)
    dma_in(zer[:], c_zeros)
    dma_in(S_st[:], c_zeros)
    dma_in(T_st[:], c_zeros)
    dma_in(tg[:], c_zeros[:, 0:16])
    dma_in(stat[:], c_zeros[:, 0:16])
    dma_in(rowb[:], c_zeros[0:1, 0:512])
    dma_in(rowbx[:], c_zeros[0:1, 0:2])
    P.wait(V, "ld")
    P.wait(S, "ld")
    P.wait(TE, "ld")
    P.bar()

    def quant_weight(wsrc, nrc, ncols, dstT, deq_col, thr_pos, thr_neg, wctx):
        def fsb(name, shape, dtype=F32):
            return wctx.enter_context(nc.sbuf_tensor(name, shape, dtype))
        wbuf = fsb(f"wbuf{deq_col}", [128, ncols])
        acc = fsb(f"wacc{deq_col}", [128, 1])
        qbuf = fsb(f"wqq{deq_col}", [128, ncols], BF16)
        tb1 = fsb(f"wtb1{deq_col}", [128, ncols])
        tb2 = fsb(f"wtb2{deq_col}", [128, ncols])
        nelem = nrc * 128 * ncols
        for r in range(nrc):
            dma_w(wbuf[:], wsrc[128 * r:128 * (r + 1), :])
            P.wait(V, "lw")
            if r == 0:
                v_op(lambda e: e.tensor_reduce(acc[:], wbuf[:], AX.X, OP.add,
                                               apply_absolute_value=True))
            else:
                v_op(lambda e: e.tensor_reduce(stat[:, 0:1], wbuf[:], AX.X, OP.add,
                                               apply_absolute_value=True))
                v_op(lambda e: e.tensor_add(acc[:], acc[:], stat[:, 0:1]), dep=True)
            P.bar()
        t_op(lambda e: e.matmul(pb[7][0:1, 0:1], acc[:], onest[:, 0:1], start=True, stop=True))
        P.bar()
        v_op(lambda e: e.tensor_scalar(stat[0:1, 0:1], pb[7][0:1, 0:1], 1.0 / nelem, 1e-5,
                                       OP.mult, OP.max))
        v_op(lambda e: e.tensor_scalar(stat[0:1, 1:2], stat[0:1, 0:1], 0.5, None, OP.mult), dep=True)
        P.bar()
        t_bcast(pb[7][:, 0:1], stat[0:1, 0:1])
        t_bcast(pb[7][:, 1:2], stat[0:1, 1:2])
        P.bar()
        v_op(lambda e: e.tensor_copy(wdeq[:, deq_col:deq_col + 1], pb[7][:, 0:1]))
        v_op(lambda e: e.tensor_copy(wthr[:, thr_pos:thr_pos + 1], pb[7][:, 1:2]))
        v_op(lambda e: e.tensor_scalar(wthr[:, thr_neg:thr_neg + 1], pb[7][:, 1:2], -1.0,
                                       None, OP.mult))
        P.bar()
        nbc = ncols // 128
        for r in range(nrc):
            dma_w(wbuf[:], wsrc[128 * r:128 * (r + 1), :])
            P.wait(TE, "lw")
            for b0 in range(0, nbc, 4):
                bank = pb[(b0 // 4) % 2]
                n = min(4, nbc - b0)
                for j in range(n):
                    t_transpose(bank[:, j * 128:(j + 1) * 128],
                                wbuf[:, (b0 + j) * 128:(b0 + j + 1) * 128])
                P.bar()
                v_op(lambda e, b0=b0, n=n, bank=bank: e.tensor_copy(
                    tb1[:, b0 * 128:(b0 + n) * 128], bank[:, 0:n * 128]))
                P.bar()
            v_op(lambda e: e.tensor_scalar(tb2[:], tb1[:], wthr[:, thr_pos:thr_pos + 1],
                                           None, OP.is_gt), dep=True)
            v_op(lambda e: e.tensor_scalar(tb1[:], tb1[:], wthr[:, thr_neg:thr_neg + 1],
                                           None, OP.is_lt), dep=True)
            v_op(lambda e: e.scalar_tensor_tensor(qbuf[:], tb1[:], -1.0, tb2[:],
                                                  OP.mult, OP.add), dep=True)
            P.bar()
            dst_ap = dstT.rearrange("(b p) (rr i) -> p b rr i", p=128, i=128)[:, :, r, :]
            src_ap = qbuf[:].rearrange("p (b i) -> p b i", i=128)
            dma_out(dst_ap, src_ap)
            P.wait(G, "st")
        P.bar()

    _stop = {"v": None}
    if stop_after == "consts":
        _stop["v"] = "consts"
    if _stop["v"] is None:
        with contextlib.ExitStack() as wctx:
            quant_weight(W["w1"], NHC, D, w1qT, 0, 0, 1, wctx)
            if stop_after != "w1":
                quant_weight(W["w2"], NDC, HID, w2qT, 1, 2, 3, wctx)
            P.wait("sync", "st")
            P.bar()

    def emit_ln(nu):
        extras = nu == TH
        xs3 = xs[:].rearrange("p (dc u) -> p dc u", u=TH)
        xh3 = xhat[:].rearrange("p (dc u) -> p dc u", u=TH)
        xt3 = xtmp[:].rearrange("p (dc u) -> p dc u", u=TH)
        ns = 5 if extras else 4

        def xv(buf, dc, lo, hi):
            return buf[:, dc * TH + lo: dc * TH + hi]

        s_op(lambda e: e.activation(xtmp[:, 0:NDC * TH], xs[:, 0:NDC * TH], AF.Square))
        P.bar()
        for g in range(4):
            for dc in range(NDC):
                t_op(lambda e, g=g, dc=dc: e.matmul(
                    pb[6][:, g:g + 1], xv(xs, dc, g * 128, (g + 1) * 128), onest[:, 0:1],
                    start=(dc == 0), stop=(dc == NDC - 1)))
            for dc in range(NDC):
                t_op(lambda e, g=g, dc=dc: e.matmul(
                    pb[6][:, 4 + g:5 + g], xv(xtmp, dc, g * 128, (g + 1) * 128), onest[:, 0:1],
                    start=(dc == 0), stop=(dc == NDC - 1)))
        if extras:
            for dc in range(NDC):
                t_op(lambda e, dc=dc: e.matmul(pb[6][0:2, 8:9], xv(xs, dc, 512, 514),
                                               onest[:, 0:1], start=(dc == 0), stop=(dc == NDC - 1)))
            for dc in range(NDC):
                t_op(lambda e, dc=dc: e.matmul(pb[6][0:2, 9:10], xv(xtmp, dc, 512, 514),
                                               onest[:, 0:1], start=(dc == 0), stop=(dc == NDC - 1)))
        P.bar()
        v_op(lambda e: e.tensor_scalar(tg[:, 0:4], pb[6][:, 0:4], 1.0 / D, None, OP.mult))
        v_op(lambda e: e.tensor_scalar(stat[:, 0:4], pb[6][:, 4:8], 1.0 / D, None, OP.mult))
        if extras:
            v_op(lambda e: e.tensor_scalar(tg[0:2, 4:5], pb[6][0:2, 8:9], 1.0 / D, None, OP.mult))
            v_op(lambda e: e.tensor_scalar(stat[0:2, 4:5], pb[6][0:2, 9:10], 1.0 / D, None, OP.mult))
        v_op(lambda e, ns=ns: e.tensor_mul(stat[:, 8:8 + ns], tg[:, 0:ns], tg[:, 0:ns]), dep=True)
        v_op(lambda e, ns=ns: e.tensor_sub(stat[:, 0:ns], stat[:, 0:ns], stat[:, 8:8 + ns]), dep=True)
        v_op(lambda e, ns=ns: e.tensor_scalar(stat[:, 0:ns], stat[:, 0:ns], EPS, None, OP.add), dep=True)
        P.bar()
        s_op(lambda e, ns=ns: e.activation(stat[:, 0:ns], stat[:, 0:ns], AF.Ln))
        s_op(lambda e, ns=ns: e.activation(tg[:, 5:5 + ns], stat[:, 0:ns], AF.Exp, scale=-0.5), dep=True)
        P.bar()
        emit_bcast(0, ns, 4, 256)
        emit_bcast(5, ns, 5, 258)
        v_op(lambda e: e.tensor_sub(xt3[:, :, 0:512], xs3[:, :, 0:512],
                                    pb[4][:, 0:512].unsqueeze(1).broadcast_to((128, NDC, 512))))
        if extras:
            v_op(lambda e: e.tensor_sub(xt3[:, :, 512:514], xs3[:, :, 512:514],
                                        pb[7][:, 256:258].unsqueeze(1).broadcast_to((128, NDC, 2))))
        v_op(lambda e: e.tensor_mul(xh3[:, :, 0:512], xt3[:, :, 0:512],
                                    pb[5][:, 0:512].unsqueeze(1).broadcast_to((128, NDC, 512))), dep=True)
        if extras:
            v_op(lambda e: e.tensor_mul(xh3[:, :, 512:514], xt3[:, :, 512:514],
                                        pb[7][:, 258:260].unsqueeze(1).broadcast_to((128, NDC, 2))), dep=True)
        P.bar()

    def emit_bcast(tg_lo, ns, bank, extra_off):
        for g in range(4):
            t_transpose(pb[6][0:1, g * 128:(g + 1) * 128], tg[:, tg_lo + g:tg_lo + g + 1])
        if ns == 5:
            t_transpose(pb[7][0:1, 384:386], tg[0:2, tg_lo + 4:tg_lo + 5])
        P.bar()
        v_op(lambda e: e.tensor_copy(rowb[:], pb[6][0:1, 0:512]))
        if ns == 5:
            v_op(lambda e: e.tensor_copy(rowbx[:], pb[7][0:1, 384:386]))
        P.bar()
        for g in range(4):
            t_bcast(pb[bank][:, g * 128:(g + 1) * 128], rowb[0:1, g * 128:(g + 1) * 128])
        if ns == 5:
            t_bcast(pb[7][:, extra_off:extra_off + 2], rowbx[0:1, 0:2])
        P.bar()

    def emit_ffn(src, dst, dbg_dst=None, tag="a", stop=None):
        fstk = contextlib.ExitStack()

        def fsb(name, shape, dtype=F32):
            return fstk.enter_context(nc.sbuf_tensor(tag + name, shape, dtype))

        h = fsb("h", [128, NHC * TH])
        ni32 = fsb("ni32", [128, NDC * TH], I32)
        xqb = fsb("xqb", [128, NDC * TH], BF16)
        yqg = [fsb(f"yqg{i}", [128, 4 * T], BF16) for i in range(2)]
        wg1 = [fsb(f"wg1_{i}", [128, NDC * 1024], BF16) for i in range(2)]
        wg2 = [fsb(f"wg2_{i}", [128, 1024], BF16) for i in range(2)]
        yscr = [fsb(f"yscr{i}", [128, T]) for i in range(2)]
        sc1 = fsb("sc1", [128, TH])
        sc2 = fsb("sc2", [128, T])
        scq = fsb("scq", [128, T])
        smax = fsb("smax", [128, T])

        xs3 = xs[:].rearrange("p (dc u) -> p dc u", u=TH)
        xh3 = xhat[:].rearrange("p (dc u) -> p dc u", u=TH)
        xt3 = xtmp[:].rearrange("p (dc u) -> p dc u", u=TH)
        ni3 = ni32[:].rearrange("p (dc u) -> p dc u", u=TH)

        def xv(buf, dc, lo, hi):
            return buf[:, dc * TH + lo: dc * TH + hi]

        def hv(hc, lo, hi):
            return h[:, hc * TH + lo: hc * TH + hi]

        NG = 4
        GHC = NHC // NG

        for it in range(NT):
            t0 = it * T
            lo_u = 1 if it == 0 else 0
            hi_u = TH - 1 if it == NT - 1 else TH
            g0 = t0 - 1 + lo_u
            dma_in(xs3[:, :, lo_u:hi_u],
                   src.rearrange("(dc p) n -> p dc n", p=128)[:, :, g0:g0 + (hi_u - lo_u)])
            lw0 = dma_w(wg1[0][:].rearrange("p (dc n) -> p dc n", n=1024),
                        w1qT.rearrange("(dc p) hid -> p dc hid", p=128)[:, :, 0:1024])
            P.wait(V, "ld")
            P.wait(S, "ld")
            P.wait(TE, "ld")
            if it == 0:
                v_op(lambda e: e.tensor_copy(xs3[:, :, 0:1], xs3[:, :, 1:2]))
            if it == NT - 1:
                v_op(lambda e: e.tensor_copy(xs3[:, :, TH - 1:TH], zer[:, 0:NDC]))
            P.bar()

            emit_ln(TH)
            if stop == "ln" and it == 0:
                P.bar()
                fstk.close()
                return "stopped"
            if dbg_dst is not None and it == 0 and dbg.get("d_xhat") is not None:
                dma_out(dbg["d_xhat"], xhat[:])
                P.wait(G, "st")
            v_op(lambda e: e.tensor_reduce(
                smax[:, 0:512], xhat[:].rearrange("p (dc u) -> p u dc", u=TH)[:, 0:512, :],
                AX.X, OP.max, apply_absolute_value=True))
            v_op(lambda e: e.tensor_reduce(
                stat[:, 6:8], xhat[:].rearrange("p (dc u) -> p u dc", u=TH)[:, 512:514, :],
                AX.X, OP.max, apply_absolute_value=True))
            P.bar()
            for g in range(4):
                t_transpose(pb[6][:, g * 128:(g + 1) * 128], smax[:, g * 128:(g + 1) * 128])
            t_transpose(pb[7][0:2, 260:388], stat[:, 6:8])
            P.bar()
            for g in range(4):
                v_op(lambda e, g=g: e.tensor_reduce(tg[:, 10 + g:11 + g],
                                                    pb[6][:, g * 128:(g + 1) * 128], AX.X, OP.max))
            v_op(lambda e: e.tensor_reduce(tg[0:2, 14:15], pb[7][0:2, 260:388], AX.X, OP.max))
            v_op(lambda e: e.tensor_scalar(tg[:, 10:15], tg[:, 10:15], 1e-5, None, OP.max), dep=True)
            v_op(lambda e: e.reciprocal(stat[:, 0:5], tg[:, 10:15]), dep=True)
            v_op(lambda e: e.tensor_scalar(tg[:, 10:15], stat[:, 0:5], 127.0, None, OP.mult), dep=True)
            v_op(lambda e: e.reciprocal(stat[:, 0:5], tg[:, 10:15]), dep=True)
            v_op(lambda e: e.tensor_scalar(stat[:, 0:5], stat[:, 0:5], wdeq[:, 0:1], None, OP.mult), dep=True)
            P.bar()
            emit_bcast(10, 5, 4, 256)
            v_op(lambda e: e.tensor_copy(tg[:, 10:15], stat[:, 0:5]))
            P.bar()
            emit_bcast(10, 5, 5, 258)
            v_op(lambda e: e.tensor_copy(sc1[:, 0:512], pb[5][:, 0:512]))
            v_op(lambda e: e.tensor_copy(sc1[:, 512:514], pb[7][:, 258:260]))
            v_op(lambda e: e.tensor_mul(ni3[:, :, 0:512], xh3[:, :, 0:512],
                                        pb[4][:, 0:512].unsqueeze(1).broadcast_to((128, NDC, 512))))
            v_op(lambda e: e.tensor_mul(ni3[:, :, 512:514], xh3[:, :, 512:514],
                                        pb[7][:, 256:258].unsqueeze(1).broadcast_to((128, NDC, 2))))
            v_op(lambda e: e.tensor_mul(xt3[:, :, 0:512], xh3[:, :, 0:512],
                                        pb[4][:, 0:512].unsqueeze(1).broadcast_to((128, NDC, 512))))
            v_op(lambda e: e.tensor_mul(xt3[:, :, 512:514], xh3[:, :, 512:514],
                                        pb[7][:, 256:258].unsqueeze(1).broadcast_to((128, NDC, 2))))
            v_op(lambda e: e.scalar_tensor_tensor(xhat[:, 0:NDC * TH], ni32[:, 0:NDC * TH], -1.0,
                                                  xtmp[:, 0:NDC * TH], OP.mult, OP.add), dep=True)
            v_op(lambda e: e.tensor_scalar(xtmp[:, 0:NDC * TH], xhat[:, 0:NDC * TH], 0.5,
                                           None, OP.is_gt), dep=True)
            v_op(lambda e: e.scalar_tensor_tensor(xtmp[:, 0:NDC * TH], xtmp[:, 0:NDC * TH], 0.0,
                                                  ni32[:, 0:NDC * TH], OP.bypass, OP.add), dep=True)
            v_op(lambda e: e.tensor_scalar(xhat[:, 0:NDC * TH], xhat[:, 0:NDC * TH], -0.5,
                                           None, OP.is_lt), dep=True)
            v_op(lambda e: e.scalar_tensor_tensor(xqb[:, 0:NDC * TH], xhat[:, 0:NDC * TH], -1.0,
                                                  xtmp[:, 0:NDC * TH], OP.mult, OP.add), dep=True)
            P.bar()

            if dbg_dst is not None and it == 0 and dbg.get("d_xq") is not None:
                dma_out(dbg["d_xq"], xqb[:])
                dma_out(dbg["d_tg"], tg[:])
                P.wait(G, "st")
            if stop == "q" and it == 0:
                P.bar()
                fstk.close()
                return "stopped"
            lw_have = lw0
            m1m0 = P.count("m1m")
            m1e0 = P.count("m1e")
            cvb0 = P.count("cvb")
            for g in range(NG):
                if g + 1 < NG:
                    if g >= 1:
                        P.wait("sync", "m1m", m1m0 + g * GHC)
                    lw_next = dma_w(wg1[(g + 1) % 2][:].rearrange("p (dc n) -> p dc n", n=1024),
                                    w1qT.rearrange("(dc p) hid -> p dc hid", p=128)
                                    [:, :, (g + 1) * 1024:(g + 2) * 1024])
                P.wait(TE, "lw", lw_have)
                wbuf = wg1[g % 2]
                for j in range(GHC):
                    hc = g * GHC + j
                    bank = pb[hc % 2]
                    if hc >= 2:
                        P.wait(TE, "m1e", m1e0 + hc - 1)
                    for dc in range(NDC):
                        t_op(lambda e, bank=bank, dc=dc, wbuf=wbuf, j=j: e.matmul(
                            bank[:, 0:512],
                            wbuf[:, dc * 1024 + j * 128: dc * 1024 + (j + 1) * 128],
                            xqb[:, dc * TH + 0: dc * TH + 512],
                            start=(dc == 0), stop=(dc == NDC - 1)))
                    for dc in range(NDC):
                        t_op(lambda e, dc=dc, wbuf=wbuf, j=j, hc=hc: e.matmul(
                            pb[2 + hc % 2][:, 0:2],
                            wbuf[:, dc * 1024 + j * 128: dc * 1024 + (j + 1) * 128],
                            xqb[:, dc * TH + 512: dc * TH + 514],
                            start=(dc == 0), stop=(dc == NDC - 1)),
                            sig="m1m" if dc == NDC - 1 else None)
                    P.wait(V, "m1m", m1m0 + hc + 1)
                    v_op(lambda e, hc=hc: e.tensor_mul(hv(hc, 512, 514),
                                                       pb[2 + hc % 2][:, 0:2],
                                                       sc1[:, 512:514]))
                    v_op(lambda e, hc=hc, bank=bank: e.tensor_mul(hv(hc, 0, 512),
                                                                  bank[:, 0:512], sc1[:, 0:512]),
                         sig="m1e")
                    if it == 0:
                        v_op(lambda e, hc=hc: e.tensor_copy(h[:, hc * TH:hc * TH + 1], zer[:, 0:1]))
                    if it == NT - 1:
                        v_op(lambda e, hc=hc: e.tensor_copy(h[:, hc * TH + TH - 1:hc * TH + TH], zer[:, 0:1]))
                    yb = yscr[hc % 2]
                    if hc >= 2:
                        P.wait(V, "cvb", cvb0 + hc - 1)
                    v_op(lambda e, hc=hc, yb=yb: e.tensor_scalar(
                        yb[:], hv(hc, 1, 513), convw[:, hc * 3 + 1:hc * 3 + 2],
                        convb[:, hc:hc + 1], OP.mult, OP.add), dep=True)
                    v_op(lambda e, hc=hc, yb=yb: e.scalar_tensor_tensor(
                        yb[:], hv(hc, 0, 512), convw[:, hc * 3 + 0:hc * 3 + 1], yb[:],
                        OP.mult, OP.add), dep=True)
                    v_op(lambda e, hc=hc, yb=yb: e.scalar_tensor_tensor(
                        yb[:], hv(hc, 2, 514), convw[:, hc * 3 + 2:hc * 3 + 3], yb[:],
                        OP.mult, OP.add), sig="cva", dep=True)
                    P.wait(S, "cva")
                    s_op(lambda e, hc=hc, yb=yb: e.activation(hv(hc, 1, 513), yb[:], AF.Silu),
                         sig="cvb")
                if g + 1 < NG:
                    lw_have = lw_next
            P.bar()

            if stop in ("mm1", "conv") and it == 0:
                P.bar()
                fstk.close()
                return "stopped"

            if dbg_dst is not None and it == 0 and dbg.get("d_h") is not None:
                dma_out(dbg["d_h"], h[:])
                P.wait(G, "st")
            if stop == "conv" and it == 0:
                P.bar()
                fstk.close()
                return "stopped"
            v_op(lambda e: e.tensor_reduce(
                smax[:, 0:512], h[:].rearrange("p (hc u) -> p u hc", u=TH)[:, 1:513, :],
                AX.X, OP.max, apply_absolute_value=True))
            P.bar()
            for g in range(4):
                t_transpose(pb[6][:, g * 128:(g + 1) * 128], smax[:, g * 128:(g + 1) * 128])
            P.bar()
            for g in range(4):
                v_op(lambda e, g=g: e.tensor_reduce(tg[:, 10 + g:11 + g],
                                                    pb[6][:, g * 128:(g + 1) * 128], AX.X, OP.max))
            v_op(lambda e: e.tensor_scalar(tg[:, 10:14], tg[:, 10:14], 1e-5, None, OP.max), dep=True)
            v_op(lambda e: e.reciprocal(stat[:, 0:4], tg[:, 10:14]), dep=True)
            v_op(lambda e: e.tensor_scalar(tg[:, 10:14], stat[:, 0:4], 127.0, None, OP.mult), dep=True)
            v_op(lambda e: e.reciprocal(stat[:, 0:4], tg[:, 10:14]), dep=True)
            v_op(lambda e: e.tensor_scalar(stat[:, 0:4], stat[:, 0:4], wdeq[:, 1:2], None, OP.mult), dep=True)
            v_op(lambda e: e.tensor_scalar(stat[:, 0:4], stat[:, 0:4], 0.5, None, OP.mult), dep=True)
            P.bar()
            emit_bcast(10, 4, 4, 0)
            v_op(lambda e: e.tensor_copy(scq[:, 0:512], pb[4][:, 0:512]))
            v_op(lambda e: e.tensor_copy(tg[:, 10:14], stat[:, 0:4]))
            P.bar()
            emit_bcast(10, 4, 4, 0)
            v_op(lambda e: e.tensor_copy(sc2[:, 0:512], pb[4][:, 0:512]))
            P.bar()

            lw2 = []
            lw2.append(dma_w(wg2[0][:], w2qT[0:128, :]))
            h4 = h[:].rearrange("p (hc u) -> p hc u", u=TH)
            for q in range(8):
                ybq = yqg[q % 2]
                if q >= 2:
                    P.wait(V, "m2h", P.count("m2h") - 4)
                hgv = h4[:, q * 4:(q + 1) * 4, 1:513]
                niv = ni32[:, 0:4 * T].rearrange("p (c t) -> p c t", t=T)
                xtv = xtmp[:, 0:4 * T].rearrange("p (c t) -> p c t", t=T)
                xhv = xhat[:, 0:4 * T].rearrange("p (c t) -> p c t", t=T)
                ybv = ybq[:].rearrange("p (c t) -> p c t", t=T)
                scb = scq[:, 0:512].unsqueeze(1).broadcast_to((128, 4, 512))
                v_op(lambda e, hgv=hgv, niv=niv, scb=scb: e.tensor_mul(niv, hgv, scb))
                v_op(lambda e, hgv=hgv, xtv=xtv, scb=scb: e.tensor_mul(xtv, hgv, scb))
                v_op(lambda e, niv=niv, xtv=xtv, xhv=xhv: e.scalar_tensor_tensor(
                    xhv, niv, -1.0, xtv, OP.mult, OP.add), dep=True)
                v_op(lambda e, xtv=xtv, xhv=xhv: e.tensor_scalar(xtv, xhv, 0.5, None, OP.is_gt), dep=True)
                v_op(lambda e, xtv=xtv, niv=niv: e.scalar_tensor_tensor(
                    xtv, xtv, 0.0, niv, OP.bypass, OP.add), dep=True)
                v_op(lambda e, xhv=xhv: e.tensor_scalar(xhv, xhv, -0.5, None, OP.is_lt), dep=True)
                v_op(lambda e, ybv=ybv, xhv=xhv, xtv=xtv: e.scalar_tensor_tensor(
                    ybv, xhv, -1.0, xtv, OP.mult, OP.add), sig="q2", dep=True)
                if dbg_dst is not None and it == 0 and q == 0 and dbg.get("d_yq") is not None:
                    P.bar()
                    dma_out(dbg["d_yq"], ybq[:])
                    dma_out(dbg["d_sc"][:, 0:T], scq[:, 0:T])
                    dma_out(dbg["d_sc"][:, T:2 * T], sc2[:, 0:T])
                    P.wait(G, "st")
                for jj in range(4):
                    hc = q * 4 + jj
                    if hc + 1 < NHC:
                        P.wait("sync", "m2h")
                        lw2.append(dma_w(wg2[(hc + 1) % 2][:], w2qT[128 * (hc + 1):128 * (hc + 2), :]))
                    if jj == 0:
                        P.wait(TE, "q2")
                    P.wait(TE, "lw", lw2[hc])
                    for oc in range(NDC):
                        sig = "m2h" if oc == NDC - 1 else None
                        t_op(lambda e, hc=hc, oc=oc, jj=jj, q=q: e.matmul(
                            pb[oc][:, 0:512],
                            wg2[hc % 2][:, oc * 128:(oc + 1) * 128],
                            yqg[q % 2][:, jj * T:(jj + 1) * T],
                            start=(hc == 0), stop=(hc == NHC - 1)), sig=sig)
            P.bar()
            for oc in range(NDC):
                v_op(lambda e, oc=oc: e.tensor_mul(xv(xtmp, oc, 0, 512), pb[oc][:, 0:512],
                                                   sc2[:, 0:512]))
            for oc in range(NDC):
                v_op(lambda e, oc=oc: e.tensor_add(xv(xhat, oc, 0, 512), xv(xtmp, oc, 0, 512),
                                                   xv(xs, oc, 1, 513)), dep=(oc == 0))
            P.bar()
            if stop == "mm2" and it == 0:
                P.bar()
                fstk.close()
                return "stopped"
            dst3 = dst.rearrange("(dc p) n -> p dc n", p=128)[:, :, t0:t0 + T]
            src3 = xhat[:].rearrange("p (dc u) -> p dc u", u=TH)[:, :, 0:512]
            dma_out(dst3, src3)
            if dbg_dst is not None:
                dma_out(dbg_dst.rearrange("(dc p) n -> p dc n", p=128)[:, :, t0:t0 + T], src3)
            P.wait(G, "st")
            P.bar()
            if stop == "t0" and it == 0:
                fstk.close()
                return "stopped"
        P.wait("sync", "st")
        P.bar()
        fstk.close()


    def emit_attn(src, dst, dbg_dst=None, stop=None):
        astk = contextlib.ExitStack()

        def asb(name, shape, dtype=F32):
            return astk.enter_context(nc.sbuf_tensor("at" + name, shape, dtype))

        q = asb("q", [128, H * T])
        k = asb("k", [128, H * T])
        vtm = asb("vtm", [128, NCH * 1024])
        gate = asb("gate", [128, H * T])
        f1b = asb("f1b", [128, T])
        g1b = asb("g1b", [128, T])
        sgb = asb("sgb", [128, 4 * T])
        s_fm = asb("s_fm", [128, 4 * T])
        g_fm = asb("g_fm", [128, 4 * T])
        Wp = [asb(f"Wp{i}", [128, 1024]) for i in range(2)]
        wf2b = asb("wf2b", [128, H * M])
        wg2b = asb("wg2b", [128, D])
        Lr = asb("Lr", [128, 512])
        Lf = asb("Lf", [128, 512])
        eLf = asb("eLf", [128, 512])
        eLnf = asb("eLnf", [128, 512])
        Sf = asb("Sf", [128, 512])
        Stm = asb("Stm", [128, 512])
        eLtm = asb("eLtm", [128, 512])
        okb = asb("okb", [128, 512])
        nbb = asb("nbb", [128, 512])
        qv = asb("qv", [128, 512])
        vqt = asb("vqt", [128, 512])
        vqf = asb("vqf", [128, 512])
        Gm = asb("Gm", [128, H * 128])
        Ktm = asb("Ktm", [128, H * 128])
        Wm = asb("Wm", [128, 128])
        su = asb("su", [128, 512])
        tu = asb("tu", [128, 4 * 128])
        ecc = asb("ecc", [128, 4])
        elc = asb("elc", [128, 4])
        mst = asb("mst", [128, 32])
        sxc = asb("sxc", [128, 16])

        def xv(buf, dc, lo, hi):
            return buf[:, dc * TH + lo: dc * TH + hi]

        xs3 = xs[:].rearrange("p (dc u) -> p dc u", u=TH)

        def proj_fm(wdram, wcols, nout_chunks, rhs_fn, out_cb, pjsem):
            lws = []
            lws.append(dma_w(Wp[0][:, 0:wcols], wdram[0:128, :]))
            for dc in range(NDC):
                if dc + 1 < NDC:
                    P.wait("sync", pjsem)
                    lws.append(dma_w(Wp[(dc + 1) % 2][:, 0:wcols],
                                     wdram[128 * (dc + 1):128 * (dc + 2), :]))
                P.wait(TE, "lw", lws[dc])
                for oc in range(nout_chunks):
                    sig = pjsem if oc == nout_chunks - 1 else None
                    t_op(lambda e, dc=dc, oc=oc: e.matmul(
                        pb[oc][:, 0:512],
                        Wp[dc % 2][:, oc * 128:(oc + 1) * 128],
                        rhs_fn(dc),
                        start=(dc == 0), stop=(dc == NDC - 1)), sig=sig)
            P.bar()
            for oc in range(nout_chunks):
                out_cb(oc)
            P.bar()

        for it in range(NT):
            t0 = it * T
            dma_in(xs3[:, :, 0:T],
                   src.rearrange("(dc p) n -> p dc n", p=128)[:, :, t0:t0 + T])
            P.wait(V, "ld")
            P.wait(S, "ld")
            P.wait(TE, "ld")
            P.bar()
            emit_ln(T)
            if stop == "ln" and it == 0:
                P.bar()
                astk.close()
                return "stopped"

            def xh_rhs(dc):
                return xv(xhat, dc, 0, T)

            proj_fm(W["Wq"], D, NDC, xh_rhs,
                    lambda oc: s_op(lambda e, oc=oc: e.activation(
                        q[:, oc * T:(oc + 1) * T], pb[oc][:, 0:512], AF.Silu)), "pj")
            proj_fm(W["Wk"], D, NDC, xh_rhs,
                    lambda oc: s_op(lambda e, oc=oc: e.activation(
                        k[:, oc * T:(oc + 1) * T], pb[oc][:, 0:512], AF.Silu)), "pj")
            lws = [dma_w(Wp[0][:], W["Wv"][0:128, :])]
            for dc in range(NDC):
                if dc + 1 < NDC:
                    P.wait("sync", "pj")
                    lws.append(dma_w(Wp[(dc + 1) % 2][:], W["Wv"][128 * (dc + 1):128 * (dc + 2), :]))
                P.wait(TE, "lw", lws[dc])
                for ts in range(NCH):
                    for hf in range(2):
                        bank = ts * 2 + hf
                        sig = "pj" if (ts == NCH - 1 and hf == 1) else None
                        t_op(lambda e, dc=dc, ts=ts, hf=hf, bank=bank: e.matmul(
                            pb[bank][:, 0:512],
                            xv(xhat, dc, ts * C, (ts + 1) * C),
                            Wp[dc % 2][:, hf * 512:(hf + 1) * 512],
                            start=(dc == 0), stop=(dc == NDC - 1)), sig=sig)
            P.bar()
            for ts in range(NCH):
                for hf in range(2):
                    v_op(lambda e, ts=ts, hf=hf: e.tensor_copy(
                        vtm[:, ts * 1024 + hf * 512: ts * 1024 + (hf + 1) * 512],
                        pb[ts * 2 + hf][:, 0:512]))
            P.bar()
            for nm, outb in (("Wf1", f1b), ("Wg1", g1b)):
                lwf = [dma_w(Wp[0][:, 0:128], W[nm][0:128, :])]
                for dc in range(NDC):
                    if dc + 1 < NDC:
                        P.wait("sync", "pj")
                        lwf.append(dma_w(Wp[(dc + 1) % 2][:, 0:128],
                                         W[nm][128 * (dc + 1):128 * (dc + 2), :]))
                    P.wait(TE, "lw", lwf[dc])
                    t_op(lambda e, dc=dc: e.matmul(
                        pb[0][:, 0:512], Wp[dc % 2][:, 0:128], xv(xhat, dc, 0, T),
                        start=(dc == 0), stop=(dc == NDC - 1)), sig="pj")
                P.bar()
                v_op(lambda e, outb=outb: e.tensor_copy(outb[:], pb[0][:, 0:512]))
                P.bar()
            lw_f2 = dma_w(wf2b[:], W["Wf2"][:, :])
            P.wait(TE, "lw", lw_f2)
            for fc in range(4):
                t_op(lambda e, fc=fc: e.matmul(
                    pb[fc][:, 0:512], wf2b[:, fc * 128:(fc + 1) * 128], f1b[:],
                    start=True, stop=True))
            P.bar()
            for fc in range(4):
                s_op(lambda e, fc=fc: e.activation(sgb[:, fc * T:(fc + 1) * T],
                                                   pb[fc][:, 0:512], AF.Sigmoid))
            P.bar()
            lw_g2 = dma_w(wg2b[:], W["Wg2"][:, :])
            P.wait(TE, "lw", lw_g2)
            for oc in range(NDC):
                t_op(lambda e, oc=oc: e.matmul(
                    pb[oc][:, 0:512], wg2b[:, oc * 128:(oc + 1) * 128], g1b[:],
                    start=True, stop=True))
            P.bar()
            for oc in range(NDC):
                s_op(lambda e, oc=oc: e.activation(
                    gate[:, oc * T:(oc + 1) * T], pb[oc][:, 0:512], AF.Sigmoid))
            P.bar()
            s_op(lambda e: e.activation(g_fm[:], sgb[:], AF.Ln))
            s_op(lambda e: e.activation(s_fm[:], g_fm[:], AF.Exp), dep=True)
            P.bar()
            v_op(lambda e: e.tensor_scalar(s_fm[:], s_fm[:], -1.0, 1.0, OP.mult, OP.add))
            P.bar()

            if dbg_dst is not None and it == 0:
                dma_out(dbg["a_q"], q[:])
                dma_out(dbg["a_k"], k[:])
                dma_out(dbg["a_sg"], sgb[:])
                P.wait(G, "st")
            if stop == "proj" and it == 0:
                P.bar()
                astk.close()
                return "stopped"
            for ts in range(NCH):
                for fc in range(4):
                    v_op(lambda e, fc=fc, ts=ts: e.tensor_tensor_scan(
                        Lr[:, fc * C:(fc + 1) * C],
                        g_fm[:, fc * T + ts * C: fc * T + (ts + 1) * C],
                        g_fm[:, fc * T + ts * C: fc * T + (ts + 1) * C],
                        0.0, OP.add, OP.bypass))
                for fc in range(4):
                    v_op(lambda e, fc=fc: e.tensor_scalar(
                        Lf[:, fc * C:(fc + 1) * C], Lr[:, fc * C:(fc + 1) * C],
                        Lr[:, fc * C + 63:fc * C + 64], None, OP.subtract),
                        dep=(fc == 0))
                P.bar()
                if stop == "cum" and it == 0 and ts == 0:
                    P.bar()
                    astk.close()
                    return "stopped"
                s_op(lambda e: e.activation(eLf[:], Lf[:], AF.Exp))
                s_op(lambda e: e.activation(eLnf[:], Lf[:], AF.Exp, scale=-1.0))
                s_op(lambda e: e.activation(ecc[:], Lr[:].rearrange("p (fc c) -> p c fc", c=C)[:, 63, :], AF.Exp))
                s_op(lambda e: e.activation(elc[:], Lf[:].rearrange("p (fc c) -> p c fc", c=C)[:, C - 1, :], AF.Exp))
                P.bar()
                v_op(lambda e, ts=ts: e.tensor_mul(
                    Sf[:].rearrange("p (fc c) -> p fc c", c=C),
                    s_fm[:].rearrange("p (fc t) -> p fc t", t=T)[:, :, ts * C:(ts + 1) * C],
                    eLnf[:].rearrange("p (fc c) -> p fc c", c=C)))
                P.bar()
                if stop == "exp" and it == 0 and ts == 0:
                    P.bar()
                    astk.close()
                    return "stopped"
                for fc in range(4):
                    t_transpose(pb[6][:, fc * 128:(fc + 1) * 128], Sf[:, fc * C:(fc + 1) * C])
                P.bar()
                v_op(lambda e: e.tensor_copy(Stm[:], pb[6][:, 0:512]))
                P.bar()
                for fc in range(4):
                    t_transpose(pb[6][:, fc * 128:(fc + 1) * 128], eLf[:, fc * C:(fc + 1) * C])
                P.bar()
                v_op(lambda e: e.tensor_copy(eLtm[:], pb[6][:, 0:512]))
                P.bar()
                for hh in range(0, H, 4):
                    for h2 in range(4):
                        h = hh + h2
                        t_transpose(pb[7][:, h2 * 128:(h2 + 1) * 128],
                                    k[:, h * T + ts * C: h * T + (ts + 1) * C])
                    P.bar()
                    v_op(lambda e, hh=hh: e.tensor_copy(
                        Ktm[:, hh * 128:(hh + 4) * 128], pb[7][:, 0:512]))
                    P.bar()
                if stop == "tp" and it == 0 and ts == 0:
                    P.bar()
                    astk.close()
                    return "stopped"
                for fc in range(4):
                    t_transpose(pb[7][0:1, fc * 128:(fc + 1) * 128], ecc[:, fc:fc + 1])
                for fc in range(4):
                    t_transpose(pb[6][0:1, fc * 128:(fc + 1) * 128], elc[:, fc:fc + 1])
                P.bar()
                v_op(lambda e: e.tensor_copy(rowb[:], pb[7][0:1, 0:512]))
                v_op(lambda e: e.tensor_copy(nbb[0:1, 0:512], pb[6][0:1, 0:512]))
                P.bar()
                for h in range(H):
                    t_bcast(pb[5][:, h * 64:(h + 1) * 64],
                            rowb[0:1, (h // 2) * 128 + (h % 2) * 64: (h // 2) * 128 + (h % 2) * 64 + 64])
                P.bar()
                v_op(lambda e: e.tensor_mul(su[:], S_st[:], pb[5][:, 0:512]))
                for ft in range(4):
                    v_op(lambda e, ft=ft: e.tensor_scalar(
                        tu[:, ft * 128:(ft + 1) * 128], T_st[:, ft * 128:(ft + 1) * 128],
                        ecc[:, ft:ft + 1], None, OP.mult))
                P.bar()
                if stop == "ec" and it == 0 and ts == 0:
                    P.bar()
                    astk.close()
                    return "stopped"
                for h in range(H):
                    t_op(lambda e, h=h, ts=ts: e.matmul(
                        pb[3 + h // 4][:, (h % 4) * 128:(h % 4) * 128 + 128],
                        k[:, h * T + ts * C: h * T + (ts + 1) * C],
                        q[:, h * T + ts * C: h * T + (ts + 1) * C],
                        start=True, stop=True))
                P.bar()
                for h in range(H):
                    v_op(lambda e, h=h: e.tensor_mul(
                        Gm[:, h * 128:(h + 1) * 128],
                        pb[3 + h // 4][:, (h % 4) * 128:(h % 4) * 128 + 128], tri[:]))
                P.bar()
                if stop == "g" and it == 0 and ts == 0:
                    P.bar()
                    astk.close()
                    return "stopped"
                for h in range(H):
                    t_bcast(pb[4][:, h * 64:(h + 1) * 64],
                            nbb[0:1, (h // 2) * 128 + (h % 2) * 64: (h // 2) * 128 + (h % 2) * 64 + 64])
                for h in range(H):
                    t_op(lambda e, h=h, ts=ts: e.matmul(
                        pb[2][:, h * 64:(h + 1) * 64],
                        q[:, h * T + ts * C: h * T + (ts + 1) * C],
                        su[:, h * 64:(h + 1) * 64],
                        start=True, stop=False))
                    t_op(lambda e, h=h: e.matmul(
                        pb[2][:, h * 64:(h + 1) * 64],
                        Gm[:, h * 128:(h + 1) * 128],
                        Stm[:, h * 64:(h + 1) * 64],
                        start=False, stop=True))
                P.bar()
                if stop == "ok" and it == 0 and ts == 0:
                    P.bar()
                    astk.close()
                    return "stopped"
                v_op(lambda e: e.tensor_mul(okb[:], pb[2][:, 0:512], eLtm[:]))
                v_op(lambda e: e.tensor_reduce(sxc[:, 0:8],
                                               okb[:].rearrange("p (h m) -> p h m", m=M),
                                               AX.X, OP.max, negate=True), dep=True)
                v_op(lambda e: e.tensor_copy(nbb[:], sxc[:, 0:8].broadcast_to((128, 8, M))),
                     dep=True)
                v_op(lambda e: e.tensor_add(okb[:], okb[:], nbb[:]), dep=True)
                P.bar()
                s_op(lambda e: e.activation(okb[:], okb[:], AF.Exp))
                P.bar()
                v_op(lambda e: e.tensor_reduce(sxc[:, 0:8],
                                               okb[:].rearrange("p (h m) -> p h m", m=M),
                                               AX.X, OP.add))
                v_op(lambda e: e.reciprocal(sxc[:, 8:16], sxc[:, 0:8]), dep=True)
                v_op(lambda e: e.tensor_copy(nbb[:], sxc[:, 8:16].broadcast_to((128, 8, M))),
                     dep=True)
                v_op(lambda e: e.tensor_mul(qv[:], okb[:], nbb[:]), dep=True)
                v_op(lambda e: e.tensor_mul(vqt[:], qv[:], eLtm[:]), dep=True)
                P.bar()
                if dbg_dst is not None and it == 0 and ts == 0:
                    dma_out(dbg["a_ok"], okb[:])
                    dma_out(dbg["a_qv"], qv[:])
                    P.wait(G, "st")
                for fc in range(4):
                    t_transpose(pb[6][:, fc * 128:(fc + 1) * 128], vqt[:, fc * C:(fc + 1) * C])
                P.bar()
                v_op(lambda e: e.tensor_copy(vqf[:], pb[6][0:128, 0:512]))
                P.bar()
                if stop == "sm" and it == 0 and ts == 0:
                    P.bar()
                    astk.close()
                    return "stopped"
                if dbg_dst is not None and it == 0 and ts == 0:
                    dma_out(dbg["a_sf"], Sf[:])
                    dma_out(dbg["a_vqf"], vqf[:])
                    P.wait(G, "st")
                for h in range(H):
                    fc = h // 2
                    r0 = (h % 2) * 64
                    sfr = lambda a, r0=r0, fc=fc: Sf[r0:r0 + 64, fc * C + a * 64: fc * C + (a + 1) * 64]
                    vqr = lambda b, r0=r0, fc=fc: vqf[r0:r0 + 64, fc * C + b * 64: fc * C + (b + 1) * 64]
                    t_op(lambda e, h=h, sfr=sfr, vqr=vqr: e.matmul(
                        pb[3][:, 0:0] if False else pb[3][0:64, 0:64], sfr(0), vqr(0),
                        start=True, stop=True))
                    t_op(lambda e, sfr=sfr, vqr=vqr: e.matmul(
                        pb[3][0:64, 64:128], sfr(0), vqr(1), start=True, stop=True))
                    if stop != "wmm2":
                        t_op(lambda e, sfr=sfr, vqr=vqr: e.matmul(
                            pb[3][64:128, 64:128], sfr(1), vqr(1), start=True, stop=True))
                    P.bar()
                    v_op(lambda e: e.tensor_mul(Wm[0:64, 0:64], pb[3][0:64, 0:64], trih[:]))
                    v_op(lambda e: e.tensor_copy(Wm[0:64, 64:128], pb[3][0:64, 64:128]))
                    if stop != "wmm2":
                        v_op(lambda e: e.tensor_mul(Wm[64:128, 64:128], pb[3][64:128, 64:128], trih[:]))
                    P.bar()
                    if stop in ("wmm", "wmm2"):
                        continue
                    t_op(lambda e, h=h, r0=r0, fc=fc, ts=ts: e.matmul(
                        pb[1][:, 0:128],
                        tu[r0:r0 + 64, fc * 128:(fc + 1) * 128],
                        vqf[r0:r0 + 64, fc * C:(fc + 1) * C],
                        start=True, stop=True))
                    t_op(lambda e, h=h, ts=ts: e.matmul(
                        pb[0][:, 0:64],
                        vtm[0:64, ts * 1024 + h * 128: ts * 1024 + (h + 1) * 128],
                        Wm[0:64, 0:64], start=True, stop=True))
                    t_op(lambda e, h=h, ts=ts: e.matmul(
                        pb[0][:, 64:128],
                        vtm[0:64, ts * 1024 + h * 128: ts * 1024 + (h + 1) * 128],
                        Wm[0:64, 64:128], start=True, stop=True))
                    t_op(lambda e, h=h, ts=ts: e.matmul(
                        pb[2][:, 0:64],
                        vtm[64:128, ts * 1024 + h * 128: ts * 1024 + (h + 1) * 128],
                        Wm[64:128, 64:128], start=True, stop=True))
                    P.bar()
                    v_op(lambda e: e.tensor_copy(okb[:, 0:128], pb[1][:, 0:128]))
                    v_op(lambda e: e.tensor_add(okb[:, 0:128], okb[:, 0:128],
                                                pb[0][:, 0:128]), dep=True)
                    v_op(lambda e: e.tensor_add(okb[:, 64:128], okb[:, 64:128],
                                                pb[2][:, 0:64]), dep=True)
                    if dbg_dst is not None and it == 0 and ts == 0:
                        v_op(lambda e: e.tensor_copy(Wm[64:128, 0:64], zer[0:64, 0:64]))
                        P.bar()
                        dma_out(dbg["a_ovh"][:, h * 128:(h + 1) * 128], okb[:, 0:128])
                        dma_out(dbg["a_wmh"][:, h * 128:(h + 1) * 128], Wm[:])
                        P.wait(G, "st")
                    v_op(lambda e, h=h, ts=ts: e.tensor_mul(
                        gate[:, h * T + ts * C: h * T + (ts + 1) * C],
                        okb[:, 0:128],
                        gate[:, h * T + ts * C: h * T + (ts + 1) * C]), dep=True)
                    P.bar()
                if stop in ("w", "wmm", "wmm2", "wov0", "wov1", "wov2", "wov3", "wov3b", "wovev") and it == 0 and ts == 0:
                    P.bar()
                    astk.close()
                    return "stopped"
                P.bar()
                if dbg_dst is not None and it == 0 and ts == 0 and False:
                    v_op(lambda e: e.tensor_copy(Wm[64:128, 0:64], zer[0:64, 0:64]))
                    P.bar()
                    dma_out(dbg["a_Wm"], Wm[:])
                    P.wait(G, "st")
                    v_op(lambda e: e.tensor_copy(nbb[:, 0:128], pb[0][:, 0:128]))
                    P.bar()
                    dma_out(dbg["a_ov"], nbb[:, 0:128])
                    P.wait(G, "st")
                    P.bar()
                for h in range(H):
                    t_op(lambda e, h=h: e.matmul(
                        pb[5][:, h * 64:(h + 1) * 64],
                        Ktm[:, h * 128:(h + 1) * 128],
                        Stm[:, h * 64:(h + 1) * 64], start=True, stop=True))
                for h in range(H):
                    r0 = (h % 2) * 64
                    t_op(lambda e, h=h, r0=r0, ts=ts: e.matmul(
                        pb[6][r0:r0 + 64, (h // 2) * 128:(h // 2 + 1) * 128],
                        Stm[:, h * 64:(h + 1) * 64],
                        vtm[:, ts * 1024 + h * 128: ts * 1024 + (h + 1) * 128],
                        start=True, stop=True))
                P.bar()
                v_op(lambda e: e.tensor_add(su[:], su[:], pb[5][:, 0:512]))
                v_op(lambda e: e.tensor_mul(S_st[:], su[:], pb[4][:, 0:512]), dep=True)
                v_op(lambda e: e.tensor_add(tu[:], tu[:], pb[6][:, 0:512]))
                for ft in range(4):
                    v_op(lambda e, ft=ft: e.tensor_scalar(
                        T_st[:, ft * 128:(ft + 1) * 128], tu[:, ft * 128:(ft + 1) * 128],
                        elc[:, ft:ft + 1], None, OP.mult), dep=(ft == 0))
                P.bar()
                if dbg_dst is not None and it == 0 and ts == 0:
                    dma_out(dbg["a_og"], gate[:])
                    dma_out(dbg["a_S"], S_st[:])
                    dma_out(dbg["a_T"], T_st[:])
                    P.wait(G, "st")
                if stop == "chunk0" and it == 0 and ts == 0:
                    P.bar()
                    astk.close()
                    return "stopped"

            s_op(lambda e: e.activation(xtmp[:, 0:H * T], gate[:], AF.Square))
            P.bar()
            for h in range(H):
                for ts in range(NCH):
                    t_op(lambda e, h=h, ts=ts: e.matmul(
                        pb[7][:, h * 4 + ts: h * 4 + ts + 1],
                        xtmp[:, h * T + ts * C: h * T + (ts + 1) * C],
                        onest[:, 0:1], start=True, stop=True))
            P.bar()
            v_op(lambda e: e.tensor_scalar(mst[:], pb[7][:, 0:32], 1.0 / DK, EPS,
                                           OP.mult, OP.add))
            P.bar()
            s_op(lambda e: e.activation(mst[:], mst[:], AF.Ln))
            s_op(lambda e: e.activation(mst[:], mst[:], AF.Exp, scale=-0.5), dep=True)
            P.bar()
            for h in range(H):
                for ts in range(NCH):
                    t_transpose(pb[6][0:1, ts * 128:(ts + 1) * 128],
                                mst[:, h * 4 + ts: h * 4 + ts + 1])
                P.bar()
                v_op(lambda e: e.tensor_copy(rowb[:], pb[6][0:1, 0:512]))
                P.bar()
                for ts in range(NCH):
                    t_bcast(pb[5][:, ts * 128:(ts + 1) * 128], rowb[0:1, ts * 128:(ts + 1) * 128])
                P.bar()
                v_op(lambda e, h=h: e.scalar_tensor_tensor(
                    k[:, h * T:(h + 1) * T], gate[:, h * T:(h + 1) * T],
                    normw[:, h:h + 1], pb[5][:, 0:512], OP.mult, OP.mult))
                P.bar()
            proj_fm(W["Wo"], D, NDC, lambda dc: k[:, dc * T:(dc + 1) * T],
                    lambda oc: v_op(lambda e, oc=oc: e.tensor_add(
                        xv(xhat, oc, 0, T), pb[oc][:, 0:512], xv(xs, oc, 0, T))), "pj")
            dst3 = dst.rearrange("(dc p) n -> p dc n", p=128)[:, :, t0:t0 + T]
            src3 = xhat[:].rearrange("p (dc u) -> p dc u", u=TH)[:, :, 0:T]
            dma_out(dst3, src3)
            if dbg_dst is not None:
                dma_out(dbg_dst.rearrange("(dc p) n -> p dc n", p=128)[:, :, t0:t0 + T], src3)
            P.wait(G, "st")
            P.bar()
            if stop == "tile0" and it == 0:
                astk.close()
                return "stopped"
        P.wait("sync", "st")
        P.bar()
        astk.close()

    def emit_attn_copy(src, dst, dbg_dst=None):
        for it in range(NT):
            t0 = it * T
            dma_in(xs[:, 0:NDC * T].rearrange("p (dc n) -> p dc n", n=T),
                   src.rearrange("(dc p) n -> p dc n", p=128)[:, :, t0:t0 + T])
            P.wait(G, "ld")
            dma_out(dst.rearrange("(dc p) n -> p dc n", p=128)[:, :, t0:t0 + T],
                    xs[:, 0:NDC * T].rearrange("p (dc n) -> p dc n", n=T))
            if dbg_dst is not None:
                dma_out(dbg_dst.rearrange("(dc p) n -> p dc n", p=128)[:, :, t0:t0 + T],
                        xs[:, 0:NDC * T].rearrange("p (dc n) -> p dc n", n=T))
            P.wait("sync", "st")
            P.bar()

    def finish_early():
        dma_out(outT.rearrange("(dc p) n -> p dc n", p=128)[:, :, 0:512],
                xs[:, 0:NDC * 512].rearrange("p (dc n) -> p dc n", n=512))
        P.wait(G, "st")
        P.bar()
        P.finalize()
        ctx.close()
        return nc

    if stop_after in ("prologue", "consts", "w1"):
        dma_in(xs[:, 0:NDC * 512].rearrange("p (dc n) -> p dc n", n=512),
               xT.rearrange("(dc p) n -> p dc n", p=128)[:, :, 0:512])
        P.wait(G, "ld")
        return finish_early()
    if stop_after.startswith("ffn_"):
        r = emit_ffn(xT, x1d, dbg.get("x1o"), stop=stop_after[4:])
        return finish_early()
    emit_ffn(xT, x1d, dbg.get("x1o"))
    if stop_after == "ffnA":
        dma_in(xs[:, 0:NDC * 512].rearrange("p (dc n) -> p dc n", n=512),
               x1d.rearrange("(dc p) n -> p dc n", p=128)[:, :, 0:512])
        P.wait(G, "ld")
        return finish_early()
    if attn_mode == "copy":
        emit_attn_copy(x1d, x2d, dbg.get("x2o"))
    elif stop_after.startswith("attn_"):
        emit_attn(x1d, x2d, dbg.get("x2o"), stop=stop_after[5:])
        return finish_early()
    else:
        emit_attn(x1d, x2d, dbg.get("x2o"))
    emit_ffn(x2d, outT, None, tag="c")
    P.wait(G, "st")
    P.bar()

    P.finalize()
    ctx.close()
    return nc


def _pack_consts():
    j = np.arange(128)
    tri = (j[:, None] <= j[None, :]).astype(np.float32)
    trih = (np.arange(64)[:, None] <= np.arange(64)[None, :]).astype(np.float32)
    return {
        "c_ident": np.eye(128, dtype=np.float32),
        "c_tri": tri,
        "c_trih": trih,
        "c_ones": np.ones((128, 128), np.float32),
        "c_zeros": np.zeros((128, 512), np.float32),
    }


def kernel(L0_attn_x, L0_attn_Wq, L0_attn_Wk, L0_attn_Wv, L0_attn_Wf1, L0_attn_Wf2,
           L0_attn_Wg1, L0_attn_Wg2, L0_attn_Wo, L0_attn_norm_w,
           L0_ffn_w1, L0_ffn_conv_w, L0_ffn_conv_b, L0_ffn_w2):
    x = np.asarray(L0_attn_x, dtype=np.float32)
    consts = _pack_consts()
    convw = np.ascontiguousarray(
        np.asarray(L0_ffn_conv_w, np.float32).reshape(NHC, 128, 3).transpose(1, 0, 2)
        .reshape(128, NHC * 3))
    convw = np.ascontiguousarray(
        np.asarray(L0_ffn_conv_w, np.float32).reshape(NHC, 128, 3)
        .transpose(1, 0, 2).reshape(128, NHC * 3))
    convb = np.ascontiguousarray(np.asarray(L0_ffn_conv_b, np.float32).reshape(NHC, 128).T)
    normw = np.ascontiguousarray(np.asarray(L0_attn_norm_w, np.float32).reshape(NDC, 128).T)

    nc = build_program(attn_mode="attn")
    base = {
        "Wq": np.asarray(L0_attn_Wq, np.float32), "Wk": np.asarray(L0_attn_Wk, np.float32),
        "Wv": np.asarray(L0_attn_Wv, np.float32), "Wo": np.asarray(L0_attn_Wo, np.float32),
        "Wf1": np.asarray(L0_attn_Wf1, np.float32), "Wf2": np.asarray(L0_attn_Wf2, np.float32),
        "Wg1": np.asarray(L0_attn_Wg1, np.float32), "Wg2": np.asarray(L0_attn_Wg2, np.float32),
        "w1": np.asarray(L0_ffn_w1, np.float32), "w2": np.asarray(L0_ffn_w2, np.float32),
        "c_convw": convw, "c_convb": convb, "c_normw": normw,
        **consts,
    }
    in_maps = []
    for b in range(NCORE):
        m = dict(base)
        m["xT"] = np.ascontiguousarray(x[b].T)
        in_maps.append(m)
    res = run_bass_kernel_spmd(nc, in_maps, core_ids=list(range(NCORE)))
    out = np.stack([np.asarray(res.results[b]["outT"]).T for b in range(NCORE)], axis=0)
    return out.astype(np.float32)


# version 10 (final)
# speedup vs baseline: 1.0973x; 1.0177x over previous
import contextlib
import numpy as np

import concourse.bass as bass
import concourse.mybir as mybir
from concourse.bass_utils import run_bass_kernel_spmd

F32 = mybir.dt.float32
BF16 = mybir.dt.bfloat16
I32 = mybir.dt.int32
AF = mybir.ActivationFunctionType
OP = mybir.AluOpType
AX = mybir.AxisListType

D = 1024
NTOK = 4096
T = 512
NT = NTOK // T
NDC = D // 128
HID = 4096
NHC = HID // 128
TH = T + 2
C = 128
NCH = T // C
H = 8
DK = 128
M = 64
EPS = 1e-5
NCORE = 8


class Prog:
    ENGS = ("sync", "vector", "scalar", "tensor", "gpsimd")
    DMA_ENGS = ("sync", "gpsimd")

    def __init__(self, nc, ctx):
        self.nc = nc
        self.streams = {k: [] for k in self.ENGS}
        self.sems = {}
        self.dirty = set()
        for e in self.ENGS:
            self.sems["bar_" + e] = [ctx.enter_context(nc.semaphore("bar_" + e)), 0]
            self.sems["ch_" + e] = [ctx.enter_context(nc.semaphore("ch_" + e)), 0]
        self._ctx = ctx

    def sem(self, name):
        self.sems[name] = [self._ctx.enter_context(self.nc.semaphore(name)), 0]

    def count(self, name):
        return self.sems[name][1]

    def do(self, eng, fn, sig=None, inc=1, dep=False):
        self.dirty.add(eng)
        if dep:
            tok = self._retire_token(eng)
            if tok is not None:
                h, v = tok
                self.streams[eng].append(
                    ["wait", lambda e, h=h, v=v: e.wait_ge(h, v), []])
        sigs = []
        if sig is not None:
            rec = self.sems[sig]
            rec[1] += inc
            sigs.append((rec[0], inc, rec[1]))
        self.streams[eng].append(["op", fn, sigs])
        return self.sems[sig][1] if sig is not None else None

    def _retire_token(self, eng):
        for ent in reversed(self.streams[eng]):
            if ent[0] == "op":
                if ent[2]:
                    h, inc, val = ent[2][-1]
                    return (h, val)
                rec = self.sems["ch_" + eng]
                rec[1] += 1
                ent[2].append((rec[0], 1, rec[1]))
                return (rec[0], rec[1])
        return None

    def wait(self, eng, sig, val=None):
        rec = self.sems[sig]
        v = rec[1] if val is None else val
        h = rec[0]
        self.streams[eng].append(["wait", lambda e, h=h, v=v: e.wait_ge(h, v), []])

    def bar(self):
        prods = sorted(self.dirty)
        if not prods:
            return
        toks = []
        for p in prods:
            if p in self.DMA_ENGS:
                rec = self.sems["bar_" + p]
                rec[1] += 1
                self.streams[p].append(
                    ["op", lambda e: e.nop(), [(rec[0], 1, rec[1])]])
                toks.append((p, rec[0], rec[1]))
            else:
                tok = self._retire_token(p)
                if tok is None:
                    raise RuntimeError(f"bar: no op to attach on {p}")
                toks.append((p, tok[0], tok[1]))
        for e in self.ENGS:
            for p, h, v in toks:
                if p == e:
                    continue
                self.streams[e].append(
                    ["wait", lambda en, h=h, v=v: en.wait_ge(h, v), []])
        self.dirty = set()

    def _replay(self, e, entries):
        for kind, fn, sigs in entries:
            inst = fn(e)
            for h, inc, _v in sigs:
                inst = inst.then_inc(h, inc)

    def finalize(self):
        with self.nc.Block() as block:
            @block.sync
            def _(e):
                self._replay(e, self.streams["sync"])

            @block.vector
            def _(e):
                self._replay(e, self.streams["vector"])

            @block.scalar
            def _(e):
                self._replay(e, self.streams["scalar"])

            @block.tensor
            def _(e):
                self._replay(e, self.streams["tensor"])

            @block.gpsimd
            def _(e):
                self._replay(e, self.streams["gpsimd"])


def build_program(attn_mode="attn", debug=False, stop_after="full"):
    nc = bass.Bass("TRN2", target_bir_lowering=False, detect_race_conditions=False)
    ctx = contextlib.ExitStack()

    xT = nc.dram_tensor("xT", [D, NTOK], F32, kind="ExternalInput").ap()
    outT = nc.dram_tensor("outT", [D, NTOK], F32, kind="ExternalOutput").ap()
    W = {}
    for name, shape in [
        ("Wq", [D, D]), ("Wk", [D, D]), ("Wv", [D, D]), ("Wo", [D, D]),
        ("Wf1", [D, DK]), ("Wf2", [DK, H * M]), ("Wg1", [D, DK]), ("Wg2", [DK, D]),
        ("w1", [HID, D]), ("w2", [D, HID]),
    ]:
        W[name] = nc.dram_tensor(name, shape, F32, kind="ExternalInput").ap()
    c_ident = nc.dram_tensor("c_ident", [128, 128], F32, kind="ExternalInput").ap()
    c_tri = nc.dram_tensor("c_tri", [128, 128], F32, kind="ExternalInput").ap()
    c_trih = nc.dram_tensor("c_trih", [64, 64], F32, kind="ExternalInput").ap()
    c_convw = nc.dram_tensor("c_convw", [128, 3 * NHC], F32, kind="ExternalInput").ap()
    c_convb = nc.dram_tensor("c_convb", [128, NHC], F32, kind="ExternalInput").ap()
    c_normw = nc.dram_tensor("c_normw", [128, NDC], F32, kind="ExternalInput").ap()
    c_ones = nc.dram_tensor("c_ones", [128, 128], F32, kind="ExternalInput").ap()
    c_zeros = nc.dram_tensor("c_zeros", [128, 512], F32, kind="ExternalInput").ap()
    x1d = nc.dram_tensor("x1d", [D, NTOK], F32).ap()
    x2d = nc.dram_tensor("x2d", [D, NTOK], F32).ap()
    w1qT = nc.dram_tensor("w1qT", [D, HID], BF16).ap()
    w2qT = nc.dram_tensor("w2qT", [HID, D], BF16).ap()
    dbg = {}
    if debug:
        for nm in ("x1o", "x2o"):
            dbg[nm] = nc.dram_tensor(nm, [D, NTOK], F32, kind="ExternalOutput").ap()
        dbg["d_xq"] = nc.dram_tensor("d_xq", [128, NDC * TH], BF16, kind="ExternalOutput").ap()
        dbg["d_h"] = nc.dram_tensor("d_h", [128, NHC * TH], F32, kind="ExternalOutput").ap()
        dbg["d_tg"] = nc.dram_tensor("d_tg", [128, 16], F32, kind="ExternalOutput").ap()
        dbg["d_xhat"] = nc.dram_tensor("d_xhat", [128, NDC * TH], F32, kind="ExternalOutput").ap()
        dbg["d_yq"] = nc.dram_tensor("d_yq", [128, 4 * T], BF16, kind="ExternalOutput").ap()
        dbg["d_sc"] = nc.dram_tensor("d_sc", [128, 2 * T], F32, kind="ExternalOutput").ap()
        dbg["a_q"] = nc.dram_tensor("a_q", [128, H * T], F32, kind="ExternalOutput").ap()
        dbg["a_k"] = nc.dram_tensor("a_k", [128, H * T], F32, kind="ExternalOutput").ap()
        dbg["a_sg"] = nc.dram_tensor("a_sg", [128, 4 * T], F32, kind="ExternalOutput").ap()
        dbg["a_ok"] = nc.dram_tensor("a_ok", [128, 512], F32, kind="ExternalOutput").ap()
        dbg["a_qv"] = nc.dram_tensor("a_qv", [128, 512], F32, kind="ExternalOutput").ap()
        dbg["a_og"] = nc.dram_tensor("a_og", [128, H * T], F32, kind="ExternalOutput").ap()
        dbg["a_S"] = nc.dram_tensor("a_S", [128, 512], F32, kind="ExternalOutput").ap()
        dbg["a_T"] = nc.dram_tensor("a_T", [128, 512], F32, kind="ExternalOutput").ap()
        dbg["a_Wm"] = nc.dram_tensor("a_Wm", [128, 128], F32, kind="ExternalOutput").ap()
        dbg["a_ov"] = nc.dram_tensor("a_ov", [128, 128], F32, kind="ExternalOutput").ap()
        dbg["a_ovh"] = nc.dram_tensor("a_ovh", [128, H * 128], F32, kind="ExternalOutput").ap()
        dbg["a_wmh"] = nc.dram_tensor("a_wmh", [128, H * 128], F32, kind="ExternalOutput").ap()
        dbg["a_sf"] = nc.dram_tensor("a_sf", [128, 512], F32, kind="ExternalOutput").ap()
        dbg["a_vqf"] = nc.dram_tensor("a_vqf", [128, 512], F32, kind="ExternalOutput").ap()

    def sb(name, shape, dtype=F32):
        return ctx.enter_context(nc.sbuf_tensor(name, shape, dtype))

    def psum(name, shape, dtype=F32):
        return ctx.enter_context(nc.psum_tensor(name, shape, dtype))

    ident = sb("ident", [128, 128])
    tri = sb("tri", [128, 128])
    trih = sb("trih", [64, 64])
    convw = sb("convw", [128, 3 * NHC])
    convb = sb("convb", [128, NHC])
    normw = sb("normw", [128, NDC])
    onest = sb("onest", [128, 128])
    zer = sb("zer", [128, 512])
    wdeq = sb("wdeq", [128, 2])
    wthr = sb("wthr", [128, 4])
    tg = sb("tg", [128, 16])
    tgrow = sb("tgrow", [16, 128])
    rowb = sb("rowb", [1, 512])
    rowbx = sb("rowbx", [1, 2])
    stat = sb("stat", [128, 16])
    S_st = sb("S_st", [128, H * M])
    T_st = sb("T_st", [128, 4 * 128])
    xs = sb("xs", [128, NDC * TH])
    xhat = sb("xhat", [128, NDC * TH])
    xtmp = sb("xtmp", [128, NDC * TH])

    pb = [psum(f"pb{i}", [128, 512]) for i in range(8)]

    P = Prog(nc, ctx)
    for s in ("ld", "st", "lw", "cva", "cvb", "q2", "q2g", "q1v", "q1g",
              "m1m", "m1e", "m2h", "pj", "wmm", "wme", "ovm", "ovc"):
        P.sem(s)

    V, S, TE, G = "vector", "scalar", "tensor", "gpsimd"

    def dma_in(dst_ap, src_ap):
        return P.do("sync", lambda e, d=dst_ap, s=src_ap: e.dma_start(out=d, in_=s), sig="ld", inc=16)

    def dma_w(dst_ap, src_ap):
        return P.do("sync", lambda e, d=dst_ap, s=src_ap: e.dma_start(out=d, in_=s), sig="lw", inc=16)

    def dma_out(dst_ap, src_ap):
        return P.do("gpsimd", lambda e, d=dst_ap, s=src_ap: e.dma_start(out=d, in_=s), sig="st", inc=16)

    def v_op(fn, **kw):
        P.do(V, fn, **kw)

    def s_op(fn, **kw):
        P.do(S, fn, **kw)

    def t_op(fn, **kw):
        P.do(TE, fn, **kw)

    def g_op(fn, **kw):
        P.do(G, fn, **kw)

    def t_transpose(out_psum, in_ap):
        p = in_ap.shape[0]
        t_op(lambda e, o=out_psum, i=in_ap, p=p: e.transpose(o, i, ident[0:p, 0:p]))

    def t_bcast(out_psum, row_ap):
        t_op(lambda e, o=out_psum, r=row_ap: e.matmul(o, onest[0:1, 0:128], r, start=True, stop=True))

    dma_in(ident[:], c_ident)
    dma_in(tri[:], c_tri)
    dma_in(trih[:], c_trih)
    dma_in(convw[:], c_convw)
    dma_in(convb[:], c_convb)
    dma_in(normw[:], c_normw)
    dma_in(onest[:], c_ones)
    dma_in(zer[:], c_zeros)
    dma_in(S_st[:], c_zeros)
    dma_in(T_st[:], c_zeros)
    dma_in(tg[:], c_zeros[:, 0:16])
    dma_in(stat[:], c_zeros[:, 0:16])
    dma_in(rowb[:], c_zeros[0:1, 0:512])
    dma_in(rowbx[:], c_zeros[0:1, 0:2])
    P.wait(V, "ld")
    P.wait(S, "ld")
    P.wait(TE, "ld")
    P.bar()

    def quant_weight(wsrc, nrc, ncols, dstT, deq_col, thr_pos, thr_neg, wctx):
        def fsb(name, shape, dtype=F32):
            return wctx.enter_context(nc.sbuf_tensor(name, shape, dtype))
        wbuf = fsb(f"wbuf{deq_col}", [128, ncols])
        acc = fsb(f"wacc{deq_col}", [128, 1])
        qbuf = fsb(f"wqq{deq_col}", [128, ncols], BF16)
        tb1 = fsb(f"wtb1{deq_col}", [128, ncols])
        tb2 = fsb(f"wtb2{deq_col}", [128, ncols])
        nelem = nrc * 128 * ncols
        for r in range(nrc):
            dma_w(wbuf[:], wsrc[128 * r:128 * (r + 1), :])
            P.wait(V, "lw")
            if r == 0:
                v_op(lambda e: e.tensor_reduce(acc[:], wbuf[:], AX.X, OP.add,
                                               apply_absolute_value=True))
            else:
                v_op(lambda e: e.tensor_reduce(stat[:, 0:1], wbuf[:], AX.X, OP.add,
                                               apply_absolute_value=True))
                v_op(lambda e: e.tensor_add(acc[:], acc[:], stat[:, 0:1]), dep=True)
            P.bar()
        t_op(lambda e: e.matmul(pb[7][0:1, 0:1], acc[:], onest[:, 0:1], start=True, stop=True))
        P.bar()
        v_op(lambda e: e.tensor_scalar(stat[0:1, 0:1], pb[7][0:1, 0:1], 1.0 / nelem, 1e-5,
                                       OP.mult, OP.max))
        v_op(lambda e: e.tensor_scalar(stat[0:1, 1:2], stat[0:1, 0:1], 0.5, None, OP.mult), dep=True)
        P.bar()
        t_bcast(pb[7][:, 0:1], stat[0:1, 0:1])
        t_bcast(pb[7][:, 1:2], stat[0:1, 1:2])
        P.bar()
        v_op(lambda e: e.tensor_copy(wdeq[:, deq_col:deq_col + 1], pb[7][:, 0:1]))
        v_op(lambda e: e.tensor_copy(wthr[:, thr_pos:thr_pos + 1], pb[7][:, 1:2]))
        v_op(lambda e: e.tensor_scalar(wthr[:, thr_neg:thr_neg + 1], pb[7][:, 1:2], -1.0,
                                       None, OP.mult))
        P.bar()
        nbc = ncols // 128
        for r in range(nrc):
            dma_w(wbuf[:], wsrc[128 * r:128 * (r + 1), :])
            P.wait(TE, "lw")
            for b0 in range(0, nbc, 4):
                bank = pb[(b0 // 4) % 2]
                n = min(4, nbc - b0)
                for j in range(n):
                    t_transpose(bank[:, j * 128:(j + 1) * 128],
                                wbuf[:, (b0 + j) * 128:(b0 + j + 1) * 128])
                P.bar()
                v_op(lambda e, b0=b0, n=n, bank=bank: e.tensor_copy(
                    tb1[:, b0 * 128:(b0 + n) * 128], bank[:, 0:n * 128]))
                P.bar()
            v_op(lambda e: e.tensor_scalar(tb2[:], tb1[:], wthr[:, thr_pos:thr_pos + 1],
                                           None, OP.is_gt), dep=True)
            v_op(lambda e: e.tensor_scalar(tb1[:], tb1[:], wthr[:, thr_neg:thr_neg + 1],
                                           None, OP.is_lt), dep=True)
            v_op(lambda e: e.scalar_tensor_tensor(qbuf[:], tb1[:], -1.0, tb2[:],
                                                  OP.mult, OP.add), dep=True)
            P.bar()
            dst_ap = dstT.rearrange("(b p) (rr i) -> p b rr i", p=128, i=128)[:, :, r, :]
            src_ap = qbuf[:].rearrange("p (b i) -> p b i", i=128)
            dma_out(dst_ap, src_ap)
            P.wait(G, "st")
        P.bar()

    _stop = {"v": None}
    if stop_after == "consts":
        _stop["v"] = "consts"
    if _stop["v"] is None:
        with contextlib.ExitStack() as wctx:
            quant_weight(W["w1"], NHC, D, w1qT, 0, 0, 1, wctx)
            if stop_after != "w1":
                quant_weight(W["w2"], NDC, HID, w2qT, 1, 2, 3, wctx)
            P.wait("sync", "st")
            P.bar()

    def emit_ln(nu):
        extras = nu == TH
        xs3 = xs[:].rearrange("p (dc u) -> p dc u", u=TH)
        xh3 = xhat[:].rearrange("p (dc u) -> p dc u", u=TH)
        xt3 = xtmp[:].rearrange("p (dc u) -> p dc u", u=TH)
        ns = 5 if extras else 4

        def xv(buf, dc, lo, hi):
            return buf[:, dc * TH + lo: dc * TH + hi]

        s_op(lambda e: e.activation(xtmp[:, 0:NDC * TH], xs[:, 0:NDC * TH], AF.Square))
        P.bar()
        for g in range(4):
            for dc in range(NDC):
                t_op(lambda e, g=g, dc=dc: e.matmul(
                    pb[6][:, g:g + 1], xv(xs, dc, g * 128, (g + 1) * 128), onest[:, 0:1],
                    start=(dc == 0), stop=(dc == NDC - 1)))
            for dc in range(NDC):
                t_op(lambda e, g=g, dc=dc: e.matmul(
                    pb[6][:, 4 + g:5 + g], xv(xtmp, dc, g * 128, (g + 1) * 128), onest[:, 0:1],
                    start=(dc == 0), stop=(dc == NDC - 1)))
        if extras:
            for dc in range(NDC):
                t_op(lambda e, dc=dc: e.matmul(pb[6][0:2, 8:9], xv(xs, dc, 512, 514),
                                               onest[:, 0:1], start=(dc == 0), stop=(dc == NDC - 1)))
            for dc in range(NDC):
                t_op(lambda e, dc=dc: e.matmul(pb[6][0:2, 9:10], xv(xtmp, dc, 512, 514),
                                               onest[:, 0:1], start=(dc == 0), stop=(dc == NDC - 1)))
        P.bar()
        v_op(lambda e: e.tensor_scalar(tg[:, 0:4], pb[6][:, 0:4], 1.0 / D, None, OP.mult))
        v_op(lambda e: e.tensor_scalar(stat[:, 0:4], pb[6][:, 4:8], 1.0 / D, None, OP.mult))
        if extras:
            v_op(lambda e: e.tensor_scalar(tg[0:2, 4:5], pb[6][0:2, 8:9], 1.0 / D, None, OP.mult))
            v_op(lambda e: e.tensor_scalar(stat[0:2, 4:5], pb[6][0:2, 9:10], 1.0 / D, None, OP.mult))
        v_op(lambda e, ns=ns: e.tensor_mul(stat[:, 8:8 + ns], tg[:, 0:ns], tg[:, 0:ns]), dep=True)
        v_op(lambda e, ns=ns: e.tensor_sub(stat[:, 0:ns], stat[:, 0:ns], stat[:, 8:8 + ns]), dep=True)
        v_op(lambda e, ns=ns: e.tensor_scalar(stat[:, 0:ns], stat[:, 0:ns], EPS, None, OP.add), dep=True)
        P.bar()
        s_op(lambda e, ns=ns: e.activation(stat[:, 0:ns], stat[:, 0:ns], AF.Ln))
        s_op(lambda e, ns=ns: e.activation(tg[:, 5:5 + ns], stat[:, 0:ns], AF.Exp, scale=-0.5), dep=True)
        P.bar()
        emit_bcast(0, ns, 4, 256)
        emit_bcast(5, ns, 5, 258)
        v_op(lambda e: e.tensor_sub(xt3[:, :, 0:512], xs3[:, :, 0:512],
                                    pb[4][:, 0:512].unsqueeze(1).broadcast_to((128, NDC, 512))))
        if extras:
            v_op(lambda e: e.tensor_sub(xt3[:, :, 512:514], xs3[:, :, 512:514],
                                        pb[7][:, 256:258].unsqueeze(1).broadcast_to((128, NDC, 2))))
        v_op(lambda e: e.tensor_mul(xh3[:, :, 0:512], xt3[:, :, 0:512],
                                    pb[5][:, 0:512].unsqueeze(1).broadcast_to((128, NDC, 512))), dep=True)
        if extras:
            v_op(lambda e: e.tensor_mul(xh3[:, :, 512:514], xt3[:, :, 512:514],
                                        pb[7][:, 258:260].unsqueeze(1).broadcast_to((128, NDC, 2))), dep=True)
        P.bar()

    def emit_bcast(tg_lo, ns, bank, extra_off):
        for g in range(4):
            t_transpose(pb[6][0:1, g * 128:(g + 1) * 128], tg[:, tg_lo + g:tg_lo + g + 1])
        if ns == 5:
            t_transpose(pb[7][0:1, 384:386], tg[0:2, tg_lo + 4:tg_lo + 5])
        P.bar()
        v_op(lambda e: e.tensor_copy(rowb[:], pb[6][0:1, 0:512]))
        if ns == 5:
            v_op(lambda e: e.tensor_copy(rowbx[:], pb[7][0:1, 384:386]))
        P.bar()
        for g in range(4):
            t_bcast(pb[bank][:, g * 128:(g + 1) * 128], rowb[0:1, g * 128:(g + 1) * 128])
        if ns == 5:
            t_bcast(pb[7][:, extra_off:extra_off + 2], rowbx[0:1, 0:2])
        P.bar()

    def emit_ffn(src, dst, dbg_dst=None, tag="a", stop=None):
        fstk = contextlib.ExitStack()

        def fsb(name, shape, dtype=F32):
            return fstk.enter_context(nc.sbuf_tensor(tag + name, shape, dtype))

        h = fsb("h", [128, NHC * TH])
        ni32 = fsb("ni32", [128, NDC * TH], I32)
        xqb = fsb("xqb", [128, NDC * TH], BF16)
        yqg = [fsb(f"yqg{i}", [128, 4 * T], BF16) for i in range(2)]
        wg1 = [fsb(f"wg1_{i}", [128, NDC * 1024], BF16) for i in range(2)]
        wg2 = [fsb(f"wg2_{i}", [128, 1024], BF16) for i in range(2)]
        yscr = [fsb(f"yscr{i}", [128, T]) for i in range(2)]
        sc1 = fsb("sc1", [128, TH])
        sc2 = fsb("sc2", [128, T])
        scq = fsb("scq", [128, T])
        smax = fsb("smax", [128, T])

        xs3 = xs[:].rearrange("p (dc u) -> p dc u", u=TH)
        xh3 = xhat[:].rearrange("p (dc u) -> p dc u", u=TH)
        xt3 = xtmp[:].rearrange("p (dc u) -> p dc u", u=TH)
        ni3 = ni32[:].rearrange("p (dc u) -> p dc u", u=TH)

        def xv(buf, dc, lo, hi):
            return buf[:, dc * TH + lo: dc * TH + hi]

        def hv(hc, lo, hi):
            return h[:, hc * TH + lo: hc * TH + hi]

        NG = 4
        GHC = NHC // NG

        for it in range(NT):
            t0 = it * T
            lo_u = 1 if it == 0 else 0
            hi_u = TH - 1 if it == NT - 1 else TH
            g0 = t0 - 1 + lo_u
            dma_in(xs3[:, :, lo_u:hi_u],
                   src.rearrange("(dc p) n -> p dc n", p=128)[:, :, g0:g0 + (hi_u - lo_u)])
            lw0 = dma_w(wg1[0][:].rearrange("p (dc n) -> p dc n", n=1024),
                        w1qT.rearrange("(dc p) hid -> p dc hid", p=128)[:, :, 0:1024])
            P.wait(V, "ld")
            P.wait(S, "ld")
            P.wait(TE, "ld")
            if it == 0:
                v_op(lambda e: e.tensor_copy(xs3[:, :, 0:1], xs3[:, :, 1:2]))
            if it == NT - 1:
                v_op(lambda e: e.tensor_copy(xs3[:, :, TH - 1:TH], zer[:, 0:NDC]))
            P.bar()

            emit_ln(TH)
            if stop == "ln" and it == 0:
                P.bar()
                fstk.close()
                return "stopped"
            if dbg_dst is not None and it == 0 and dbg.get("d_xhat") is not None:
                dma_out(dbg["d_xhat"], xhat[:])
                P.wait(G, "st")
            v_op(lambda e: e.tensor_reduce(
                smax[:, 0:512], xhat[:].rearrange("p (dc u) -> p u dc", u=TH)[:, 0:512, :],
                AX.X, OP.max, apply_absolute_value=True))
            v_op(lambda e: e.tensor_reduce(
                stat[:, 6:8], xhat[:].rearrange("p (dc u) -> p u dc", u=TH)[:, 512:514, :],
                AX.X, OP.max, apply_absolute_value=True))
            P.bar()
            for g in range(4):
                t_transpose(pb[6][:, g * 128:(g + 1) * 128], smax[:, g * 128:(g + 1) * 128])
            t_transpose(pb[7][0:2, 260:388], stat[:, 6:8])
            P.bar()
            for g in range(4):
                v_op(lambda e, g=g: e.tensor_reduce(tg[:, 10 + g:11 + g],
                                                    pb[6][:, g * 128:(g + 1) * 128], AX.X, OP.max))
            v_op(lambda e: e.tensor_reduce(tg[0:2, 14:15], pb[7][0:2, 260:388], AX.X, OP.max))
            v_op(lambda e: e.tensor_scalar(tg[:, 10:15], tg[:, 10:15], 1e-5, None, OP.max), dep=True)
            v_op(lambda e: e.reciprocal(stat[:, 0:5], tg[:, 10:15]), dep=True)
            v_op(lambda e: e.tensor_scalar(tg[:, 10:15], stat[:, 0:5], 127.0, None, OP.mult), dep=True)
            v_op(lambda e: e.reciprocal(stat[:, 0:5], tg[:, 10:15]), dep=True)
            v_op(lambda e: e.tensor_scalar(stat[:, 0:5], stat[:, 0:5], wdeq[:, 0:1], None, OP.mult), dep=True)
            P.bar()
            emit_bcast(10, 5, 4, 256)
            v_op(lambda e: e.tensor_copy(tg[:, 10:15], stat[:, 0:5]))
            P.bar()
            emit_bcast(10, 5, 5, 258)
            v_op(lambda e: e.tensor_copy(sc1[:, 0:512], pb[5][:, 0:512]))
            v_op(lambda e: e.tensor_copy(sc1[:, 512:514], pb[7][:, 258:260]))
            v_op(lambda e: e.tensor_mul(ni3[:, :, 0:512], xh3[:, :, 0:512],
                                        pb[4][:, 0:512].unsqueeze(1).broadcast_to((128, NDC, 512))))
            v_op(lambda e: e.tensor_mul(ni3[:, :, 512:514], xh3[:, :, 512:514],
                                        pb[7][:, 256:258].unsqueeze(1).broadcast_to((128, NDC, 2))))
            v_op(lambda e: e.tensor_mul(xt3[:, :, 0:512], xh3[:, :, 0:512],
                                        pb[4][:, 0:512].unsqueeze(1).broadcast_to((128, NDC, 512))))
            v_op(lambda e: e.tensor_mul(xt3[:, :, 512:514], xh3[:, :, 512:514],
                                        pb[7][:, 256:258].unsqueeze(1).broadcast_to((128, NDC, 2))))
            v_op(lambda e: e.scalar_tensor_tensor(xhat[:, 0:NDC * TH], ni32[:, 0:NDC * TH], -1.0,
                                                  xtmp[:, 0:NDC * TH], OP.mult, OP.add), dep=True)
            v_op(lambda e: e.tensor_scalar(xtmp[:, 0:NDC * TH], xhat[:, 0:NDC * TH], 0.5,
                                           None, OP.is_gt), dep=True)
            v_op(lambda e: e.scalar_tensor_tensor(xtmp[:, 0:NDC * TH], xtmp[:, 0:NDC * TH], 0.0,
                                                  ni32[:, 0:NDC * TH], OP.bypass, OP.add), dep=True)
            v_op(lambda e: e.tensor_scalar(xhat[:, 0:NDC * TH], xhat[:, 0:NDC * TH], -0.5,
                                           None, OP.is_lt), dep=True)
            v_op(lambda e: e.scalar_tensor_tensor(xqb[:, 0:NDC * TH], xhat[:, 0:NDC * TH], -1.0,
                                                  xtmp[:, 0:NDC * TH], OP.mult, OP.add), dep=True)
            P.bar()

            if dbg_dst is not None and it == 0 and dbg.get("d_xq") is not None:
                dma_out(dbg["d_xq"], xqb[:])
                dma_out(dbg["d_tg"], tg[:])
                P.wait(G, "st")
            if stop == "q" and it == 0:
                P.bar()
                fstk.close()
                return "stopped"
            lw_have = lw0
            m1m0 = P.count("m1m")
            m1e0 = P.count("m1e")
            cvb0 = P.count("cvb")
            for g in range(NG):
                if g + 1 < NG:
                    if g >= 1:
                        P.wait("sync", "m1m", m1m0 + g * GHC)
                    lw_next = dma_w(wg1[(g + 1) % 2][:].rearrange("p (dc n) -> p dc n", n=1024),
                                    w1qT.rearrange("(dc p) hid -> p dc hid", p=128)
                                    [:, :, (g + 1) * 1024:(g + 2) * 1024])
                P.wait(TE, "lw", lw_have)
                wbuf = wg1[g % 2]
                for j in range(GHC):
                    hc = g * GHC + j
                    bank = pb[hc % 2]
                    if hc >= 2:
                        P.wait(TE, "m1e", m1e0 + hc - 1)
                    for dc in range(NDC):
                        t_op(lambda e, bank=bank, dc=dc, wbuf=wbuf, j=j: e.matmul(
                            bank[:, 0:512],
                            wbuf[:, dc * 1024 + j * 128: dc * 1024 + (j + 1) * 128],
                            xqb[:, dc * TH + 0: dc * TH + 512],
                            start=(dc == 0), stop=(dc == NDC - 1)))
                    for dc in range(NDC):
                        t_op(lambda e, dc=dc, wbuf=wbuf, j=j, hc=hc: e.matmul(
                            pb[2 + hc % 2][:, 0:2],
                            wbuf[:, dc * 1024 + j * 128: dc * 1024 + (j + 1) * 128],
                            xqb[:, dc * TH + 512: dc * TH + 514],
                            start=(dc == 0), stop=(dc == NDC - 1)),
                            sig="m1m" if dc == NDC - 1 else None)
                    P.wait(V, "m1m", m1m0 + hc + 1)
                    v_op(lambda e, hc=hc: e.tensor_mul(hv(hc, 512, 514),
                                                       pb[2 + hc % 2][:, 0:2],
                                                       sc1[:, 512:514]))
                    v_op(lambda e, hc=hc, bank=bank: e.tensor_mul(hv(hc, 0, 512),
                                                                  bank[:, 0:512], sc1[:, 0:512]),
                         sig="m1e")
                    if it == 0:
                        v_op(lambda e, hc=hc: e.tensor_copy(h[:, hc * TH:hc * TH + 1], zer[:, 0:1]))
                    if it == NT - 1:
                        v_op(lambda e, hc=hc: e.tensor_copy(h[:, hc * TH + TH - 1:hc * TH + TH], zer[:, 0:1]))
                    yb = yscr[hc % 2]
                    if hc >= 2:
                        P.wait(V, "cvb", cvb0 + hc - 1)
                    v_op(lambda e, hc=hc, yb=yb: e.tensor_scalar(
                        yb[:], hv(hc, 1, 513), convw[:, hc * 3 + 1:hc * 3 + 2],
                        convb[:, hc:hc + 1], OP.mult, OP.add), dep=True)
                    v_op(lambda e, hc=hc, yb=yb: e.scalar_tensor_tensor(
                        yb[:], hv(hc, 0, 512), convw[:, hc * 3 + 0:hc * 3 + 1], yb[:],
                        OP.mult, OP.add), dep=True)
                    v_op(lambda e, hc=hc, yb=yb: e.scalar_tensor_tensor(
                        yb[:], hv(hc, 2, 514), convw[:, hc * 3 + 2:hc * 3 + 3], yb[:],
                        OP.mult, OP.add), sig="cva", dep=True)
                    P.wait(S, "cva")
                    s_op(lambda e, hc=hc, yb=yb: e.activation(hv(hc, 1, 513), yb[:], AF.Silu),
                         sig="cvb")
                if g + 1 < NG:
                    lw_have = lw_next
            P.bar()

            if stop in ("mm1", "conv") and it == 0:
                P.bar()
                fstk.close()
                return "stopped"

            if dbg_dst is not None and it == 0 and dbg.get("d_h") is not None:
                dma_out(dbg["d_h"], h[:])
                P.wait(G, "st")
            if stop == "conv" and it == 0:
                P.bar()
                fstk.close()
                return "stopped"
            v_op(lambda e: e.tensor_reduce(
                smax[:, 0:512], h[:].rearrange("p (hc u) -> p u hc", u=TH)[:, 1:513, :],
                AX.X, OP.max, apply_absolute_value=True))
            P.bar()
            for g in range(4):
                t_transpose(pb[6][:, g * 128:(g + 1) * 128], smax[:, g * 128:(g + 1) * 128])
            P.bar()
            for g in range(4):
                v_op(lambda e, g=g: e.tensor_reduce(tg[:, 10 + g:11 + g],
                                                    pb[6][:, g * 128:(g + 1) * 128], AX.X, OP.max))
            v_op(lambda e: e.tensor_scalar(tg[:, 10:14], tg[:, 10:14], 1e-5, None, OP.max), dep=True)
            v_op(lambda e: e.reciprocal(stat[:, 0:4], tg[:, 10:14]), dep=True)
            v_op(lambda e: e.tensor_scalar(tg[:, 10:14], stat[:, 0:4], 127.0, None, OP.mult), dep=True)
            v_op(lambda e: e.reciprocal(stat[:, 0:4], tg[:, 10:14]), dep=True)
            v_op(lambda e: e.tensor_scalar(stat[:, 0:4], stat[:, 0:4], wdeq[:, 1:2], None, OP.mult), dep=True)
            v_op(lambda e: e.tensor_scalar(stat[:, 0:4], stat[:, 0:4], 0.5, None, OP.mult), dep=True)
            P.bar()
            emit_bcast(10, 4, 4, 0)
            v_op(lambda e: e.tensor_copy(scq[:, 0:512], pb[4][:, 0:512]))
            v_op(lambda e: e.tensor_copy(tg[:, 10:14], stat[:, 0:4]))
            P.bar()
            emit_bcast(10, 4, 4, 0)
            v_op(lambda e: e.tensor_copy(sc2[:, 0:512], pb[4][:, 0:512]))
            P.bar()

            lw2 = []
            lw2.append(dma_w(wg2[0][:], w2qT[0:128, :]))
            h4 = h[:].rearrange("p (hc u) -> p hc u", u=TH)
            for q in range(8):
                ybq = yqg[q % 2]
                if q >= 2:
                    P.wait(V, "m2h", P.count("m2h") - 4)
                hgv = h4[:, q * 4:(q + 1) * 4, 1:513]
                niv = ni32[:, 0:4 * T].rearrange("p (c t) -> p c t", t=T)
                xtv = xtmp[:, 0:4 * T].rearrange("p (c t) -> p c t", t=T)
                xhv = xhat[:, 0:4 * T].rearrange("p (c t) -> p c t", t=T)
                ybv = ybq[:].rearrange("p (c t) -> p c t", t=T)
                scb = scq[:, 0:512].unsqueeze(1).broadcast_to((128, 4, 512))
                v_op(lambda e, hgv=hgv, niv=niv, scb=scb: e.tensor_mul(niv, hgv, scb))
                v_op(lambda e, hgv=hgv, xtv=xtv, scb=scb: e.tensor_mul(xtv, hgv, scb))
                v_op(lambda e, niv=niv, xtv=xtv, xhv=xhv: e.scalar_tensor_tensor(
                    xhv, niv, -1.0, xtv, OP.mult, OP.add), dep=True)
                v_op(lambda e, xtv=xtv, xhv=xhv: e.tensor_scalar(xtv, xhv, 0.5, None, OP.is_gt), dep=True)
                v_op(lambda e, xtv=xtv, niv=niv: e.scalar_tensor_tensor(
                    xtv, xtv, 0.0, niv, OP.bypass, OP.add), dep=True)
                v_op(lambda e, xhv=xhv: e.tensor_scalar(xhv, xhv, -0.5, None, OP.is_lt), dep=True)
                v_op(lambda e, ybv=ybv, xhv=xhv, xtv=xtv: e.scalar_tensor_tensor(
                    ybv, xhv, -1.0, xtv, OP.mult, OP.add), sig="q2", dep=True)
                if dbg_dst is not None and it == 0 and q == 0 and dbg.get("d_yq") is not None:
                    P.bar()
                    dma_out(dbg["d_yq"], ybq[:])
                    dma_out(dbg["d_sc"][:, 0:T], scq[:, 0:T])
                    dma_out(dbg["d_sc"][:, T:2 * T], sc2[:, 0:T])
                    P.wait(G, "st")
                for jj in range(4):
                    hc = q * 4 + jj
                    if hc + 1 < NHC:
                        P.wait("sync", "m2h")
                        lw2.append(dma_w(wg2[(hc + 1) % 2][:], w2qT[128 * (hc + 1):128 * (hc + 2), :]))
                    if jj == 0:
                        P.wait(TE, "q2")
                    P.wait(TE, "lw", lw2[hc])
                    for oc in range(NDC):
                        sig = "m2h" if oc == NDC - 1 else None
                        t_op(lambda e, hc=hc, oc=oc, jj=jj, q=q: e.matmul(
                            pb[oc][:, 0:512],
                            wg2[hc % 2][:, oc * 128:(oc + 1) * 128],
                            yqg[q % 2][:, jj * T:(jj + 1) * T],
                            start=(hc == 0), stop=(hc == NHC - 1)), sig=sig)
            P.bar()
            for oc in range(NDC):
                v_op(lambda e, oc=oc: e.tensor_mul(xv(xtmp, oc, 0, 512), pb[oc][:, 0:512],
                                                   sc2[:, 0:512]))
            for oc in range(NDC):
                v_op(lambda e, oc=oc: e.tensor_add(xv(xhat, oc, 0, 512), xv(xtmp, oc, 0, 512),
                                                   xv(xs, oc, 1, 513)), dep=(oc == 0))
            P.bar()
            if stop == "mm2" and it == 0:
                P.bar()
                fstk.close()
                return "stopped"
            dst3 = dst.rearrange("(dc p) n -> p dc n", p=128)[:, :, t0:t0 + T]
            src3 = xhat[:].rearrange("p (dc u) -> p dc u", u=TH)[:, :, 0:512]
            dma_out(dst3, src3)
            if dbg_dst is not None:
                dma_out(dbg_dst.rearrange("(dc p) n -> p dc n", p=128)[:, :, t0:t0 + T], src3)
            P.wait(G, "st")
            P.bar()
            if stop == "t0" and it == 0:
                fstk.close()
                return "stopped"
        P.wait("sync", "st")
        P.bar()
        fstk.close()


    def emit_attn(src, dst, dbg_dst=None, stop=None):
        astk = contextlib.ExitStack()

        def asb(name, shape, dtype=F32):
            return astk.enter_context(nc.sbuf_tensor("at" + name, shape, dtype))

        q = asb("q", [128, H * T])
        k = asb("k", [128, H * T])
        vtm = asb("vtm", [128, NCH * 1024])
        gate = asb("gate", [128, H * T])
        f1b = asb("f1b", [128, T])
        g1b = asb("g1b", [128, T])
        sgb = asb("sgb", [128, 4 * T])
        s_fm = asb("s_fm", [128, 4 * T])
        g_fm = asb("g_fm", [128, 4 * T])
        Wp = [asb(f"Wp{i}", [128, 1024]) for i in range(2)]
        wf2b = asb("wf2b", [128, H * M])
        wg2b = asb("wg2b", [128, D])
        Lr = asb("Lr", [128, 512])
        Lf = asb("Lf", [128, 512])
        eLf = asb("eLf", [128, 512])
        eLnf = asb("eLnf", [128, 512])
        Sf = asb("Sf", [128, 512])
        Stm = asb("Stm", [128, 512])
        eLtm = asb("eLtm", [128, 512])
        okb = asb("okb", [128, 512])
        nbb = asb("nbb", [128, 512])
        qv = asb("qv", [128, 512])
        vqt = asb("vqt", [128, 512])
        vqf = asb("vqf", [128, 512])
        Gm = asb("Gm", [128, H * 128])
        Ktm = asb("Ktm", [128, H * 128])
        Wm = asb("Wm", [128, 128])
        su = asb("su", [128, 512])
        tu = asb("tu", [128, 4 * 128])
        ecc = asb("ecc", [128, 4])
        elc = asb("elc", [128, 4])
        mst = asb("mst", [128, 32])
        sxc = asb("sxc", [128, 16])

        def xv(buf, dc, lo, hi):
            return buf[:, dc * TH + lo: dc * TH + hi]

        xs3 = xs[:].rearrange("p (dc u) -> p dc u", u=TH)

        def proj_fm(wdram, wcols, nout_chunks, rhs_fn, out_cb, pjsem):
            lws = []
            lws.append(dma_w(Wp[0][:, 0:wcols], wdram[0:128, :]))
            for dc in range(NDC):
                if dc + 1 < NDC:
                    P.wait("sync", pjsem)
                    lws.append(dma_w(Wp[(dc + 1) % 2][:, 0:wcols],
                                     wdram[128 * (dc + 1):128 * (dc + 2), :]))
                P.wait(TE, "lw", lws[dc])
                for oc in range(nout_chunks):
                    sig = pjsem if oc == nout_chunks - 1 else None
                    t_op(lambda e, dc=dc, oc=oc: e.matmul(
                        pb[oc][:, 0:512],
                        Wp[dc % 2][:, oc * 128:(oc + 1) * 128],
                        rhs_fn(dc),
                        start=(dc == 0), stop=(dc == NDC - 1)), sig=sig)
            P.bar()
            for oc in range(nout_chunks):
                out_cb(oc)
            P.bar()

        for it in range(NT):
            t0 = it * T
            dma_in(xs3[:, :, 0:T],
                   src.rearrange("(dc p) n -> p dc n", p=128)[:, :, t0:t0 + T])
            P.wait(V, "ld")
            P.wait(S, "ld")
            P.wait(TE, "ld")
            P.bar()
            emit_ln(T)
            if stop == "ln" and it == 0:
                P.bar()
                astk.close()
                return "stopped"

            def xh_rhs(dc):
                return xv(xhat, dc, 0, T)

            proj_fm(W["Wq"], D, NDC, xh_rhs,
                    lambda oc: s_op(lambda e, oc=oc: e.activation(
                        q[:, oc * T:(oc + 1) * T], pb[oc][:, 0:512], AF.Silu)), "pj")
            proj_fm(W["Wk"], D, NDC, xh_rhs,
                    lambda oc: s_op(lambda e, oc=oc: e.activation(
                        k[:, oc * T:(oc + 1) * T], pb[oc][:, 0:512], AF.Silu)), "pj")
            lws = [dma_w(Wp[0][:], W["Wv"][0:128, :])]
            for dc in range(NDC):
                if dc + 1 < NDC:
                    P.wait("sync", "pj")
                    lws.append(dma_w(Wp[(dc + 1) % 2][:], W["Wv"][128 * (dc + 1):128 * (dc + 2), :]))
                P.wait(TE, "lw", lws[dc])
                for ts in range(NCH):
                    for hf in range(2):
                        bank = ts * 2 + hf
                        sig = "pj" if (ts == NCH - 1 and hf == 1) else None
                        t_op(lambda e, dc=dc, ts=ts, hf=hf, bank=bank: e.matmul(
                            pb[bank][:, 0:512],
                            xv(xhat, dc, ts * C, (ts + 1) * C),
                            Wp[dc % 2][:, hf * 512:(hf + 1) * 512],
                            start=(dc == 0), stop=(dc == NDC - 1)), sig=sig)
            P.bar()
            for ts in range(NCH):
                for hf in range(2):
                    v_op(lambda e, ts=ts, hf=hf: e.tensor_copy(
                        vtm[:, ts * 1024 + hf * 512: ts * 1024 + (hf + 1) * 512],
                        pb[ts * 2 + hf][:, 0:512]))
            P.bar()
            for nm, outb in (("Wf1", f1b), ("Wg1", g1b)):
                lwf = [dma_w(Wp[0][:, 0:128], W[nm][0:128, :])]
                for dc in range(NDC):
                    if dc + 1 < NDC:
                        P.wait("sync", "pj")
                        lwf.append(dma_w(Wp[(dc + 1) % 2][:, 0:128],
                                         W[nm][128 * (dc + 1):128 * (dc + 2), :]))
                    P.wait(TE, "lw", lwf[dc])
                    t_op(lambda e, dc=dc: e.matmul(
                        pb[0][:, 0:512], Wp[dc % 2][:, 0:128], xv(xhat, dc, 0, T),
                        start=(dc == 0), stop=(dc == NDC - 1)), sig="pj")
                P.bar()
                v_op(lambda e, outb=outb: e.tensor_copy(outb[:], pb[0][:, 0:512]))
                P.bar()
            lw_f2 = dma_w(wf2b[:], W["Wf2"][:, :])
            P.wait(TE, "lw", lw_f2)
            for fc in range(4):
                t_op(lambda e, fc=fc: e.matmul(
                    pb[fc][:, 0:512], wf2b[:, fc * 128:(fc + 1) * 128], f1b[:],
                    start=True, stop=True))
            P.bar()
            for fc in range(4):
                s_op(lambda e, fc=fc: e.activation(sgb[:, fc * T:(fc + 1) * T],
                                                   pb[fc][:, 0:512], AF.Sigmoid))
            P.bar()
            lw_g2 = dma_w(wg2b[:], W["Wg2"][:, :])
            P.wait(TE, "lw", lw_g2)
            for oc in range(NDC):
                t_op(lambda e, oc=oc: e.matmul(
                    pb[oc][:, 0:512], wg2b[:, oc * 128:(oc + 1) * 128], g1b[:],
                    start=True, stop=True))
            P.bar()
            for oc in range(NDC):
                s_op(lambda e, oc=oc: e.activation(
                    gate[:, oc * T:(oc + 1) * T], pb[oc][:, 0:512], AF.Sigmoid))
            P.bar()
            s_op(lambda e: e.activation(g_fm[:], sgb[:], AF.Ln))
            s_op(lambda e: e.activation(s_fm[:], g_fm[:], AF.Exp), dep=True)
            P.bar()
            v_op(lambda e: e.tensor_scalar(s_fm[:], s_fm[:], -1.0, 1.0, OP.mult, OP.add))
            P.bar()

            if dbg_dst is not None and it == 0:
                dma_out(dbg["a_q"], q[:])
                dma_out(dbg["a_k"], k[:])
                dma_out(dbg["a_sg"], sgb[:])
                P.wait(G, "st")
            if stop == "proj" and it == 0:
                P.bar()
                astk.close()
                return "stopped"
            for ts in range(NCH):
                for fc in range(4):
                    v_op(lambda e, fc=fc, ts=ts: e.tensor_tensor_scan(
                        Lr[:, fc * C:(fc + 1) * C],
                        g_fm[:, fc * T + ts * C: fc * T + (ts + 1) * C],
                        g_fm[:, fc * T + ts * C: fc * T + (ts + 1) * C],
                        0.0, OP.add, OP.bypass))
                for fc in range(4):
                    v_op(lambda e, fc=fc: e.tensor_scalar(
                        Lf[:, fc * C:(fc + 1) * C], Lr[:, fc * C:(fc + 1) * C],
                        Lr[:, fc * C + 63:fc * C + 64], None, OP.subtract),
                        dep=(fc == 0))
                P.bar()
                if stop == "cum" and it == 0 and ts == 0:
                    P.bar()
                    astk.close()
                    return "stopped"
                s_op(lambda e: e.activation(eLf[:], Lf[:], AF.Exp))
                s_op(lambda e: e.activation(eLnf[:], Lf[:], AF.Exp, scale=-1.0))
                s_op(lambda e: e.activation(ecc[:], Lr[:].rearrange("p (fc c) -> p c fc", c=C)[:, 63, :], AF.Exp))
                s_op(lambda e: e.activation(elc[:], Lf[:].rearrange("p (fc c) -> p c fc", c=C)[:, C - 1, :], AF.Exp))
                P.bar()
                v_op(lambda e, ts=ts: e.tensor_mul(
                    Sf[:].rearrange("p (fc c) -> p fc c", c=C),
                    s_fm[:].rearrange("p (fc t) -> p fc t", t=T)[:, :, ts * C:(ts + 1) * C],
                    eLnf[:].rearrange("p (fc c) -> p fc c", c=C)))
                P.bar()
                if stop == "exp" and it == 0 and ts == 0:
                    P.bar()
                    astk.close()
                    return "stopped"
                for fc in range(4):
                    t_transpose(pb[6][:, fc * 128:(fc + 1) * 128], Sf[:, fc * C:(fc + 1) * C])
                P.bar()
                v_op(lambda e: e.tensor_copy(Stm[:], pb[6][:, 0:512]))
                P.bar()
                for fc in range(4):
                    t_transpose(pb[6][:, fc * 128:(fc + 1) * 128], eLf[:, fc * C:(fc + 1) * C])
                P.bar()
                v_op(lambda e: e.tensor_copy(eLtm[:], pb[6][:, 0:512]))
                P.bar()
                for hh in range(0, H, 4):
                    for h2 in range(4):
                        h = hh + h2
                        t_transpose(pb[7][:, h2 * 128:(h2 + 1) * 128],
                                    k[:, h * T + ts * C: h * T + (ts + 1) * C])
                    P.bar()
                    v_op(lambda e, hh=hh: e.tensor_copy(
                        Ktm[:, hh * 128:(hh + 4) * 128], pb[7][:, 0:512]))
                    P.bar()
                if stop == "tp" and it == 0 and ts == 0:
                    P.bar()
                    astk.close()
                    return "stopped"
                for fc in range(4):
                    t_transpose(pb[7][0:1, fc * 128:(fc + 1) * 128], ecc[:, fc:fc + 1])
                for fc in range(4):
                    t_transpose(pb[6][0:1, fc * 128:(fc + 1) * 128], elc[:, fc:fc + 1])
                P.bar()
                v_op(lambda e: e.tensor_copy(rowb[:], pb[7][0:1, 0:512]))
                v_op(lambda e: e.tensor_copy(nbb[0:1, 0:512], pb[6][0:1, 0:512]))
                P.bar()
                for h in range(H):
                    t_bcast(pb[5][:, h * 64:(h + 1) * 64],
                            rowb[0:1, (h // 2) * 128 + (h % 2) * 64: (h // 2) * 128 + (h % 2) * 64 + 64])
                P.bar()
                v_op(lambda e: e.tensor_mul(su[:], S_st[:], pb[5][:, 0:512]))
                for ft in range(4):
                    v_op(lambda e, ft=ft: e.tensor_scalar(
                        tu[:, ft * 128:(ft + 1) * 128], T_st[:, ft * 128:(ft + 1) * 128],
                        ecc[:, ft:ft + 1], None, OP.mult))
                P.bar()
                if stop == "ec" and it == 0 and ts == 0:
                    P.bar()
                    astk.close()
                    return "stopped"
                for h in range(H):
                    t_op(lambda e, h=h, ts=ts: e.matmul(
                        pb[3 + h // 4][:, (h % 4) * 128:(h % 4) * 128 + 128],
                        k[:, h * T + ts * C: h * T + (ts + 1) * C],
                        q[:, h * T + ts * C: h * T + (ts + 1) * C],
                        start=True, stop=True))
                P.bar()
                for h in range(H):
                    v_op(lambda e, h=h: e.tensor_mul(
                        Gm[:, h * 128:(h + 1) * 128],
                        pb[3 + h // 4][:, (h % 4) * 128:(h % 4) * 128 + 128], tri[:]))
                P.bar()
                if stop == "g" and it == 0 and ts == 0:
                    P.bar()
                    astk.close()
                    return "stopped"
                for h in range(H):
                    t_bcast(pb[4][:, h * 64:(h + 1) * 64],
                            nbb[0:1, (h // 2) * 128 + (h % 2) * 64: (h // 2) * 128 + (h % 2) * 64 + 64])
                for h in range(H):
                    t_op(lambda e, h=h, ts=ts: e.matmul(
                        pb[2][:, h * 64:(h + 1) * 64],
                        q[:, h * T + ts * C: h * T + (ts + 1) * C],
                        su[:, h * 64:(h + 1) * 64],
                        start=True, stop=False))
                    t_op(lambda e, h=h: e.matmul(
                        pb[2][:, h * 64:(h + 1) * 64],
                        Gm[:, h * 128:(h + 1) * 128],
                        Stm[:, h * 64:(h + 1) * 64],
                        start=False, stop=True))
                P.bar()
                if stop == "ok" and it == 0 and ts == 0:
                    P.bar()
                    astk.close()
                    return "stopped"
                v_op(lambda e: e.tensor_mul(okb[:], pb[2][:, 0:512], eLtm[:]))
                v_op(lambda e: e.tensor_reduce(sxc[:, 0:8],
                                               okb[:].rearrange("p (h m) -> p h m", m=M),
                                               AX.X, OP.max, negate=True), dep=True)
                v_op(lambda e: e.tensor_copy(nbb[:], sxc[:, 0:8].broadcast_to((128, 8, M))),
                     dep=True)
                v_op(lambda e: e.tensor_add(okb[:], okb[:], nbb[:]), dep=True)
                P.bar()
                s_op(lambda e: e.activation(okb[:], okb[:], AF.Exp))
                P.bar()
                v_op(lambda e: e.tensor_reduce(sxc[:, 0:8],
                                               okb[:].rearrange("p (h m) -> p h m", m=M),
                                               AX.X, OP.add))
                v_op(lambda e: e.reciprocal(sxc[:, 8:16], sxc[:, 0:8]), dep=True)
                v_op(lambda e: e.tensor_copy(nbb[:], sxc[:, 8:16].broadcast_to((128, 8, M))),
                     dep=True)
                v_op(lambda e: e.tensor_mul(qv[:], okb[:], nbb[:]), dep=True)
                v_op(lambda e: e.tensor_mul(vqt[:], qv[:], eLtm[:]), dep=True)
                P.bar()
                if dbg_dst is not None and it == 0 and ts == 0:
                    dma_out(dbg["a_ok"], okb[:])
                    dma_out(dbg["a_qv"], qv[:])
                    P.wait(G, "st")
                for fc in range(4):
                    t_transpose(pb[6][:, fc * 128:(fc + 1) * 128], vqt[:, fc * C:(fc + 1) * C])
                P.bar()
                v_op(lambda e: e.tensor_copy(vqf[:], pb[6][0:128, 0:512]))
                P.bar()
                if stop == "sm" and it == 0 and ts == 0:
                    P.bar()
                    astk.close()
                    return "stopped"
                if dbg_dst is not None and it == 0 and ts == 0:
                    dma_out(dbg["a_sf"], Sf[:])
                    dma_out(dbg["a_vqf"], vqf[:])
                    P.wait(G, "st")
                wmm0, wme0 = P.count("wmm"), P.count("wme")
                ovm0, ovc0 = P.count("ovm"), P.count("ovc")
                for h in range(H):
                    fc = h // 2
                    r0 = (h % 2) * 64
                    sfr = lambda a, r0=r0, fc=fc: Sf[r0:r0 + 64, fc * C + a * 64: fc * C + (a + 1) * 64]
                    vqr = lambda b, r0=r0, fc=fc: vqf[r0:r0 + 64, fc * C + b * 64: fc * C + (b + 1) * 64]
                    if h >= 1:
                        P.wait(TE, "wme", wme0 + h)
                    t_op(lambda e, h=h, sfr=sfr, vqr=vqr: e.matmul(
                        pb[3][0:64, 0:64], sfr(0), vqr(0), start=True, stop=True))
                    t_op(lambda e, sfr=sfr, vqr=vqr: e.matmul(
                        pb[3][0:64, 64:128], sfr(0), vqr(1), start=True, stop=True),
                        sig="wmm" if stop == "wmm2" else None)
                    if stop != "wmm2":
                        t_op(lambda e, sfr=sfr, vqr=vqr: e.matmul(
                            pb[3][64:128, 64:128], sfr(1), vqr(1), start=True, stop=True),
                            sig="wmm")
                    P.wait(V, "wmm", wmm0 + h + 1)
                    v_op(lambda e: e.tensor_mul(Wm[0:64, 0:64], pb[3][0:64, 0:64], trih[:]))
                    if stop != "wmm2":
                        v_op(lambda e: e.tensor_copy(Wm[0:64, 64:128], pb[3][0:64, 64:128]))
                        v_op(lambda e: e.tensor_mul(Wm[64:128, 64:128], pb[3][64:128, 64:128],
                                                    trih[:]), sig="wme", dep=True)
                    else:
                        v_op(lambda e: e.tensor_copy(Wm[0:64, 64:128], pb[3][0:64, 64:128]),
                             sig="wme", dep=True)
                    if stop in ("wmm", "wmm2"):
                        continue
                    if h >= 1:
                        P.wait(TE, "ovc", ovc0 + h)
                    P.wait(TE, "wme", wme0 + h + 1)
                    t_op(lambda e, h=h, r0=r0, fc=fc, ts=ts: e.matmul(
                        pb[1][:, 0:128],
                        tu[r0:r0 + 64, fc * 128:(fc + 1) * 128],
                        vqf[r0:r0 + 64, fc * C:(fc + 1) * C],
                        start=True, stop=True))
                    t_op(lambda e, h=h, ts=ts: e.matmul(
                        pb[0][:, 0:64],
                        vtm[0:64, ts * 1024 + h * 128: ts * 1024 + (h + 1) * 128],
                        Wm[0:64, 0:64], start=True, stop=True))
                    t_op(lambda e, h=h, ts=ts: e.matmul(
                        pb[0][:, 64:128],
                        vtm[0:64, ts * 1024 + h * 128: ts * 1024 + (h + 1) * 128],
                        Wm[0:64, 64:128], start=True, stop=True))
                    t_op(lambda e, h=h, ts=ts: e.matmul(
                        pb[2][:, 0:64],
                        vtm[64:128, ts * 1024 + h * 128: ts * 1024 + (h + 1) * 128],
                        Wm[64:128, 64:128], start=True, stop=True), sig="ovm")
                    P.wait(V, "ovm", ovm0 + h + 1)
                    v_op(lambda e: e.tensor_copy(okb[:, 0:128], pb[1][:, 0:128]))
                    v_op(lambda e: e.tensor_add(okb[:, 0:128], okb[:, 0:128],
                                                pb[0][:, 0:128]), dep=True)
                    v_op(lambda e: e.tensor_add(okb[:, 64:128], okb[:, 64:128],
                                                pb[2][:, 0:64]), dep=True)
                    if dbg_dst is not None and it == 0 and ts == 0:
                        v_op(lambda e: e.tensor_copy(Wm[64:128, 0:64], zer[0:64, 0:64]))
                        P.bar()
                        dma_out(dbg["a_ovh"][:, h * 128:(h + 1) * 128], okb[:, 0:128])
                        dma_out(dbg["a_wmh"][:, h * 128:(h + 1) * 128], Wm[:])
                        P.wait(G, "st")
                    v_op(lambda e, h=h, ts=ts: e.tensor_mul(
                        gate[:, h * T + ts * C: h * T + (ts + 1) * C],
                        okb[:, 0:128],
                        gate[:, h * T + ts * C: h * T + (ts + 1) * C]), sig="ovc", dep=True)
                if stop in ("w", "wmm", "wmm2", "wov0", "wov1", "wov2", "wov3", "wov3b", "wovev") and it == 0 and ts == 0:
                    P.bar()
                    astk.close()
                    return "stopped"
                P.bar()
                if dbg_dst is not None and it == 0 and ts == 0 and False:
                    v_op(lambda e: e.tensor_copy(Wm[64:128, 0:64], zer[0:64, 0:64]))
                    P.bar()
                    dma_out(dbg["a_Wm"], Wm[:])
                    P.wait(G, "st")
                    v_op(lambda e: e.tensor_copy(nbb[:, 0:128], pb[0][:, 0:128]))
                    P.bar()
                    dma_out(dbg["a_ov"], nbb[:, 0:128])
                    P.wait(G, "st")
                    P.bar()
                for h in range(H):
                    t_op(lambda e, h=h: e.matmul(
                        pb[5][:, h * 64:(h + 1) * 64],
                        Ktm[:, h * 128:(h + 1) * 128],
                        Stm[:, h * 64:(h + 1) * 64], start=True, stop=True))
                for h in range(H):
                    r0 = (h % 2) * 64
                    t_op(lambda e, h=h, r0=r0, ts=ts: e.matmul(
                        pb[6][r0:r0 + 64, (h // 2) * 128:(h // 2 + 1) * 128],
                        Stm[:, h * 64:(h + 1) * 64],
                        vtm[:, ts * 1024 + h * 128: ts * 1024 + (h + 1) * 128],
                        start=True, stop=True))
                P.bar()
                v_op(lambda e: e.tensor_add(su[:], su[:], pb[5][:, 0:512]))
                v_op(lambda e: e.tensor_mul(S_st[:], su[:], pb[4][:, 0:512]), dep=True)
                v_op(lambda e: e.tensor_add(tu[:], tu[:], pb[6][:, 0:512]))
                for ft in range(4):
                    v_op(lambda e, ft=ft: e.tensor_scalar(
                        T_st[:, ft * 128:(ft + 1) * 128], tu[:, ft * 128:(ft + 1) * 128],
                        elc[:, ft:ft + 1], None, OP.mult), dep=(ft == 0))
                P.bar()
                if dbg_dst is not None and it == 0 and ts == 0:
                    dma_out(dbg["a_og"], gate[:])
                    dma_out(dbg["a_S"], S_st[:])
                    dma_out(dbg["a_T"], T_st[:])
                    P.wait(G, "st")
                if stop == "chunk0" and it == 0 and ts == 0:
                    P.bar()
                    astk.close()
                    return "stopped"

            s_op(lambda e: e.activation(xtmp[:, 0:H * T], gate[:], AF.Square))
            P.bar()
            for h in range(H):
                for ts in range(NCH):
                    t_op(lambda e, h=h, ts=ts: e.matmul(
                        pb[7][:, h * 4 + ts: h * 4 + ts + 1],
                        xtmp[:, h * T + ts * C: h * T + (ts + 1) * C],
                        onest[:, 0:1], start=True, stop=True))
            P.bar()
            v_op(lambda e: e.tensor_scalar(mst[:], pb[7][:, 0:32], 1.0 / DK, EPS,
                                           OP.mult, OP.add))
            P.bar()
            s_op(lambda e: e.activation(mst[:], mst[:], AF.Ln))
            s_op(lambda e: e.activation(mst[:], mst[:], AF.Exp, scale=-0.5), dep=True)
            P.bar()
            for h in range(H):
                for ts in range(NCH):
                    t_transpose(pb[6][0:1, ts * 128:(ts + 1) * 128],
                                mst[:, h * 4 + ts: h * 4 + ts + 1])
                P.bar()
                v_op(lambda e: e.tensor_copy(rowb[:], pb[6][0:1, 0:512]))
                P.bar()
                for ts in range(NCH):
                    t_bcast(pb[5][:, ts * 128:(ts + 1) * 128], rowb[0:1, ts * 128:(ts + 1) * 128])
                P.bar()
                v_op(lambda e, h=h: e.scalar_tensor_tensor(
                    k[:, h * T:(h + 1) * T], gate[:, h * T:(h + 1) * T],
                    normw[:, h:h + 1], pb[5][:, 0:512], OP.mult, OP.mult))
                P.bar()
            proj_fm(W["Wo"], D, NDC, lambda dc: k[:, dc * T:(dc + 1) * T],
                    lambda oc: v_op(lambda e, oc=oc: e.tensor_add(
                        xv(xhat, oc, 0, T), pb[oc][:, 0:512], xv(xs, oc, 0, T))), "pj")
            dst3 = dst.rearrange("(dc p) n -> p dc n", p=128)[:, :, t0:t0 + T]
            src3 = xhat[:].rearrange("p (dc u) -> p dc u", u=TH)[:, :, 0:T]
            dma_out(dst3, src3)
            if dbg_dst is not None:
                dma_out(dbg_dst.rearrange("(dc p) n -> p dc n", p=128)[:, :, t0:t0 + T], src3)
            P.wait(G, "st")
            P.bar()
            if stop == "tile0" and it == 0:
                astk.close()
                return "stopped"
        P.wait("sync", "st")
        P.bar()
        astk.close()

    def emit_attn_copy(src, dst, dbg_dst=None):
        for it in range(NT):
            t0 = it * T
            dma_in(xs[:, 0:NDC * T].rearrange("p (dc n) -> p dc n", n=T),
                   src.rearrange("(dc p) n -> p dc n", p=128)[:, :, t0:t0 + T])
            P.wait(G, "ld")
            dma_out(dst.rearrange("(dc p) n -> p dc n", p=128)[:, :, t0:t0 + T],
                    xs[:, 0:NDC * T].rearrange("p (dc n) -> p dc n", n=T))
            if dbg_dst is not None:
                dma_out(dbg_dst.rearrange("(dc p) n -> p dc n", p=128)[:, :, t0:t0 + T],
                        xs[:, 0:NDC * T].rearrange("p (dc n) -> p dc n", n=T))
            P.wait("sync", "st")
            P.bar()

    def finish_early():
        dma_out(outT.rearrange("(dc p) n -> p dc n", p=128)[:, :, 0:512],
                xs[:, 0:NDC * 512].rearrange("p (dc n) -> p dc n", n=512))
        P.wait(G, "st")
        P.bar()
        P.finalize()
        ctx.close()
        return nc

    if stop_after in ("prologue", "consts", "w1"):
        dma_in(xs[:, 0:NDC * 512].rearrange("p (dc n) -> p dc n", n=512),
               xT.rearrange("(dc p) n -> p dc n", p=128)[:, :, 0:512])
        P.wait(G, "ld")
        return finish_early()
    if stop_after.startswith("ffn_"):
        r = emit_ffn(xT, x1d, dbg.get("x1o"), stop=stop_after[4:])
        return finish_early()
    emit_ffn(xT, x1d, dbg.get("x1o"))
    if stop_after == "ffnA":
        dma_in(xs[:, 0:NDC * 512].rearrange("p (dc n) -> p dc n", n=512),
               x1d.rearrange("(dc p) n -> p dc n", p=128)[:, :, 0:512])
        P.wait(G, "ld")
        return finish_early()
    if attn_mode == "copy":
        emit_attn_copy(x1d, x2d, dbg.get("x2o"))
    elif stop_after.startswith("attn_"):
        emit_attn(x1d, x2d, dbg.get("x2o"), stop=stop_after[5:])
        return finish_early()
    else:
        emit_attn(x1d, x2d, dbg.get("x2o"))
    emit_ffn(x2d, outT, None, tag="c")
    P.wait(G, "st")
    P.bar()

    P.finalize()
    ctx.close()
    return nc


def _pack_consts():
    j = np.arange(128)
    tri = (j[:, None] <= j[None, :]).astype(np.float32)
    trih = (np.arange(64)[:, None] <= np.arange(64)[None, :]).astype(np.float32)
    return {
        "c_ident": np.eye(128, dtype=np.float32),
        "c_tri": tri,
        "c_trih": trih,
        "c_ones": np.ones((128, 128), np.float32),
        "c_zeros": np.zeros((128, 512), np.float32),
    }


def kernel(L0_attn_x, L0_attn_Wq, L0_attn_Wk, L0_attn_Wv, L0_attn_Wf1, L0_attn_Wf2,
           L0_attn_Wg1, L0_attn_Wg2, L0_attn_Wo, L0_attn_norm_w,
           L0_ffn_w1, L0_ffn_conv_w, L0_ffn_conv_b, L0_ffn_w2):
    x = np.asarray(L0_attn_x, dtype=np.float32)
    consts = _pack_consts()
    convw = np.ascontiguousarray(
        np.asarray(L0_ffn_conv_w, np.float32).reshape(NHC, 128, 3).transpose(1, 0, 2)
        .reshape(128, NHC * 3))
    convw = np.ascontiguousarray(
        np.asarray(L0_ffn_conv_w, np.float32).reshape(NHC, 128, 3)
        .transpose(1, 0, 2).reshape(128, NHC * 3))
    convb = np.ascontiguousarray(np.asarray(L0_ffn_conv_b, np.float32).reshape(NHC, 128).T)
    normw = np.ascontiguousarray(np.asarray(L0_attn_norm_w, np.float32).reshape(NDC, 128).T)

    nc = build_program(attn_mode="attn")
    base = {
        "Wq": np.asarray(L0_attn_Wq, np.float32), "Wk": np.asarray(L0_attn_Wk, np.float32),
        "Wv": np.asarray(L0_attn_Wv, np.float32), "Wo": np.asarray(L0_attn_Wo, np.float32),
        "Wf1": np.asarray(L0_attn_Wf1, np.float32), "Wf2": np.asarray(L0_attn_Wf2, np.float32),
        "Wg1": np.asarray(L0_attn_Wg1, np.float32), "Wg2": np.asarray(L0_attn_Wg2, np.float32),
        "w1": np.asarray(L0_ffn_w1, np.float32), "w2": np.asarray(L0_ffn_w2, np.float32),
        "c_convw": convw, "c_convb": convb, "c_normw": normw,
        **consts,
    }
    in_maps = []
    for b in range(NCORE):
        m = dict(base)
        m["xT"] = np.ascontiguousarray(x[b].T)
        in_maps.append(m)
    res = run_bass_kernel_spmd(nc, in_maps, core_ids=list(range(NCORE)))
    out = np.stack([np.asarray(res.results[b]["outT"]).T for b in range(NCORE)], axis=0)
    return out.astype(np.float32)
